# version 02
# baseline (speedup 1.0000x reference)
import numpy as np
import ml_dtypes
from contextlib import ExitStack

import concourse.bass as bass
import concourse.mybir as mybir
from concourse.bass_utils import run_bass_kernel_spmd

F32 = mybir.dt.float32
BF16 = mybir.dt.bfloat16
ALU = mybir.AluOpType
AF = mybir.ActivationFunctionType
AX = mybir.AxisListType

T = 4096
D = 2048
NTT = T // 128
NTG = T // 512
ICH = D // 128
BIG = -1e30
MAGIC = 12582912.0
LN_EPS = 1e-5
RMS_EPS_F32 = float(np.finfo(np.float32).eps)
L1_EPS = 1e-6

ENG = ("sync", "gpsimd", "tensor", "vector", "scalar")


class _Instr:
    __slots__ = ("incs",)

    def __init__(self):
        self.incs = []

    def then_inc(self, sem, n=1):
        self.incs.append((sem, n))
        return self


class _EngRec:
    def __init__(self, rec, eng):
        self._rec = rec
        self._eng = eng

    def __getattr__(self, name):
        rec, eng = self._rec, self._eng

        def call(*a, **k):
            inst = _Instr()
            rec.q[eng].append((name, a, k, inst))
            return inst

        return call


class Rec:
    def __init__(self):
        self.q = {e: [] for e in ENG}
        self.sync = _EngRec(self, "sync")
        self.gpsimd = _EngRec(self, "gpsimd")
        self.tensor = _EngRec(self, "tensor")
        self.vector = _EngRec(self, "vector")
        self.scalar = _EngRec(self, "scalar")

    def replay(self, nc):
        with nc.Block() as block:
            for e in ENG:
                ops = self.q[e]
                if not ops:
                    continue

                def body(eng, ops=ops):
                    for name, a, k, inst in ops:
                        r = getattr(eng, name)(*a, **k)
                        for (s, n) in inst.incs:
                            r = r.then_inc(s, n)

                getattr(block, e)(body)
        self.q = {e: [] for e in ENG}


class Sem:
    __slots__ = ("h", "n")

    def __init__(self, h):
        self.h = h
        self.n = 0

    def inc(self, k=1):
        self.n += k
        return self.n


def host_tables():
    t = {}
    t["tab_ident_bf"] = np.eye(128, dtype=ml_dtypes.bfloat16)
    t["tab_ident_f32"] = np.eye(128, dtype=np.float32)
    tri = np.where(np.triu(np.ones((128, 128), np.float32), 1) > 0, np.float32(BIG), np.float32(0))
    t["tab_tri"] = tri
    db = np.ones((128, 128), np.float32)
    np.fill_diagonal(db, np.float32(0))
    t["tab_diagboost"] = db
    inv = 1.0 / (10000.0 ** (np.arange(0, 128, 2, dtype=np.float64) / 128.0))
    j = np.arange(64)
    ang = np.arange(T, dtype=np.float64)[:, None] * inv[(2 * j) % 64][None, :]
    t["tab_cos"] = np.cos(ang).astype(np.float32)
    t["tab_sin"] = np.sin(ang).astype(np.float32)
    slopes = np.array([2.0 ** (-0.5 * (i + 1)) for i in range(16)], np.float64)
    t["tab_alibi"] = (slopes[:, None] * np.arange(T, dtype=np.float64)[None, :]).astype(np.float32)
    t["tab_kpos"] = np.arange(T, dtype=np.float32)[None, :]
    return t


TABLE_SPECS = {
    "tab_ident_bf": ([128, 128], BF16),
    "tab_ident_f32": ([128, 128], F32),
    "tab_tri": ([128, 128], F32),
    "tab_diagboost": ([128, 128], F32),
    "tab_cos": ([T, 64], F32),
    "tab_sin": ([T, 64], F32),
    "tab_alibi": ([16, T], F32),
    "tab_kpos": ([1, T], F32),
}

INPUT_SPECS = {
    "L0_attn_Wq": ([D, D], F32), "L0_attn_Wk": ([D, D], F32),
    "L0_attn_Wv": ([D, D], F32), "L0_attn_Wo": ([D, D], F32),
    "L0_attn_bo": ([D], F32),
    "L0_ffn_w_gate_up": ([16384, D], F32), "L0_ffn_w_proj": ([D, 8192], F32),
    "L1_attn_w_q": ([D, D], F32), "L1_attn_b_q": ([D], F32),
    "L1_attn_w_down": ([D, 512], F32), "L1_attn_b_down": ([512], F32),
    "L1_attn_w_up": ([512, 1024], F32), "L1_attn_b_up": ([1024], F32),
    "L1_attn_w_o": ([D, D], F32), "L1_attn_b_o": ([D], F32),
    "L1_attn_qn_w": ([128], F32), "L1_attn_kn_w": ([128], F32),
    "L1_attn_gate_w": ([16, 128, 128], F32), "L1_attn_gate_b": ([16, 128], F32),
    "L1_ffn_gate_weight": ([8, D], F32),
    "L1_ffn_gate_w": ([8, 1408, D], F32), "L1_ffn_up_w": ([8, 1408, D], F32),
    "L1_ffn_down_w": ([8, D, 1408], F32),
    "L1_ffn_sh_gate_w": ([2816, D], F32), "L1_ffn_sh_up_w": ([2816, D], F32),
    "L1_ffn_sh_down_w": ([D, 2816], F32),
}


class B:
    pass


def make_context(es, debug=False):
    nc = bass.Bass("TRN2", target_bir_lowering=False)
    b = B()
    b.nc = nc
    b.x = nc.dram_tensor("x", [T, D], F32, kind="ExternalInput").ap()
    b.inp = {}
    for name, (shape, dt) in INPUT_SPECS.items():
        b.inp[name] = nc.dram_tensor(name, shape, dt, kind="ExternalInput").ap()
    for name, (shape, dt) in TABLE_SPECS.items():
        b.inp[name] = nc.dram_tensor(name, shape, dt, kind="ExternalInput").ap()
    b.out = nc.dram_tensor("out", [T, D], F32, kind="ExternalOutput").ap()
    kd = dict(kind="ExternalOutput") if debug else {}
    b.resid = nc.dram_tensor("resid", [T, D], F32, **kd).ap()
    b.lnscr = nc.dram_tensor("lnscr", [ICH, 128, T], BF16, **kd).ap()
    b.qfm = nc.dram_tensor("qfm", [ICH, 128, T], BF16, **kd).ap()
    b.kfm = nc.dram_tensor("kfm", [ICH, 128, T], BF16, **kd).ap()
    b.vtm = nc.dram_tensor("vtm", [16, T // 128, 128, 128], BF16, **kd).ap()
    b.ctx = nc.dram_tensor("ctxs", [16, 128, T], BF16, **kd).ap()
    b.yp = nc.dram_tensor("yps", [16, 128, T], BF16, **kd).ap()
    b.dbg_wmat = nc.dram_tensor("dbg_wmat", [NTG, 128, 32], F32, **kd).ap() if debug else None
    b.dbg_a = nc.dram_tensor("dbg_a", [128, 56320], BF16, **kd).ap() if debug else None
    b.dbg_ln = nc.dram_tensor("dbg_ln", [128, ICH * 512], BF16, **kd).ap() if debug else None
    b.dbg_kfm = nc.dram_tensor("dbg_kfm", [128, 4 * T], BF16, **kd).ap() if debug else None
    b.dbg_vtm = nc.dram_tensor("dbg_vtm", [128, T // 128 * 512], BF16, **kd).ap() if debug else None
    b.dbg_S = nc.dram_tensor("dbg_S", [4, 128, T], F32, **kd).ap() if debug else None
    b.dbg_wbc = nc.dram_tensor("dbg_wbc", [NTG, 128, 8 * 512], BF16, **kd).ap() if debug else None
    b.debug = debug
    b.c_wgu = nc.dram_tensor("c_wgu", [128, 128, ICH * 128], BF16).ap()
    b.c_wp = nc.dram_tensor("c_wp", [16, 128, 64 * 128], BF16).ap()
    b.c_g = nc.dram_tensor("c_g", [8, 11, 128, ICH * 128], BF16).ap()
    b.c_u = nc.dram_tensor("c_u", [8, 11, 128, ICH * 128], BF16).ap()
    b.c_d = nc.dram_tensor("c_d", [8, 16, 128, 11 * 128], BF16).ap()
    b.c_shg = nc.dram_tensor("c_shg", [22, 128, ICH * 128], BF16).ap()
    b.c_shu = nc.dram_tensor("c_shu", [22, 128, ICH * 128], BF16).ap()
    b.c_shd = nc.dram_tensor("c_shd", [16, 128, 22 * 128], BF16).ap()

    sb = lambda name, shape, dt: es.enter_context(nc.sbuf_tensor(name, shape, dt))
    ps = lambda name, shape, dt: es.enter_context(nc.psum_tensor(name, shape, dt))
    b.ident_bf = sb("ident_bf", [128, 128], BF16)
    b.ident_f32 = sb("ident_f32", [128, 128], F32)
    b.tri = sb("tri", [128, 128], F32)
    b.diagboost = sb("diagboost", [128, 128], F32)
    b.ones1 = sb("ones1", [1, 128], BF16)
    b.ones1f = sb("ones1f", [1, 128], F32)
    b.onescol = sb("onescol", [128, 1], BF16)
    b.stats = sb("stats", [128, 64], F32)
    b.stats2 = sb("stats2", [128, 64], F32)
    b.xt = sb("xt", [128, 2 * D], F32)
    b.lnt = sb("lnt", [128, 2 * D], BF16)
    b.lnfm = sb("lnfm", [128, ICH * 512], BF16)
    b.dump = sb("dumpb", [128, D], BF16)
    b.res = sb("res", [128, 2 * D], F32)
    b.istd = sb("istdb", [128, 8], F32)
    b.qs32 = sb("qs32", [128, 128], F32)
    b.qsb = sb("qsb", [128, 128], BF16)
    b.qr = sb("qr", [128, 128], F32)
    b.big = sb("big", [128, 73472], BF16)

    def carve(off_bytes, nbytes, dt=BF16):
        a = off_bytes // 2
        v = b.big[:, a:a + nbytes // 2]
        return v if dt == BF16 else v.bitcast(dt)
    b.carve = carve
    b.h = carve(0, 65536)
    b.qa = b.h[:, 0:2048]
    b.qb = b.h[:, 2048:4096]
    b.wnat = b.h[:, 4096:8192]
    b.wT = carve(65536, 32768)
    b.wT2 = carve(98304, 16384)
    b.outfm = carve(114688, 16384)
    b.silu = carve(131072, 4096, F32)
    b.hsq = carve(135168, 2048)
    b.row1 = carve(137216, 8192, F32)[0:1, :]
    b.pb = [ps(f"pb{i}", [128, 512], F32) for i in range(5)]
    b.pbT = ps("pbT", [128, 512], F32)
    b.pt = [ps(f"pt{i}", [128, 1024], BF16) for i in range(2)]
    b.sem = {}
    for name in ["sx", "sln", "slnT", "sw", "sw2", "str", "ste", "smm", "sev",
                 "sact", "sst", "sq", "sk", "ssel", "sbis", "sctx", "smisc",
                 "sr1", "sr2", "sr3", "sr4", "sr5"]:
        b.sem[name] = Sem(es.enter_context(nc.semaphore(name)))
    b.hist = {}
    return b


def emit_init(R, b):
    s = b.sem["smisc"]
    g = R.gpsimd
    g.memset(b.ones1[:, :], 1.0)
    g.memset(b.ones1f[:, :], 1.0)
    g.memset(b.onescol[:, :], 1.0).then_inc(s.h, 1)
    s.inc(1)
    sy = R.sync
    v = b.sem["sx"]
    sy.dma_start(out=b.ident_bf[:, :], in_=b.inp["tab_ident_bf"][:, :]).then_inc(v.h, 16)
    sy.dma_start(out=b.ident_f32[:, :], in_=b.inp["tab_ident_f32"][:, :]).then_inc(v.h, 16)
    sy.dma_start(out=b.tri[:, :], in_=b.inp["tab_tri"][:, :]).then_inc(v.h, 16)
    sy.dma_start(out=b.diagboost[:, :], in_=b.inp["tab_diagboost"][:, :]).then_inc(v.h, 16)
    v.inc(64)
    b.init_done = (b.sem["smisc"].n, b.sem["sx"].n)


def wait_init(R, b, engs=ENG):
    mis, sx = b.init_done
    for e in engs:
        getattr(R, e).wait_ge(b.sem["smisc"].h, mis)
        getattr(R, e).wait_ge(b.sem["sx"].h, sx)


FLOW_SEMS = ["sx", "sw", "sw2", "str", "ste", "smm", "sev", "sact", "sst",
             "sln", "slnT", "sq", "sk", "ssel", "sbis", "sctx", "smisc"]


def emit_barrier(R, b):
    for e in ENG:
        eng = getattr(R, e)
        for name in FLOW_SEMS:
            s = b.sem[name]
            if s.n:
                eng.wait_ge(s.h, s.n)
    keep = {k: v for k, v in b.hist.items() if k in ("l0pp_done", "l1pp_done")}
    b.hist = keep


def emit_ln_group(R, b, src, tg, buf, war=None):
    sx, sln, slnT = b.sem["sx"], b.sem["slnT"], None
    sln = b.sem["sln"]
    slnT = b.sem["slnT"]
    sact = b.sem["sact"]
    strp = b.sem["str"]
    sev = b.sem["sev"]
    if war is not None:
        R.vector.wait_ge(war[0].h, war[1])
    for tt in range(4):
        B0 = (tt % 2) * 10
        st = b.stats
        row0 = tg * 512 + tt * 128
        xv = b.xt[:, (tt % 2) * D:(tt % 2 + 1) * D]
        lv = b.lnt[:, (tt % 2) * D:(tt % 2 + 1) * D]
        k = b.hist.get(("ln_x", tt % 2))
        if k is not None:
            R.sync.wait_ge(k[0].h, k[1])
        R.sync.dma_start(out=xv, in_=src[row0:row0 + 128, :]).then_inc(sx.h, 16)
        vx = sx.inc(16)
        R.vector.wait_ge(sx.h, vx)
        R.scalar.wait_ge(sx.h, vx)
        R.vector.wait_ge(sx.h, vx)
        for c4 in range(4):
            R.vector.bn_stats(out=b.stats2[:, 32 + 6 * c4:38 + 6 * c4],
                              in_=xv[:, c4 * 512:(c4 + 1) * 512])
        vspace(R, b)
        R.vector.bn_aggr(out=st[:, B0 + 0:B0 + 2], in_=b.stats2[:, 32:56])
        vspace(R, b)
        R.vector.tensor_scalar(out=st[:, B0 + 2:B0 + 3], in0=st[:, B0 + 0:B0 + 1],
                               scalar1=-1.0, scalar2=None, op0=ALU.mult)
        R.vector.tensor_scalar(out=st[:, B0 + 3:B0 + 4], in0=st[:, B0 + 1:B0 + 2],
                               scalar1=1.0, scalar2=LN_EPS, op0=ALU.mult, op1=ALU.add)
        vspace(R, b)
        R.vector.reciprocal(out=st[:, B0 + 4:B0 + 5], in_=st[:, B0 + 3:B0 + 4])
        vr = vpub(R, b, sev)
        R.scalar.wait_ge(sev.h, vr)
        R.scalar.activation(out=st[:, B0 + 5:B0 + 6], in_=st[:, B0 + 4:B0 + 5], func=AF.Sqrt)
        va2 = spub(R, b, sact)
        k = b.hist.get(("ln_t", tt % 2))
        if k is not None:
            R.vector.wait_ge(k[0].h, k[1])
        R.vector.wait_ge(sact.h, va2)
        R.vector.tensor_scalar(out=lv, in0=xv, scalar1=st[:, B0 + 2:B0 + 3],
                               scalar2=st[:, B0 + 5:B0 + 6], op0=ALU.add, op1=ALU.mult).then_inc(sln.h, 1)
        b.hist[("ln_x", tt % 2)] = (sln, sln.inc(1))
        R.tensor.wait_ge(sln.h, b.hist[("ln_x", tt % 2)][1])
        for g4 in range(4):
            ptb = b.pt[g4 % 2]
            k = b.hist.get(("pt", g4 % 2))
            if k is not None:
                R.tensor.wait_ge(k[0].h, k[1])
            ins = None
            for jj in range(4):
                i = g4 * 4 + jj
                ins = R.tensor.transpose(ptb[:, jj * 128:(jj + 1) * 128],
                                         lv[:, i * 128:(i + 1) * 128],
                                         b.ident_bf[:, :])
            ins.then_inc(strp.h, 1)
            vt = strp.inc(1)
            R.vector.wait_ge(strp.h, vt)
            dst = b.lnfm[:, buf * ICH * 512:(buf + 1) * ICH * 512] \
                .rearrange("p (i t) -> p i t", i=ICH)[:, g4 * 4:(g4 + 1) * 4, tt * 128:(tt + 1) * 128]
            R.vector.tensor_copy(dst, ptb[:, 0:512].rearrange("p (i t) -> p i t", i=4)) \
                .then_inc(slnT.h, 1)
            b.hist[("pt", g4 % 2)] = (slnT, slnT.inc(1))
        b.hist[("ln_t", tt % 2)] = (strp, strp.n)
    return slnT.n


def lnfm_tile(b, buf, i, lo, width):
    base = buf * ICH * 512 + i * 512 + lo
    return b.lnfm[:, base:base + width]


def vspace(R, b, n=2):
    for _ in range(n):
        R.vector.tensor_copy(b.stats[:, 40:41], b.stats[:, 41:42])


def vpub(R, b, sem):
    vspace(R, b, 4)
    R.vector.tensor_copy(b.stats[:, 42:43], b.stats[:, 43:44]).then_inc(sem.h, 1)
    return sem.inc(1)


def spub(R, b, sem):
    sspace(R, b, 4)
    R.scalar.activation(out=b.stats[:, 48:49], in_=b.ident_f32[:, 0:1],
                        func=AF.Copy, bias=0.0).then_inc(sem.h, 1)
    return sem.inc(1)


def sspace(R, b, n=1):
    for _ in range(n):
        R.scalar.activation(out=b.stats[:, 44:45], in_=b.ident_f32[:, 0:1],
                            func=AF.Copy, bias=0.0)


def emit_quant_chunk(R, b, ngroups):
    w = ngroups * 64
    v = R.vector
    wv = b.wnat[:, 0:w].rearrange("p (g e) -> p g e", e=64)
    v.tensor_reduce(out=b.qs32[:, 0:ngroups], in_=wv, axis=AX.X, op=ALU.add,
                    apply_absolute_value=True)
    vspace(R, b)
    v.tensor_scalar(out=b.qsb[:, 0:ngroups], in0=b.qs32[:, 0:ngroups],
                    scalar1=1.0 / 64, scalar2=1e-8, op0=ALU.mult, op1=ALU.max)
    vspace(R, b)
    v.reciprocal(out=b.qr[:, 0:ngroups], in_=b.qsb[:, 0:ngroups])
    vspace(R, b)
    rb = b.qr[:, 0:ngroups].unsqueeze(2).broadcast_to([128, ngroups, 64])
    v.tensor_tensor(out=b.qa[:, 0:w].rearrange("p (g e) -> p g e", e=64), in0=wv, in1=rb,
                    op=ALU.mult)
    v.tensor_scalar(out=b.qb[:, 0:w], in0=b.qa[:, 0:w], scalar1=MAGIC, scalar2=-MAGIC,
                    op0=ALU.add, op1=ALU.add)
    v.tensor_scalar(out=b.qa[:, 0:w], in0=b.qb[:, 0:w], scalar1=1.0, scalar2=-1.0,
                    op0=ALU.min, op1=ALU.max)
    sb2 = b.qsb[:, 0:ngroups].unsqueeze(2).broadcast_to([128, ngroups, 64])
    v.tensor_tensor(out=b.qb[:, 0:w].rearrange("p (g e) -> p g e", e=64),
                    in0=b.qa[:, 0:w].rearrange("p (g e) -> p g e", e=64), in1=sb2,
                    op=ALU.mult)


def emit_transpose16(R, b, src_bf, dst, n, war_key):
    strp, ste = b.sem["str"], b.sem["ste"]
    ng = (n + 3) // 4
    for g4 in range(ng):
        ptb = b.pt[g4 % 2]
        k = b.hist.get(("pt", g4 % 2))
        if k is not None:
            R.tensor.wait_ge(k[0].h, k[1])
        cnt = min(4, n - g4 * 4)
        ins = None
        for jj in range(cnt):
            i = g4 * 4 + jj
            ins = R.tensor.transpose(ptb[:, jj * 128:(jj + 1) * 128],
                                     src_bf[:, i * 128:(i + 1) * 128],
                                     b.ident_bf[:, :])
        ins.then_inc(strp.h, 1)
        vt = strp.inc(1)
        R.vector.wait_ge(strp.h, vt)
        R.vector.tensor_copy(dst[:, g4 * 512:g4 * 512 + cnt * 128], ptb[:, 0:cnt * 128]) \
            .then_inc(ste.h, 1)
        b.hist[("pt", g4 % 2)] = (ste, ste.inc(1))
    return ste.n


def emit_l0ffn_prepass(R, b):
    wgu = b.inp["L0_ffn_w_gate_up"]
    wp = b.inp["L0_ffn_w_proj"]
    sw, sev, sst = b.sem["sw"], b.sem["sev"], b.sem["sst"]
    smm = b.sem["smm"]
    for fc in range(128):
        k = b.hist.get("qwnat")
        if k is not None:
            R.gpsimd.wait_ge(k[0].h, k[1])
        R.gpsimd.dma_start(out=b.wnat[:, 0:D], in_=wgu[fc * 128:(fc + 1) * 128, :],
                           max_dma_last_dim=8192).then_inc(sw.h, 16)
        vw = sw.inc(16)
        R.vector.wait_ge(sw.h, vw)
        k = b.hist.get("qb_free")
        if k is not None:
            R.vector.wait_ge(k[0].h, k[1])
        emit_quant_chunk(R, b, 32)
        R.vector.tensor_copy(b.stats[:, 45:46], b.stats[:, 46:47]).then_inc(sev.h, 1)
        b.hist["qwnat"] = (sev, sev.inc(1))
        R.tensor.wait_ge(sev.h, b.hist["qwnat"][1])
        slot = b.wT[:, (fc % 2) * 2048:(fc % 2) * 2048 + 2048]
        k = b.hist.get(("wTfree", fc % 2))
        if k is not None:
            R.vector.wait_ge(k[0].h, k[1])
        evdone = emit_transpose16(R, b, b.qb, slot, 16, "pp")
        b.hist["qb_free"] = (b.sem["str"], b.sem["str"].n)
        R.sync.wait_ge(b.sem["ste"].h, evdone)
        R.sync.dma_start(out=b.c_wgu[fc], in_=slot).then_inc(sst.h, 16)
        b.hist[("wTfree", fc % 2)] = (sst, sst.inc(16))
    R.vector.wait_ge(sst.h, sst.n)
    for oc in range(16):
        for pc in range(4):
            k = b.hist.get("qwnat")
            if k is not None:
                R.gpsimd.wait_ge(k[0].h, k[1])
            R.gpsimd.dma_start(out=b.wnat[:, 0:D],
                               in_=wp[oc * 128:(oc + 1) * 128, pc * 2048:(pc + 1) * 2048],
                               max_dma_last_dim=8192).then_inc(sw.h, 16)
            vw = sw.inc(16)
            R.vector.wait_ge(sw.h, vw)
            k = b.hist.get("qb_free")
            if k is not None:
                R.vector.wait_ge(k[0].h, k[1])
            emit_quant_chunk(R, b, 32)
            R.vector.tensor_copy(b.stats[:, 45:46], b.stats[:, 46:47]).then_inc(sev.h, 1)
            b.hist["qwnat"] = (sev, sev.inc(1))
            R.tensor.wait_ge(sev.h, b.hist["qwnat"][1])
            slab = b.wT[:, (oc % 2) * 8192:(oc % 2) * 8192 + 8192]
            if pc == 0:
                k = b.hist.get(("wpfree", oc % 2))
                if k is not None:
                    R.vector.wait_ge(k[0].h, k[1])
            evdone = emit_transpose16(R, b, b.qb, slab[:, pc * 2048:(pc + 1) * 2048], 16, "pp")
            b.hist["qb_free"] = (b.sem["str"], b.sem["str"].n)
        R.sync.wait_ge(b.sem["ste"].h, evdone)
        R.sync.dma_start(out=b.c_wp[oc], in_=slab).then_inc(sst.h, 16)
        b.hist[("wpfree", oc % 2)] = (sst, sst.inc(16))
    b.hist["l0pp_done"] = (sst, sst.n)


def emit_l0ffn(R, b, src, dst, first_call):
    emit_barrier(R, b)
    if first_call:
        emit_l0ffn_prepass(R, b)
    sw, sw2 = b.sem["sw"], b.sem["sw2"]
    smm, sev, sact = b.sem["smm"], b.sem["sev"], b.sem["sact"]
    sst, sx = b.sem["sst"], b.sem["sx"]
    ppd = b.hist.get("l0pp_done")
    if ppd is not None:
        R.gpsimd.wait_ge(ppd[0].h, ppd[1])
    for tg in range(NTG):
        buf = 0
        war = b.hist.get(("lnfm", buf))
        emit_ln_group(R, b, src, tg, buf, war=war)
        ln_ready = b.sem["slnT"].n
        R.tensor.wait_ge(b.sem["slnT"].h, ln_ready)
        for j in range(64):
            gslot = b.wT2[:, (j % 2) * 2048:(j % 2) * 2048 + 2048]
            uslot = b.wT2[:, 4096 + (j % 2) * 2048:4096 + (j % 2) * 2048 + 2048]
            k = b.hist.get(("gw", j % 2))
            if k is not None:
                R.gpsimd.wait_ge(k[0].h, k[1])
            R.gpsimd.dma_start(out=gslot, in_=b.c_wgu[j]).then_inc(sw.h, 16)
            R.gpsimd.dma_start(out=uslot, in_=b.c_wgu[64 + j]).then_inc(sw.h, 16)
            vw = sw.inc(32)
            pbg = b.pb[(j % 2) * 2]
            pbu = b.pb[(j % 2) * 2 + 1]
            k = b.hist.get(("pbgu", j % 2))
            if k is not None:
                R.tensor.wait_ge(k[0].h, k[1])
            R.tensor.wait_ge(sw.h, vw)
            insg = insu = None
            for i in range(ICH):
                insg = R.tensor.matmul(pbg[:, :], gslot[:, i * 128:(i + 1) * 128],
                                       lnfm_tile(b, buf, i, 0, 512),
                                       start=(i == 0), stop=(i == ICH - 1))
            insg.then_inc(smm.h, 1)
            vg = smm.inc(1)
            for i in range(ICH):
                insu = R.tensor.matmul(pbu[:, :], uslot[:, i * 128:(i + 1) * 128],
                                       lnfm_tile(b, buf, i, 0, 512),
                                       start=(i == 0), stop=(i == ICH - 1))
            insu.then_inc(smm.h, 1)
            vu = smm.inc(1)
            b.hist[("gw", j % 2)] = (smm, vu)
            sislot = b.silu[:, (j % 2) * 512:(j % 2) * 512 + 512]
            R.scalar.wait_ge(smm.h, vg)
            k = b.hist.get(("silu", j % 2))
            if k is not None:
                R.scalar.wait_ge(k[0].h, k[1])
            R.scalar.activation(out=sislot, in_=pbg[:, :], func=AF.Silu).then_inc(sact.h, 1)
            va = sact.inc(1)
            R.vector.wait_ge(sact.h, va)
            R.vector.wait_ge(smm.h, vu)
            hj = b.h[:, j * 512:(j + 1) * 512]
            R.vector.tensor_tensor(out=hj, in0=sislot, in1=pbu[:, :], op=ALU.mult) \
                .then_inc(sev.h, 1)
            hval = sev.inc(1)
            b.hist[("pbgu", j % 2)] = (sev, hval)
            b.hist[("silu", j % 2)] = (sev, hval)
            hsq = b.hsq[:, (j % 2) * 512:(j % 2) * 512 + 512]
            k = b.hist.get(("hsq", j % 2))
            if k is not None:
                R.vector.wait_ge(k[0].h, k[1])
            R.vector.tensor_tensor(out=hsq, in0=hj, in1=hj, op=ALU.mult).then_inc(sev.h, 1)
            vq = sev.inc(1)
            R.tensor.wait_ge(sev.h, vq)
            if j == 0:
                k = b.hist.get("pb4")
                if k is not None:
                    R.tensor.wait_ge(k[0].h, k[1])
            insq = R.tensor.matmul(b.pb[4][0:1, :], b.onescol[:, :], hsq,
                                   start=(j == 0), stop=(j == 63))
            insq.then_inc(smm.h, 1)
            b.hist[("hsq", j % 2)] = (smm, smm.inc(1))
        sumsq_done = smm.n
        v = R.vector
        v.wait_ge(smm.h, sumsq_done)
        v.tensor_scalar(out=b.row1[0:1, 0:512], in0=b.pb[4][0:1, :], scalar1=1.0 / 8192,
                        scalar2=RMS_EPS_F32, op0=ALU.mult, op1=ALU.add).then_inc(sev.h, 1)
        b.hist["pb4"] = (sev, sev.inc(1))
        vspace(R, b)
        v.reciprocal(out=b.row1[0:1, 512:1024], in_=b.row1[0:1, 0:512])
        vr = vpub(R, b, sev)
        R.scalar.wait_ge(sev.h, vr)
        R.scalar.activation(out=b.row1[0:1, 1024:1536], in_=b.row1[0:1, 512:1024],
                            func=AF.Sqrt)
        va = spub(R, b, sact)
        v.wait_ge(sact.h, va)
        v.tensor_scalar(out=b.row1[0:1, 1536:2048], in0=b.row1[0:1, 1024:1536],
                        scalar1=0.5, scalar2=None, op0=ALU.mult)
        vh = vpub(R, b, sev)
        R.tensor.wait_ge(sev.h, vh)
        k = b.hist.get("pbT")
        if k is not None:
            R.tensor.wait_ge(k[0].h, k[1])
        ins = None
        for tt in range(4):
            ins = R.tensor.transpose(b.pbT[:, tt:tt + 1],
                                     b.row1[0:1, 1536 + tt * 128:1536 + (tt + 1) * 128],
                                     b.ones1f[0:1, 0:1])
        ins.then_inc(b.sem["str"].h, 1)
        vt = b.sem["str"].inc(1)
        v.wait_ge(b.sem["str"].h, vt)
        v.tensor_copy(b.istd[:, 0:4], b.pbT[:, 0:4])
        vspace(R, b, 2)
        v.tensor_copy(b.stats[:, 42:43], b.stats[:, 43:44]).then_inc(sev.h, 1)
        b.hist["pbT"] = (sev, sev.inc(1))
        for oc in range(16):
            slab = b.wT[:, (oc % 2) * 8192:(oc % 2) * 8192 + 8192]
            k = b.hist.get(("wp", oc % 2))
            if k is not None:
                R.gpsimd.wait_ge(k[0].h, k[1])
            R.gpsimd.dma_start(out=slab, in_=b.c_wp[oc]).then_inc(sw2.h, 16)
            vw = sw2.inc(16)
            pbo = b.pb[oc % 2]
            k = b.hist.get(("pbdown", oc % 2))
            if k is not None:
                R.tensor.wait_ge(k[0].h, k[1])
            R.tensor.wait_ge(sw2.h, vw)
            ins = None
            for f in range(64):
                ins = R.tensor.matmul(pbo[:, :], slab[:, f * 128:(f + 1) * 128],
                                      b.h[:, f * 512:(f + 1) * 512],
                                      start=(f == 0), stop=(f == 63))
            ins.then_inc(smm.h, 1)
            vm = smm.inc(1)
            b.hist[("wp", oc % 2)] = (smm, vm)
            R.scalar.wait_ge(smm.h, vm)
            if oc == 0:
                k = b.hist.get("outfm")
                if k is not None:
                    R.scalar.wait_ge(k[0].h, k[1])
            R.scalar.activation(out=b.outfm[:, oc * 512:(oc + 1) * 512], in_=pbo[:, :],
                                func=AF.Copy, bias=0.0).then_inc(sact.h, 1)
            b.hist[("pbdown", oc % 2)] = (sact, sact.inc(1))
        outfm_done = sact.n
        R.tensor.wait_ge(sact.h, outfm_done)
        for tt in range(4):
            row0 = tg * 512 + tt * 128
            xs = b.xt[:, (tt % 2) * D:(tt % 2) * D + D]
            k = b.hist.get(("ln_x", tt % 2))
            if k is not None:
                R.sync.wait_ge(k[0].h, k[1])
            R.sync.dma_start(out=xs, in_=src[row0:row0 + 128, :]).then_inc(sx.h, 16)
            vx = sx.inc(16)
            R.vector.wait_ge(sx.h, vx)
            res = b.res[:, (tt % 2) * D:(tt % 2) * D + D]
            k = b.hist.get(("res", tt % 2))
            if k is not None:
                R.vector.wait_ge(k[0].h, k[1])
            for r in range(4):
                ptb = b.pt[r % 2]
                k = b.hist.get(("pt", r % 2))
                if k is not None:
                    R.tensor.wait_ge(k[0].h, k[1])
                ins = None
                for kk in range(4):
                    oc = r * 4 + kk
                    ins = R.tensor.transpose(
                        ptb[:, kk * 128:(kk + 1) * 128],
                        b.outfm[:, oc * 512 + tt * 128:oc * 512 + (tt + 1) * 128],
                        b.ident_bf[:, :])
                ins.then_inc(b.sem["str"].h, 1)
                vt = b.sem["str"].inc(1)
                R.vector.wait_ge(b.sem["str"].h, vt)
                R.vector.scalar_tensor_tensor(
                    out=res[:, r * 512:(r + 1) * 512], in0=ptb[:, 0:512],
                    scalar=b.istd[:, tt:tt + 1], in1=xs[:, r * 512:(r + 1) * 512],
                    op0=ALU.mult, op1=ALU.add).then_inc(sev.h, 1)
                b.hist[("pt", r % 2)] = (sev, sev.inc(1))
            b.hist[("ln_x", tt % 2)] = (sev, sev.n)
            R.sync.wait_ge(sev.h, sev.n)
            R.sync.dma_start(out=dst[row0:row0 + 128, :], in_=res).then_inc(sst.h, 16)
            b.hist[("res", tt % 2)] = (sst, sst.inc(16))
        b.hist[("lnfm", buf)] = (smm, smm.n)
        b.hist["outfm"] = (b.sem["str"], b.sem["str"].n)


def emit_final_sync(R, b):
    for name in ["sst", "sx"]:
        R.sync.wait_ge(b.sem[name].h, b.sem[name].n)


def build_program(parts, debug=False):
    es = ExitStack()
    with es:
        b = make_context(es, debug=debug)
        R = Rec()
        emit_init(R, b)
        wait_init(R, b)
        ffn_calls = {"l0ffn": 0, "l1ffn": 0}
        for name, src, dst in parts:
            s = {"x": b.x, "resid": b.resid, "out": b.out}[src]
            d = {"x": b.x, "resid": b.resid, "out": b.out}[dst]
            if name == "l0ffn":
                emit_l0ffn(R, b, s, d, first_call=(ffn_calls["l0ffn"] == 0))
                ffn_calls["l0ffn"] += 1
            elif name == "l1ffn":
                emit_l1ffn(R, b, s, d, first_call=(ffn_calls["l1ffn"] == 0))
                ffn_calls["l1ffn"] += 1
            elif name == "l0attn":
                emit_l0attn(R, b, s, d)
            elif name == "l1attn":
                emit_l1attn(R, b, s, d)
            else:
                raise ValueError(name)
        emit_final_sync(R, b)
        R.replay(b.nc)
    return b.nc


FULL_CHAIN = [
    ("l0ffn", "x", "resid"),
    ("l0attn", "resid", "resid"),
    ("l0ffn", "resid", "resid"),
    ("l1ffn", "resid", "resid"),
    ("l1attn", "resid", "resid"),
    ("l1ffn", "resid", "out"),
]


def kernel(**inputs):
    tabs = host_tables()
    nc = build_program(FULL_CHAIN)
    in_maps = []
    for c in range(8):
        m = {"x": np.ascontiguousarray(inputs["L0_attn_x"][c])}
        for name in INPUT_SPECS:
            m[name] = np.asarray(inputs[name])
        m.update(tabs)
        in_maps.append(m)
    res = run_bass_kernel_spmd(nc, in_maps, core_ids=list(range(8)))
    return np.stack([r["out"] for r in res.results], axis=0)


HSCALE = float(1.0 / np.sqrt(np.float32(128.0)))


def emit_strip_load(R, b, w_ap, slab, oc, sem):
    for q4 in range(4):
        src = w_ap[:, oc * 128:(oc + 1) * 128] \
            .rearrange("(i p) c -> p i c", p=128)[:, q4 * 4:(q4 + 1) * 4, :]
        dst = slab[:, oc * 2048 + q4 * 512:oc * 2048 + (q4 + 1) * 512] \
            .rearrange("p (i c) -> p i c", c=128)
        R.gpsimd.dma_start(out=dst, in_=src, max_dma_last_dim=8192).then_inc(sem.h, 16)
        sem.inc(16)
    return sem.n


def emit_l0attn(R, b, src, dst):
    emit_barrier(R, b)
    sx, sw, sw2 = b.sem["sx"], b.sem["sw"], b.sem["sw2"]
    smm, sev, sact = b.sem["smm"], b.sem["sev"], b.sem["sact"]
    sst, sq = b.sem["sst"], b.sem["sq"]
    strp, ste = b.sem["str"], b.sem["ste"]
    sln, slnT = b.sem["sln"], b.sem["slnT"]
    carve = b.carve
    wslab = carve(0, 65536)
    stage = carve(65536, 8192)
    for tg in range(NTG):
        war = b.hist.get(("lnfm", 0))
        emit_ln_group(R, b, src, tg, 0, war=war)
        ln_ready = b.sem["slnT"].n
        R.sync.wait_ge(slnT.h, ln_ready)
        for i in range(ICH):
            R.sync.dma_start(out=b.lnscr[i][:, tg * 512:(tg + 1) * 512],
                             in_=b.lnfm[:, i * 512:(i + 1) * 512]).then_inc(sst.h, 16)
            sst.inc(16)
        b.hist[("lnfm", 0)] = (sst, sst.n)
    for wi, (wname, dstscr, is_v) in enumerate([
            ("L0_attn_Wq", b.qfm, False),
            ("L0_attn_Wk", b.kfm, False),
            ("L0_attn_Wv", None, True)]):
        k = b.hist.get("wslab")
        if k is not None:
            R.gpsimd.wait_ge(k[0].h, k[1])
        for oc in range(ICH):
            emit_strip_load(R, b, b.inp[wname], wslab, oc, sw)
        wload = sw.n
        for tg in range(NTG):
            k = b.hist.get(("lnfm", 0))
            if k is not None:
                R.sync.wait_ge(k[0].h, k[1])
            for i in range(ICH):
                R.sync.dma_start(out=b.lnfm[:, i * 512:(i + 1) * 512],
                                 in_=b.lnscr[i][:, tg * 512:(tg + 1) * 512]).then_inc(sx.h, 16)
            vls = sx.inc(16 * ICH)
            R.tensor.wait_ge(sx.h, vls)
            if tg == 0:
                R.tensor.wait_ge(sw.h, wload)
            for oc in range(ICH):
                pbo = b.pb[oc % 2]
                k = b.hist.get(("pbqkv", oc % 2))
                if k is not None:
                    R.tensor.wait_ge(k[0].h, k[1])
                ins = None
                for i in range(ICH):
                    ins = R.tensor.matmul(pbo[:, :],
                                          wslab[:, oc * 2048 + i * 128:oc * 2048 + (i + 1) * 128],
                                          b.lnfm[:, i * 512:(i + 1) * 512],
                                          start=(i == 0), stop=(i == ICH - 1))
                ins.then_inc(smm.h, 1)
                vm = smm.inc(1)
                if not is_v:
                    sslot = stage[:, (oc % 4) * 1024:(oc % 4) * 1024 + 512]
                    k = b.hist.get(("qkst", oc % 4))
                    if k is not None:
                        R.scalar.wait_ge(k[0].h, k[1])
                    R.scalar.wait_ge(smm.h, vm)
                    R.scalar.activation(out=sslot, in_=pbo[:, :], func=AF.Copy,
                                        bias=0.0).then_inc(sact.h, 1)
                    va = sact.inc(1)
                    b.hist[("pbqkv", oc % 2)] = (sact, va)
                    R.sync.wait_ge(sact.h, va)
                    R.sync.dma_start(out=dstscr[oc][:, tg * 512:(tg + 1) * 512],
                                     in_=sslot).then_inc(sst.h, 16)
                    b.hist[("qkst", oc % 4)] = (sst, sst.inc(16))
                else:
                    sslot = stage[:, (oc % 2) * 2048:(oc % 2) * 2048 + 512]
                    tslot = stage[:, (oc % 2) * 2048 + 1024:(oc % 2) * 2048 + 1536]
                    k = b.hist.get(("vst", oc % 2))
                    if k is not None:
                        R.scalar.wait_ge(k[0].h, k[1])
                    R.scalar.wait_ge(smm.h, vm)
                    R.scalar.activation(out=sslot, in_=pbo[:, :], func=AF.Copy,
                                        bias=0.0).then_inc(sact.h, 1)
                    va = sact.inc(1)
                    b.hist[("pbqkv", oc % 2)] = (sact, va)
                    R.tensor.wait_ge(sact.h, va)
                    ptb = b.pt[oc % 2]
                    k = b.hist.get(("pt", oc % 2))
                    if k is not None:
                        R.tensor.wait_ge(k[0].h, k[1])
                    ins = None
                    for kk in range(4):
                        ins = R.tensor.transpose(ptb[:, kk * 128:(kk + 1) * 128],
                                                 sslot[:, kk * 128:(kk + 1) * 128],
                                                 b.ident_bf[:, :])
                    ins.then_inc(strp.h, 1)
                    vt = strp.inc(1)
                    k = b.hist.get(("vts", oc % 2))
                    if k is not None:
                        R.vector.wait_ge(k[0].h, k[1])
                    R.vector.wait_ge(strp.h, vt)
                    R.vector.tensor_copy(tslot, ptb[:, 0:512]).then_inc(ste.h, 1)
                    vte = ste.inc(1)
                    b.hist[("pt", oc % 2)] = (ste, vte)
                    R.sync.wait_ge(ste.h, vte)
                    vdst = b.vtm[oc][tg * 4:(tg + 1) * 4].rearrange("a q f -> q a f")
                    R.sync.dma_start(out=vdst,
                                     in_=tslot.rearrange("p (a f) -> p a f", f=128)) \
                        .then_inc(sst.h, 16)
                    vs = sst.inc(16)
                    b.hist[("vst", oc % 2)] = (sst, vs)
                    b.hist[("vts", oc % 2)] = (sst, vs)
            b.hist[("lnfm", 0)] = (smm, smm.n)
        b.hist["wslab"] = (smm, smm.n)
    S = carve(0, 16384, F32)
    P = carve(16384, 8192)
    P2 = carve(24576, 8192)
    qh = carve(32768, 8192)
    kh = carve(40960, 8192)
    vh = carve(49152, 8192)
    PT = carve(57344, 1024)
    ctxst = carve(58368, 1024)
    R.sync.wait_ge(sst.h, sst.n)
    R.gpsimd.wait_ge(sst.h, sst.n)
    for p_ in range(2):
        if b.hist.get(("pbqkv", p_)) is not None:
            b.hist[("pbS", p_)] = b.hist[("pbqkv", p_)]
    for h in range(16):
        k = b.hist.get("headbufs")
        if k is not None:
            R.gpsimd.wait_ge(k[0].h, k[1])
        R.gpsimd.dma_start(out=qh, in_=b.qfm[h]).then_inc(sq.h, 16)
        R.gpsimd.dma_start(out=kh, in_=b.kfm[h]).then_inc(sq.h, 16)
        R.gpsimd.dma_start(out=vh.rearrange("p (a f) -> p a f", f=128),
                           in_=b.vtm[h].rearrange("a q f -> q a f")).then_inc(sq.h, 16)
        vq = sq.inc(48)
        R.tensor.wait_ge(sq.h, vq)
        for tq in range(NTT):
            L = (tq + 1) * 128
            nch = (L + 511) // 512
            for c in range(nch):
                pbo = b.pb[c % 2]
                k = b.hist.get(("pbS", c % 2))
                if k is not None:
                    R.tensor.wait_ge(k[0].h, k[1])
                R.tensor.matmul(pbo[:, :], qh[:, tq * 128:(tq + 1) * 128],
                                kh[:, c * 512:(c + 1) * 512],
                                start=True, stop=True).then_inc(smm.h, 1)
                vm = smm.inc(1)
                if c == 0:
                    k = b.hist.get("Sfree")
                    if k is not None:
                        R.vector.wait_ge(k[0].h, k[1])
                R.vector.wait_ge(smm.h, vm)
                ndiag = min(512, L - c * 512) - 128 if (c == nch - 1) else 512
                if c == nch - 1:
                    if ndiag > 0:
                        R.vector.tensor_scalar(out=S[:, c * 512:c * 512 + ndiag],
                                               in0=pbo[:, 0:ndiag], scalar1=HSCALE,
                                               scalar2=None, op0=ALU.mult)
                    R.vector.scalar_tensor_tensor(
                        out=S[:, c * 512 + ndiag:c * 512 + ndiag + 128],
                        in0=pbo[:, ndiag:ndiag + 128], scalar=HSCALE, in1=b.tri[:, :],
                        op0=ALU.mult, op1=ALU.add).then_inc(sev.h, 1)
                else:
                    R.vector.tensor_scalar(out=S[:, c * 512:(c + 1) * 512],
                                           in0=pbo[:, :], scalar1=HSCALE,
                                           scalar2=None, op0=ALU.mult).then_inc(sev.h, 1)
                b.hist[("pbS", c % 2)] = (sev, sev.inc(1))
            v = R.vector
            v.tensor_reduce(out=b.stats[:, 10:11], in_=S[:, 0:L], axis=AX.X,
                            op=ALU.max, negate=True)
            vspace(R, b)
            v.tensor_copy(b.stats[:, 11:12], b.stats[:, 10:11])
            vneg = vpub(R, b, sev)
            R.scalar.wait_ge(sev.h, vneg)
            k = b.hist.get("Pfree")
            if k is not None:
                R.scalar.wait_ge(k[0].h, k[1])
            R.scalar.activation(out=P[:, 0:L], in_=S[:, 0:L], func=AF.Exp,
                                bias=b.stats[:, 11:12], accum_out=b.stats[:, 12:13])
            sspace(R, b, 4)
            R.scalar.activation(out=b.stats[:, 13:14], in_=b.stats[:, 12:13],
                                func=AF.Copy, bias=0.0)
            sspace(R, b, 2)
            R.scalar.activation(out=b.stats[:, 15:16], in_=b.stats[:, 13:14],
                                func=AF.Copy, bias=0.0)
            va = spub(R, b, sact)
            b.hist["Sfree"] = (sact, va)
            v.wait_ge(sact.h, va)
            v.reciprocal(out=b.stats[:, 14:15], in_=b.stats[:, 15:16])
            vspace(R, b, 3)
            k = b.hist.get("P2free")
            if k is not None:
                v.wait_ge(k[0].h, k[1])
            v.tensor_scalar(out=P2[:, 0:L], in0=P[:, 0:L], scalar1=b.stats[:, 14:15],
                            scalar2=None, op0=ALU.mult).then_inc(sev.h, 1)
            vp2 = sev.inc(1)
            b.hist["Pfree"] = (sev, vp2)
            R.tensor.wait_ge(sev.h, vp2)
            nkb = tq + 1
            for kb in range(nkb):
                ptb = b.pt[kb % 2]
                k = b.hist.get(("pt", kb % 2))
                if k is not None:
                    R.tensor.wait_ge(k[0].h, k[1])
                R.tensor.transpose(ptb[:, 0:128], P2[:, kb * 128:(kb + 1) * 128],
                                   b.ident_bf[:, :]).then_inc(strp.h, 1)
                vt = strp.inc(1)
                k = b.hist.get(("PTs", kb % 2))
                if k is not None:
                    R.vector.wait_ge(k[0].h, k[1])
                R.vector.wait_ge(strp.h, vt)
                R.vector.tensor_copy(PT[:, (kb % 2) * 256:(kb % 2) * 256 + 128],
                                     ptb[:, 0:128]).then_inc(ste.h, 1)
                vte = ste.inc(1)
                b.hist[("pt", kb % 2)] = (ste, vte)
                R.tensor.wait_ge(ste.h, vte)
                if kb == 0:
                    k = b.hist.get("pbctx")
                    if k is not None:
                        R.tensor.wait_ge(k[0].h, k[1])
                R.tensor.matmul(b.pb[2][:, 0:128], vh[:, kb * 128:(kb + 1) * 128],
                                PT[:, (kb % 2) * 256:(kb % 2) * 256 + 128],
                                start=(kb == 0), stop=(kb == nkb - 1)).then_inc(smm.h, 1)
                b.hist[("PTs", kb % 2)] = (smm, smm.inc(1))
            b.hist["P2free"] = (smm, smm.n)
            R.scalar.wait_ge(smm.h, smm.n)
            k = b.hist.get("ctxst")
            if k is not None:
                R.scalar.wait_ge(k[0].h, k[1])
            R.scalar.activation(out=ctxst[:, (tq % 4) * 128:(tq % 4) * 128 + 128],
                                in_=b.pb[2][:, 0:128], func=AF.Copy,
                                bias=0.0).then_inc(sact.h, 1)
            b.hist["pbctx"] = (sact, sact.inc(1))
            if tq % 4 == 3:
                R.sync.wait_ge(sact.h, sact.n)
                R.sync.dma_start(out=b.ctx[h][:, (tq - 3) * 128:(tq + 1) * 128],
                                 in_=ctxst).then_inc(sst.h, 16)
                b.hist["ctxst"] = (sst, sst.inc(16))
        b.hist["headbufs"] = (smm, smm.n)
    crhs = carve(65536, 16384)
    outfm = carve(81920, 16384)
    borow = carve(98304, 8192, F32)[0:1, :]
    R.sync.wait_ge(sst.h, sst.n)
    R.gpsimd.wait_ge(sst.h, sst.n)
    R.sync.dma_start(out=borow[0:1, 0:2048], in_=b.inp["L0_attn_bo"].unsqueeze(0)) \
        .then_inc(sx.h, 16)
    vb = sx.inc(16)
    R.tensor.wait_ge(sx.h, vb)
    k = b.hist.get("pbT")
    if k is not None:
        R.tensor.wait_ge(k[0].h, k[1])
    for q4 in range(4):
        ins = None
        for kk in range(4):
            i = q4 * 4 + kk
            ins = R.tensor.transpose(b.pbT[:, i:i + 1],
                                     borow[0:1, i * 128:(i + 1) * 128],
                                     b.ones1f[0:1, 0:1])
    ins.then_inc(strp.h, 1)
    vt = strp.inc(1)
    R.vector.wait_ge(strp.h, vt)
    R.vector.tensor_copy(b.stats2[:, 16:32], b.pbT[:, 0:16]).then_inc(sev.h, 1)
    b.hist["pbT"] = (sev, sev.inc(1))
    bocol_ready = sev.n
    for oc in range(ICH):
        emit_strip_load(R, b, b.inp["L0_attn_Wo"], wslab, oc, sw)
    wload = sw.n
    for tg in range(NTG):
        k = b.hist.get("crhs")
        if k is not None:
            R.sync.wait_ge(k[0].h, k[1])
        for i in range(ICH):
            R.sync.dma_start(out=crhs[:, i * 512:(i + 1) * 512],
                             in_=b.ctx[i][:, tg * 512:(tg + 1) * 512]).then_inc(sx.h, 16)
        vls = sx.inc(16 * ICH)
        R.tensor.wait_ge(sx.h, vls)
        if tg == 0:
            R.tensor.wait_ge(sw.h, wload)
            R.scalar.wait_ge(sev.h, bocol_ready)
        for oc in range(ICH):
            pbo = b.pb[oc % 2]
            k = b.hist.get(("pbqkv", oc % 2))
            if k is not None:
                R.tensor.wait_ge(k[0].h, k[1])
            ins = None
            for i in range(ICH):
                ins = R.tensor.matmul(pbo[:, :],
                                      wslab[:, oc * 2048 + i * 128:oc * 2048 + (i + 1) * 128],
                                      crhs[:, i * 512:(i + 1) * 512],
                                      start=(i == 0), stop=(i == ICH - 1))
            ins.then_inc(smm.h, 1)
            vm = smm.inc(1)
            R.scalar.wait_ge(smm.h, vm)
            if oc == 0:
                k = b.hist.get("outfm")
                if k is not None:
                    R.scalar.wait_ge(k[0].h, k[1])
            R.scalar.activation(out=outfm[:, oc * 512:(oc + 1) * 512], in_=pbo[:, :],
                                func=AF.Identity, bias=b.stats2[:, 16 + oc:17 + oc]) \
                .then_inc(sact.h, 1)
            b.hist[("pbqkv", oc % 2)] = (sact, sact.inc(1))
        outfm_done = sact.n
        b.hist["crhs"] = (smm, smm.n)
        R.tensor.wait_ge(sact.h, outfm_done)
        for tt in range(4):
            row0 = tg * 512 + tt * 128
            xs = b.xt[:, (tt % 2) * D:(tt % 2) * D + D]
            k = b.hist.get(("ln_x", tt % 2))
            if k is not None:
                R.sync.wait_ge(k[0].h, k[1])
            R.sync.dma_start(out=xs, in_=src[row0:row0 + 128, :]).then_inc(sx.h, 16)
            vx = sx.inc(16)
            R.vector.wait_ge(sx.h, vx)
            res = b.res[:, (tt % 2) * D:(tt % 2) * D + D]
            k = b.hist.get(("res", tt % 2))
            if k is not None:
                R.vector.wait_ge(k[0].h, k[1])
            for r in range(4):
                ptb = b.pt[r % 2]
                k = b.hist.get(("pt", r % 2))
                if k is not None:
                    R.tensor.wait_ge(k[0].h, k[1])
                ins = None
                for kk in range(4):
                    oc = r * 4 + kk
                    ins = R.tensor.transpose(
                        ptb[:, kk * 128:(kk + 1) * 128],
                        outfm[:, oc * 512 + tt * 128:oc * 512 + (tt + 1) * 128],
                        b.ident_bf[:, :])
                ins.then_inc(strp.h, 1)
                vt = strp.inc(1)
                R.vector.wait_ge(strp.h, vt)
                R.vector.scalar_tensor_tensor(
                    out=res[:, r * 512:(r + 1) * 512], in0=ptb[:, 0:512],
                    scalar=1.0, in1=xs[:, r * 512:(r + 1) * 512],
                    op0=ALU.mult, op1=ALU.add).then_inc(sev.h, 1)
                b.hist[("pt", r % 2)] = (sev, sev.inc(1))
            b.hist[("ln_x", tt % 2)] = (sev, sev.n)
            R.sync.wait_ge(sev.h, sev.n)
            R.sync.dma_start(out=dst[row0:row0 + 128, :], in_=res).then_inc(sst.h, 16)
            b.hist[("res", tt % 2)] = (sst, sst.inc(16))
        b.hist["outfm"] = (b.sem["str"], b.sem["str"].n)


def emit_l1ffn_prepass(R, b):
    sw, sev, sst = b.sem["sw"], b.sem["sev"], b.sem["sst"]
    jobs = []
    gw, uw, dw = b.inp["L1_ffn_gate_w"], b.inp["L1_ffn_up_w"], b.inp["L1_ffn_down_w"]
    sg, su, sd = b.inp["L1_ffn_sh_gate_w"], b.inp["L1_ffn_sh_up_w"], b.inp["L1_ffn_sh_down_w"]
    for n in range(8):
        for j in range(11):
            jobs.append((gw[n][j * 128:(j + 1) * 128, :], 16, b.c_g[n][j]))
            jobs.append((uw[n][j * 128:(j + 1) * 128, :], 16, b.c_u[n][j]))
        for oc in range(16):
            jobs.append((dw[n][oc * 128:(oc + 1) * 128, :], 11, b.c_d[n][oc]))
    for j in range(22):
        jobs.append((sg[j * 128:(j + 1) * 128, :], 16, b.c_shg[j]))
        jobs.append((su[j * 128:(j + 1) * 128, :], 16, b.c_shu[j]))
    for oc in range(16):
        jobs.append((sd[oc * 128:(oc + 1) * 128, :], 22, b.c_shd[oc]))
    slabw = b.carve(110 * 1024, 2 * 2816 * 2)
    for idx, (srcap, ntile, cdst) in enumerate(jobs):
        w = ntile * 128
        k = b.hist.get("qwnat")
        if k is not None:
            R.gpsimd.wait_ge(k[0].h, k[1])
        R.gpsimd.dma_start(out=b.wnat[:, 0:w], in_=srcap,
                           max_dma_last_dim=8192).then_inc(sw.h, 16)
        vw = sw.inc(16)
        R.tensor.wait_ge(sw.h, vw)
        slot = slabw[:, (idx % 2) * 2816:(idx % 2) * 2816 + w]
        k = b.hist.get(("l1slab", idx % 2))
        if k is not None:
            R.vector.wait_ge(k[0].h, k[1])
        evdone = emit_transpose16(R, b, b.wnat[:, 0:w], slot, ntile, "pp")
        b.hist["qwnat"] = (b.sem["str"], b.sem["str"].n)
        R.sync.wait_ge(b.sem["ste"].h, evdone)
        R.sync.dma_start(out=cdst, in_=slot).then_inc(sst.h, 16)
        b.hist[("l1slab", idx % 2)] = (sst, sst.inc(16))
    b.hist["l1pp_done"] = (sst, sst.n)


def emit_l1ffn(R, b, src, dst, first_call):
    emit_barrier(R, b)
    if first_call:
        emit_l1ffn_prepass(R, b)
        emit_barrier(R, b)
    carve = b.carve
    sx, sw, sw2 = b.sem["sx"], b.sem["sw"], b.sem["sw2"]
    smm, sev, sact = b.sem["smm"], b.sem["sev"], b.sem["sact"]
    sst, strp, ste = b.sem["sst"], b.sem["str"], b.sem["ste"]
    st, st2 = b.stats, b.stats2
    a_of = lambda n, j: carve(n * 11264 + j * 1024, 1024)
    ash_of = lambda j: carve(90112 + j * 1024, 1024)
    slabg = carve(112640, 8192)
    slabu = carve(120832, 8192)
    silu2 = carve(129024, 4096, F32)
    wbc = carve(133120, 8192)
    slabd = carve(141312, 5632)
    stage4 = carve(112640, 4096)
    wrow = carve(0, 8192)[0:1, :]
    gwT = b.qsb
    R.gpsimd.dma_start(out=b.wnat[0:8, 0:2048], in_=b.inp["L1_ffn_gate_weight"][:, :],
                       max_dma_last_dim=8192).then_inc(sw.h, 16)
    vw = sw.inc(16)
    R.tensor.wait_ge(sw.h, vw)
    k = b.hist.get("pbT")
    if k is not None:
        R.tensor.wait_ge(k[0].h, k[1])
    ins = None
    for i in range(ICH):
        ins = R.tensor.transpose(b.pbT.bitcast(BF16)[:, i * 8:(i + 1) * 8],
                                 b.wnat[0:8, i * 128:(i + 1) * 128],
                                 b.ident_bf[0:8, 0:8])
    ins.then_inc(strp.h, 1)
    vt = strp.inc(1)
    R.vector.wait_ge(strp.h, vt)
    R.vector.tensor_copy(gwT[:, :], b.pbT.bitcast(BF16)[:, 0:128]).then_inc(sev.h, 1)
    b.hist["pbT"] = (sev, sev.inc(1))
    gwT_ready = sev.n
    R.tensor.wait_ge(sev.h, gwT_ready)
    for tg in range(NTG):
        war = b.hist.get(("lnfm", 0))
        emit_ln_group(R, b, src, tg, 0, war=war)
        ln_ready = b.sem["slnT"].n
        R.tensor.wait_ge(b.sem["slnT"].h, ln_ready)
        if b.debug and tg == 0:
            R.sync.wait_ge(b.sem["slnT"].h, ln_ready)
            R.sync.dma_start(out=b.dbg_ln, in_=b.lnfm[:, :]).then_inc(b.sem["sst"].h, 16)
            b.sem["sst"].inc(16)
        rs = carve(8192, 2048, F32)
        k = b.hist.get("wrow_free")
        if k is not None:
            R.vector.wait_ge(k[0].h, k[1])
        R.vector.wait_ge(b.sem["smm"].h, b.sem["smm"].n)
        for tt in range(4):
            if tt == 0:
                k = b.hist.get("pb4")
                if k is not None:
                    R.tensor.wait_ge(k[0].h, k[1])
            ins = None
            for i in range(ICH):
                ins = R.tensor.matmul(b.pb[4][:, tt * 8:(tt + 1) * 8],
                                      lnfm_tile(b, 0, i, tt * 128, 128),
                                      gwT[:, i * 8:(i + 1) * 8],
                                      start=(i == 0), stop=(i == ICH - 1))
            ins.then_inc(smm.h, 1)
            smm.inc(1)
        vm = smm.n
        v = R.vector
        v.wait_ge(smm.h, vm)
        c = lambda a, w=32: rs[:, a:a + w]
        L3 = lambda a, w=8: rs[:, a:a + 32].rearrange("p (t e) -> p t e", e=w)
        bc4 = lambda a, reps=8: rs[:, a:a + 4].unsqueeze(2).broadcast_to([128, 4, reps])
        v.tensor_copy(c(0), b.pb[4][:, 0:32]).then_inc(sev.h, 1)
        b.hist["pb4"] = (sev, sev.inc(1))
        vspace(R, b)
        v.tensor_reduce(out=c(400, 4), in_=L3(0), axis=AX.X, op=ALU.max)
        vspace(R, b)
        v.tensor_tensor(out=L3(32), in0=L3(0), in1=bc4(400), op=ALU.subtract)
        ve = vpub(R, b, sev)
        R.scalar.wait_ge(sev.h, ve)
        R.scalar.activation(out=c(64), in_=c(32), func=AF.Exp)
        va = spub(R, b, sact)
        v.wait_ge(sact.h, va)
        v.tensor_reduce(out=c(404, 4), in_=L3(64), axis=AX.X, op=ALU.add)
        vspace(R, b)
        v.reciprocal(out=c(408, 4), in_=c(404, 4))
        vspace(R, b)
        v.tensor_tensor(out=L3(96), in0=L3(64), in1=bc4(408), op=ALU.mult)
        vspace(R, b)
        v.tensor_tensor(out=c(128, 16).rearrange("p (t g) -> p t g", g=4),
                        in0=L3(96)[:, :, 0::2], in1=L3(96)[:, :, 1::2], op=ALU.max)
        vspace(R, b)
        v.tensor_reduce(out=c(412, 4), in_=c(128, 16).rearrange("p (t g) -> p t g", g=4),
                        axis=AX.X, op=ALU.max)
        vspace(R, b)
        v.tensor_tensor(out=c(144, 16).rearrange("p (t g) -> p t g", g=4),
                        in0=c(128, 16).rearrange("p (t g) -> p t g", g=4),
                        in1=bc4(412, 4), op=ALU.is_equal)
        vspace(R, b)
        v.scalar_tensor_tensor(out=c(160, 16), in0=c(144, 16), scalar=BIG,
                               in1=c(128, 16), op0=ALU.mult, op1=ALU.add)
        vspace(R, b)
        v.tensor_reduce(out=c(416, 4), in_=c(160, 16).rearrange("p (t g) -> p t g", g=4),
                        axis=AX.X, op=ALU.max)
        vspace(R, b)
        v.tensor_tensor(out=c(176, 16).rearrange("p (t g) -> p t g", g=4),
                        in0=c(160, 16).rearrange("p (t g) -> p t g", g=4),
                        in1=bc4(416, 4), op=ALU.is_equal)
        vspace(R, b)
        v.tensor_tensor(out=c(192, 16), in0=c(144, 16), in1=c(176, 16), op=ALU.add)
        vspace(R, b)
        gmx = c(192, 16).rearrange("p (t g) -> p t g", g=4).unsqueeze(3) \
            .broadcast_to([128, 4, 4, 2])
        v.tensor_tensor(out=c(208).rearrange("p (t g e) -> p t g e", g=4, e=2),
                        in0=L3(96).rearrange("p t (g e) -> p t g e", e=2),
                        in1=gmx, op=ALU.mult)
        vspace(R, b)
        v.tensor_reduce(out=c(420, 4), in_=L3(208), axis=AX.X, op=ALU.max)
        vspace(R, b)
        v.tensor_tensor(out=L3(240), in0=L3(208), in1=bc4(420), op=ALU.is_equal)
        vspace(R, b)
        v.scalar_tensor_tensor(out=c(272), in0=c(240), scalar=BIG, in1=c(208),
                               op0=ALU.mult, op1=ALU.add)
        vspace(R, b)
        v.tensor_reduce(out=c(424, 4), in_=L3(272), axis=AX.X, op=ALU.max)
        vspace(R, b)
        v.tensor_tensor(out=L3(304), in0=L3(272), in1=bc4(424), op=ALU.is_equal)
        vspace(R, b)
        v.tensor_tensor(out=c(336), in0=c(240), in1=c(304), op=ALU.add)
        vspace(R, b)
        v.tensor_tensor(out=c(368), in0=c(208), in1=c(336), op=ALU.mult)
        vwm = vpub(R, b, sev)
        R.tensor.wait_ge(sev.h, vwm)
        for tt in range(4):
            for half in range(2):
                k = b.hist.get("pbT")
                if k is not None:
                    R.tensor.wait_ge(k[0].h, k[1])
                ins = None
                for nn in range(4):
                    n = half * 4 + nn
                    col = 368 + tt * 8 + n
                    ins = R.tensor.transpose(b.pbT[0:1, nn * 128:(nn + 1) * 128],
                                             rs[:, col:col + 1], b.ident_f32[:, :])
                ins.then_inc(strp.h, 1)
                vt = strp.inc(1)
                v.wait_ge(strp.h, vt)
                dstw = wrow[0:1, tt * 1024 + half * 512:tt * 1024 + (half + 1) * 512]
                v.tensor_copy(dstw, b.pbT[0:1, 0:512]).then_inc(sev.h, 1)
                b.hist["pbT"] = (sev, sev.inc(1))
        vpub(R, b, sev)
        wrow_done = sev.n
        R.tensor.wait_ge(sev.h, wrow_done)
        for n in range(8):
            k = b.hist.get("pbwb")
            if k is not None:
                R.tensor.wait_ge(k[0].h, k[1])
            rhs = wrow[0:1, :].rearrange("q (a e cc) -> q a e cc", a=4, e=8)[:, :, n, :]
            R.tensor.matmul(b.pb[3][:, :].rearrange("p (a cc) -> p a cc", cc=128),
                            b.ones1[0:1, :], rhs,
                            start=True, stop=True).then_inc(smm.h, 1)
            vm = smm.inc(1)
            R.vector.wait_ge(smm.h, vm)
            R.vector.tensor_copy(wbc[:, n * 512:(n + 1) * 512], b.pb[3][:, :]) \
                .then_inc(sev.h, 1)
            b.hist["pbwb"] = (sev, sev.inc(1))
        wbc_done = sev.n
        b.hist["wrow_free"] = (sev, wbc_done)
        if b.debug:
            R.sync.wait_ge(sev.h, vwm)
            R.sync.dma_start(out=b.dbg_wmat[tg], in_=rs[:, 368:400]).then_inc(sst.h, 16)
            sst.inc(16)
            R.sync.wait_ge(sev.h, wbc_done)
            R.sync.dma_start(out=b.dbg_wbc[tg], in_=wbc[:, :]).then_inc(sst.h, 16)
            sst.inc(16)
            b.hist["wrow_free"] = (sst, sst.n)
        tmpa = b.lnt[:, 0:512]
        R.tensor.wait_ge(sev.h, wbc_done)
        R.gpsimd.wait_ge(strp.h, strp.n)
        R.vector.wait_ge(smm.h, smm.n)
        for n in range(9):
            nch = 11 if n < 8 else 22
            for j in range(nch):
                sl = (n * 11 + j) % 2 if n < 8 else (88 + j) % 2
                gslot = slabg[:, sl * 2048:sl * 2048 + 2048]
                uslot = slabu[:, sl * 2048:sl * 2048 + 2048]
                k = b.hist.get(("l1gw", sl))
                if k is not None:
                    R.gpsimd.wait_ge(k[0].h, k[1])
                gs = b.c_g[n][j] if n < 8 else b.c_shg[j]
                us = b.c_u[n][j] if n < 8 else b.c_shu[j]
                R.gpsimd.dma_start(out=gslot, in_=gs).then_inc(sw.h, 16)
                R.gpsimd.dma_start(out=uslot, in_=us).then_inc(sw.h, 16)
                vw = sw.inc(32)
                pbg = b.pb[sl * 2]
                pbu = b.pb[sl * 2 + 1]
                k = b.hist.get(("pbgu", sl))
                if k is not None:
                    R.tensor.wait_ge(k[0].h, k[1])
                R.tensor.wait_ge(sw.h, vw)
                insg = insu = None
                for i in range(ICH):
                    insg = R.tensor.matmul(pbg[:, :], gslot[:, i * 128:(i + 1) * 128],
                                           lnfm_tile(b, 0, i, 0, 512),
                                           start=(i == 0), stop=(i == ICH - 1))
                insg.then_inc(smm.h, 1)
                vg = smm.inc(1)
                for i in range(ICH):
                    insu = R.tensor.matmul(pbu[:, :], uslot[:, i * 128:(i + 1) * 128],
                                           lnfm_tile(b, 0, i, 0, 512),
                                           start=(i == 0), stop=(i == ICH - 1))
                insu.then_inc(smm.h, 1)
                vu = smm.inc(1)
                b.hist[("l1gw", sl)] = (smm, vu)
                sislot = silu2[:, sl * 512:sl * 512 + 512]
                R.scalar.wait_ge(smm.h, vg)
                k = b.hist.get(("l1si", sl))
                if k is not None:
                    R.scalar.wait_ge(k[0].h, k[1])
                R.scalar.activation(out=sislot, in_=pbg[:, :], func=AF.Silu) \
                    .then_inc(sact.h, 1)
                va = sact.inc(1)
                av = a_of(n, j) if n < 8 else ash_of(j)
                R.vector.wait_ge(sact.h, va)
                R.vector.wait_ge(smm.h, vu)
                if n < 8:
                    R.vector.tensor_tensor(out=tmpa, in0=pbu[:, :],
                                           in1=wbc[:, n * 512:(n + 1) * 512], op=ALU.mult)
                    R.vector.tensor_tensor(out=av, in0=tmpa, in1=sislot,
                                           op=ALU.mult).then_inc(sev.h, 1)
                else:
                    R.vector.tensor_tensor(out=av, in0=pbu[:, :], in1=sislot,
                                           op=ALU.mult).then_inc(sev.h, 1)
                hval = sev.inc(1)
                b.hist[("pbgu", sl)] = (sev, hval)
                b.hist[("l1si", sl)] = (sev, hval)
        a_done = sev.n
        R.gpsimd.wait_ge(sev.h, a_done)
        if b.debug and tg == 0:
            R.sync.wait_ge(sev.h, a_done)
            R.sync.dma_start(out=b.dbg_a, in_=carve(0, 112640)).then_inc(sst.h, 16)
            sst.inc(16)
        R.tensor.wait_ge(sev.h, a_done)
        for og in range(4):
            for kk in range(4):
                oc = og * 4 + kk
                pbo = b.pb[kk % 2]
                k = b.hist.get(("pbdown", kk % 2))
                if k is not None:
                    R.tensor.wait_ge(k[0].h, k[1])
                nmm = 0
                ins = None
                for n in range(9):
                    nch = 11 if n < 8 else 22
                    if n < 8:
                        dsl = n % 2
                        dslot = slabd[:, dsl * 1408:dsl * 1408 + 1408]
                        guards = [("l1dw", dsl)]
                    else:
                        dslot = slabd[:, 0:2816]
                        guards = [("l1dw", 0), ("l1dw", 1)]
                    for gk in guards:
                        k = b.hist.get(gk)
                        if k is not None:
                            R.gpsimd.wait_ge(k[0].h, k[1])
                    ds = b.c_d[n][oc] if n < 8 else b.c_shd[oc]
                    R.gpsimd.dma_start(out=dslot, in_=ds).then_inc(sw2.h, 16)
                    vw = sw2.inc(16)
                    R.tensor.wait_ge(sw2.h, vw)
                    for j in range(nch):
                        av = a_of(n, j) if n < 8 else ash_of(j)
                        ins = R.tensor.matmul(pbo[:, :], dslot[:, j * 128:(j + 1) * 128],
                                              av, start=(nmm == 0),
                                              stop=(n == 8 and j == nch - 1))
                        nmm += 1
                    ins.then_inc(smm.h, 1)
                    val = smm.inc(1)
                    for gk in guards:
                        b.hist[gk] = (smm, val)
                vm = smm.n
                R.scalar.wait_ge(smm.h, vm)
                k = b.hist.get(("st4", kk % 2))
                if k is not None:
                    R.scalar.wait_ge(k[0].h, k[1])
                R.scalar.activation(out=stage4[:, kk * 512:(kk + 1) * 512],
                                    in_=pbo[:, :], func=AF.Copy, bias=0.0) \
                    .then_inc(sact.h, 1)
                b.hist[("pbdown", kk % 2)] = (sact, sact.inc(1))
            st4_done = sact.n
            R.tensor.wait_ge(sact.h, st4_done)
            for tt in range(4):
                row0 = tg * 512 + tt * 128
                xs = b.xt[:, (tt % 2) * D + og * 512:(tt % 2) * D + (og + 1) * 512]
                k = b.hist.get(("ln_x", tt % 2))
                if k is not None:
                    R.sync.wait_ge(k[0].h, k[1])
                R.sync.dma_start(out=xs, in_=src[row0:row0 + 128, og * 512:(og + 1) * 512]) \
                    .then_inc(sx.h, 16)
                vx = sx.inc(16)
                R.vector.wait_ge(sx.h, vx)
                res = b.res[:, (tt % 2) * D + og * 512:(tt % 2) * D + (og + 1) * 512]
                k = b.hist.get(("res", tt % 2, og))
                if k is not None:
                    R.vector.wait_ge(k[0].h, k[1])
                ptb = b.pt[tt % 2]
                k = b.hist.get(("pt", tt % 2))
                if k is not None:
                    R.tensor.wait_ge(k[0].h, k[1])
                ins = None
                for kk in range(4):
                    ins = R.tensor.transpose(
                        ptb[:, kk * 128:(kk + 1) * 128],
                        stage4[:, kk * 512 + tt * 128:kk * 512 + (tt + 1) * 128],
                        b.ident_bf[:, :])
                ins.then_inc(strp.h, 1)
                vt = strp.inc(1)
                R.vector.wait_ge(strp.h, vt)
                R.vector.scalar_tensor_tensor(
                    out=res, in0=ptb[:, 0:512], scalar=0.5, in1=xs,
                    op0=ALU.mult, op1=ALU.add).then_inc(sev.h, 1)
                b.hist[("pt", tt % 2)] = (sev, sev.inc(1))
                b.hist[("ln_x", tt % 2)] = (sev, sev.n)
                R.sync.wait_ge(sev.h, sev.n)
                R.sync.dma_start(out=dst[row0:row0 + 128, og * 512:(og + 1) * 512],
                                 in_=res).then_inc(sst.h, 16)
                b.hist[("res", tt % 2, og)] = (sst, sst.inc(16))
            b.hist[("st4", 0)] = (b.sem["str"], b.sem["str"].n)
            b.hist[("st4", 1)] = (b.sem["str"], b.sem["str"].n)
        b.hist[("lnfm", 0)] = (smm, smm.n)


def emit_l1attn(R, b, src, dst):
    emit_barrier(R, b)
    carve = b.carve
    sx, sw, sw2 = b.sem["sx"], b.sem["sw"], b.sem["sw2"]
    smm, sev, sact = b.sem["smm"], b.sem["sev"], b.sem["sact"]
    sst, strp, ste = b.sem["sst"], b.sem["str"], b.sem["ste"]
    sq, sk = b.sem["sq"], b.sem["sk"]
    st, st2 = b.stats, b.stats2
    SSCALE = 1.0 / 8.0
    TSCALE = float(1.0 / (np.sqrt(np.float32(128.0)) * 30.0))
    wdown = carve(0, 16384)
    wup = carve(16384, 24576 - 16384)
    qnb0 = carve(24576, 512, F32)
    knb0 = carve(25088, 512, F32)
    bbq = carve(25600, 4096)[0:1, :]
    bbd = carve(29696, 1024)[0:1, :]
    bbu = carve(30720, 2048)[0:1, :]
    qrow = carve(32768, 512, F32)[0:1, :]
    for i in range(ICH):
        R.gpsimd.dma_start(out=wdown[:, i * 512:(i + 1) * 512],
                           in_=b.inp["L1_attn_w_down"][i * 128:(i + 1) * 128, :],
                           max_dma_last_dim=8192).then_inc(sw.h, 16)
        sw.inc(16)
    for l in range(4):
        R.gpsimd.dma_start(out=wup[:, l * 1024:(l + 1) * 1024],
                           in_=b.inp["L1_attn_w_up"][l * 128:(l + 1) * 128, :],
                           max_dma_last_dim=8192).then_inc(sw.h, 16)
        sw.inc(16)
    R.gpsimd.dma_start(out=bbq, in_=b.inp["L1_attn_b_q"].unsqueeze(0),
                       max_dma_last_dim=8192).then_inc(sw.h, 16)
    R.gpsimd.dma_start(out=bbd, in_=b.inp["L1_attn_b_down"].unsqueeze(0),
                       max_dma_last_dim=8192).then_inc(sw.h, 16)
    R.gpsimd.dma_start(out=bbu, in_=b.inp["L1_attn_b_up"].unsqueeze(0),
                       max_dma_last_dim=8192).then_inc(sw.h, 16)
    sw.inc(48)
    R.sync.dma_start(out=qrow, in_=b.inp["L1_attn_qn_w"].unsqueeze(0)).then_inc(sx.h, 16)
    R.sync.dma_start(out=carve(33280, 512, F32)[0:1, :],
                     in_=b.inp["L1_attn_kn_w"].unsqueeze(0)).then_inc(sx.h, 16)
    vb = sx.inc(32)
    R.tensor.wait_ge(sx.h, vb)
    R.tensor.matmul(b.pb[0][:, 0:128], b.ones1f[0:1, :], qrow, start=True, stop=True)
    R.tensor.matmul(b.pb[1][:, 0:128], b.ones1f[0:1, :], carve(33280, 512, F32)[0:1, :],
                    start=True, stop=True).then_inc(smm.h, 1)
    vm = smm.inc(1)
    R.vector.wait_ge(smm.h, vm)
    R.vector.tensor_copy(qnb0, b.pb[0][:, 0:128])
    R.vector.tensor_copy(knb0, b.pb[1][:, 0:128]).then_inc(sev.h, 1)
    qnb_done = sev.inc(1)
    qt = carve(65536, 8192, F32)
    qt2 = carve(73728, 8192, F32)
    qbf = carve(81920, 4096)
    qstage = carve(86016, 4096)
    cs = carve(90112, 512, F32)
    wqp = carve(90624, 2048)
    latst = carve(92672, 2048)
    latfm = carve(32768 + 0, 32768) if False else carve(32768, 32768)
    R.tensor.wait_ge(sev.h, qnb_done)
    for tg in range(NTG):
        war = b.hist.get(("lnfm", 0))
        emit_ln_group(R, b, src, tg, 0, war=war)
        ln_ready = b.sem["slnT"].n
        R.tensor.wait_ge(b.sem["slnT"].h, ln_ready)
        for tt in range(4):
            trow = tg * 512 + tt * 128
            for og in range(4):
                pbo = b.pb[og]
                k = b.hist.get(("pbq", og))
                if k is not None:
                    R.tensor.wait_ge(k[0].h, k[1])
                ins = None
                for i in range(ICH):
                    sl = i % 2
                    k = b.hist.get(("wqp", sl))
                    if k is not None:
                        R.gpsimd.wait_ge(k[0].h, k[1])
                    R.gpsimd.dma_start(
                        out=wqp[:, sl * 512:(sl + 1) * 512],
                        in_=b.inp["L1_attn_w_q"][i * 128:(i + 1) * 128,
                                                 og * 512:(og + 1) * 512],
                        max_dma_last_dim=8192).then_inc(sw2.h, 16)
                    vw = sw2.inc(16)
                    R.tensor.wait_ge(sw2.h, vw)
                    ins = R.tensor.matmul(pbo[:, :], lnfm_tile(b, 0, i, tt * 128, 128),
                                          wqp[:, sl * 512:(sl + 1) * 512],
                                          start=(i == 0), stop=False)
                    ins.then_inc(smm.h, 1)
                    b.hist[("wqp", sl)] = (smm, smm.inc(1))
                ins = R.tensor.matmul(pbo[:, :], b.ones1[0:1, tt * 128:(tt + 1) * 128]
                                      if False else b.ones1[0:1, :],
                                      bbq[0:1, og * 512:(og + 1) * 512],
                                      start=False, stop=True)
                ins.then_inc(smm.h, 1)
                vm = smm.inc(1)
                R.scalar.wait_ge(smm.h, vm)
                if og == 0:
                    k = b.hist.get("qtbuf")
                    if k is not None:
                        R.scalar.wait_ge(k[0].h, k[1])
                R.scalar.activation(out=qt.bitcast(F32)[:, og * 512:(og + 1) * 512]
                                    if False else qt[:, og * 512:(og + 1) * 512],
                                    in_=pbo[:, :], func=AF.Copy, bias=0.0).then_inc(sact.h, 1)
                b.hist[("pbq", og)] = (sact, sact.inc(1))
            qdone = sact.n
            v = R.vector
            v.wait_ge(sact.h, qdone)
            v.tensor_tensor(out=qt2, in0=qt, in1=qt, op=ALU.mult)
            v.tensor_reduce(out=st2[:, 0:16], in_=qt2.rearrange("p (h e) -> p h e", e=128),
                            axis=AX.X, op=ALU.add)
            vspace(R, b)
            v.tensor_scalar(out=st2[:, 16:32], in0=st2[:, 0:16], scalar1=1.0 / 128,
                            scalar2=L1_EPS, op0=ALU.mult, op1=ALU.add)
            vspace(R, b)
            v.reciprocal(out=st2[:, 0:16], in_=st2[:, 16:32])
            vr = vpub(R, b, sev)
            R.scalar.wait_ge(sev.h, vr)
            R.scalar.activation(out=st2[:, 16:32], in_=st2[:, 0:16], func=AF.Sqrt)
            va = spub(R, b, sact)
            v.wait_ge(sact.h, va)
            for h in range(16):
                v.tensor_scalar(out=qt2[:, h * 128:(h + 1) * 128],
                                in0=qt[:, h * 128:(h + 1) * 128],
                                scalar1=st2[:, 16 + h:17 + h], scalar2=None, op0=ALU.mult)
            v.tensor_tensor(out=qt.rearrange("p (h e) -> p h e", e=128), 
                            in0=qt2.rearrange("p (h e) -> p h e", e=128),
                            in1=qnb0.unsqueeze(1).broadcast_to([128, 16, 128]), op=ALU.mult)
            k = b.hist.get("csfree")
            if k is not None:
                R.sync.wait_ge(k[0].h, k[1])
            R.sync.dma_start(out=cs[:, 0:64], in_=b.inp["tab_cos"][trow:trow + 128, :]) \
                .then_inc(sx.h, 16)
            R.sync.dma_start(out=cs[:, 64:128], in_=b.inp["tab_sin"][trow:trow + 128, :]) \
                .then_inc(sx.h, 16)
            vcs = sx.inc(32)
            v.wait_ge(sx.h, vcs)
            ev = qt[:, 0::2].rearrange("p (h j) -> p h j", j=64)
            od = qt[:, 1::2].rearrange("p (h j) -> p h j", j=64)
            ctile = cs[:, 0:64].unsqueeze(1).broadcast_to([128, 16, 64])
            stile = cs[:, 64:128].unsqueeze(1).broadcast_to([128, 16, 64])
            o_ev = qt2[:, 0::2].rearrange("p (h j) -> p h j", j=64)
            o_od = qt2[:, 1::2].rearrange("p (h j) -> p h j", j=64)
            t1v = qbf
            tmp1 = carve(81920, 4096, F32)
            tmp2 = carve(86016, 4096, F32)
            v.tensor_tensor(out=tmp1.rearrange("p (h j) -> p h j", j=64), in0=ev, in1=ctile, op=ALU.mult)
            v.tensor_tensor(out=tmp2.rearrange("p (h j) -> p h j", j=64), in0=od, in1=stile, op=ALU.mult)
            v.tensor_tensor(out=o_ev, in0=tmp1.rearrange("p (h j) -> p h j", j=64),
                            in1=tmp2.rearrange("p (h j) -> p h j", j=64), op=ALU.subtract)
            v.tensor_tensor(out=tmp1.rearrange("p (h j) -> p h j", j=64), in0=ev, in1=stile, op=ALU.mult)
            v.tensor_tensor(out=tmp2.rearrange("p (h j) -> p h j", j=64), in0=od, in1=ctile, op=ALU.mult)
            v.tensor_tensor(out=o_od, in0=tmp1.rearrange("p (h j) -> p h j", j=64),
                            in1=tmp2.rearrange("p (h j) -> p h j", j=64), op=ALU.add)
            qb16 = carve(94720, 4096)
            v.tensor_copy(qb16, qt2).then_inc(sev.h, 1)
            vq16 = sev.inc(1)
            b.hist["qtbuf"] = (sev, vq16)
            b.hist["csfree"] = (sev, vq16)
            R.tensor.wait_ge(sev.h, vq16)
            for g4 in range(4):
                ptb = b.pt[g4 % 2]
                k = b.hist.get(("pt", g4 % 2))
                if k is not None:
                    R.tensor.wait_ge(k[0].h, k[1])
                ins = None
                for jj in range(4):
                    i = g4 * 4 + jj
                    ins = R.tensor.transpose(ptb[:, jj * 128:(jj + 1) * 128],
                                             qb16[:, i * 128:(i + 1) * 128],
                                             b.ident_bf[:, :])
                ins.then_inc(strp.h, 1)
                vt = strp.inc(1)
                k = b.hist.get("qstg")
                if k is not None:
                    R.vector.wait_ge(k[0].h, k[1])
                R.vector.wait_ge(strp.h, vt)
                R.vector.tensor_copy(qstage[:, g4 * 512:(g4 + 1) * 512], ptb[:, 0:512]) \
                    .then_inc(ste.h, 1)
                b.hist[("pt", g4 % 2)] = (ste, ste.inc(1))
            vts = ste.n
            R.sync.wait_ge(ste.h, vts)
            for i in range(ICH):
                R.sync.dma_start(out=b.qfm[i][:, trow:trow + 128],
                                 in_=qstage[:, i * 128:(i + 1) * 128]).then_inc(sst.h, 16)
                sst.inc(16)
            b.hist["qstg"] = (sst, sst.n)
            pbl = b.pb[4]
            k = b.hist.get("pblat")
            if k is not None:
                R.tensor.wait_ge(k[0].h, k[1])
            ins = None
            for i in range(ICH):
                ins = R.tensor.matmul(pbl[:, :], lnfm_tile(b, 0, i, tt * 128, 128),
                                      wdown[:, i * 512:(i + 1) * 512],
                                      start=(i == 0), stop=False)
            ins = R.tensor.matmul(pbl[:, :], b.ones1[0:1, :], bbd, start=False, stop=True)
            ins.then_inc(smm.h, 1)
            vm = smm.inc(1)
            R.scalar.wait_ge(smm.h, vm)
            k = b.hist.get(("latst", tt % 2))
            if k is not None:
                R.scalar.wait_ge(k[0].h, k[1])
            R.scalar.activation(out=latst[:, (tt % 2) * 512:(tt % 2) * 512 + 512],
                                in_=pbl[:, :], func=AF.Copy, bias=0.0).then_inc(sact.h, 1)
            vl = sact.inc(1)
            b.hist["pblat"] = (sact, vl)
            R.tensor.wait_ge(sact.h, vl)
            ptb = b.pt[tt % 2]
            k = b.hist.get(("pt", tt % 2))
            if k is not None:
                R.tensor.wait_ge(k[0].h, k[1])
            ins = None
            for jj in range(4):
                ins = R.tensor.transpose(ptb[:, jj * 128:(jj + 1) * 128],
                                         latst[:, (tt % 2) * 512 + jj * 128:(tt % 2) * 512 + (jj + 1) * 128],
                                         b.ident_bf[:, :])
            ins.then_inc(strp.h, 1)
            vt = strp.inc(1)
            R.vector.wait_ge(strp.h, vt)
            dstl = latfm.rearrange("p (l t) -> p l t", l=4)[:, :, trow:trow + 128]
            R.vector.tensor_copy(dstl, ptb[:, 0:512].rearrange("p (l t) -> p l t", l=4)) \
                .then_inc(ste.h, 1)
            lat_ev = ste.inc(1)
            b.hist[("pt", tt % 2)] = (ste, lat_ev)
            b.hist[("latst", tt % 2)] = (ste, lat_ev)
        b.hist[("lnfm", 0)] = (smm, smm.n)
    lat_done = b.sem["ste"].n
    kvev = carve(0, 8192, F32)
    ksq = carve(8192, 2048, F32)
    kno = carve(10240, 2048, F32)
    ktmp1 = carve(12288, 2048, F32)
    ktmp2 = carve(14336, 2048, F32)
    kcs = carve(131072, 512, F32)
    kbf = carve(131584, 2048)
    kfm = carve(65536, 32768)
    vtm = carve(98304, 32768)
    R.tensor.wait_ge(b.sem["ste"].h, lat_done)
    for tt2 in range(NTT):
        trow = tt2 * 128
        pbk = b.pb[tt2 % 2]
        pbv = b.pb[2 + tt2 % 2]
        for og, pbx in ((0, pbk), (1, pbv)):
            k = b.hist.get(("pbkv", og, tt2 % 2))
            if k is not None:
                R.tensor.wait_ge(k[0].h, k[1])
            ins = None
            for l in range(4):
                ins = R.tensor.matmul(pbx[:, :],
                                      latfm.rearrange("p (l t) -> p l t", l=4)[:, l, trow:trow + 128],
                                      wup[:, l * 1024 + og * 512:l * 1024 + (og + 1) * 512],
                                      start=(l == 0), stop=False)
            ins = R.tensor.matmul(pbx[:, :], b.ones1[0:1, :],
                                  bbu[0:1, og * 512:(og + 1) * 512], start=False, stop=True)
            ins.then_inc(smm.h, 1)
            smm.inc(1)
        vm = smm.n
        R.scalar.wait_ge(smm.h, vm)
        k = b.hist.get(("kvev", tt2 % 2))
        if k is not None:
            R.scalar.wait_ge(k[0].h, k[1])
        kev = kvev[:, (tt2 % 2) * 1024:(tt2 % 2) * 1024 + 512]
        vev = kvev[:, (tt2 % 2) * 1024 + 512:(tt2 % 2) * 1024 + 1024]
        R.scalar.activation(out=kev, in_=pbk[:, :], func=AF.Copy, bias=0.0).then_inc(sact.h, 1)
        R.scalar.activation(out=vev, in_=pbv[:, :], func=AF.Copy, bias=0.0).then_inc(sact.h, 1)
        va = sact.inc(2)
        v = R.vector
        v.wait_ge(sact.h, va)
        k = b.hist.get("vtmw")
        v.tensor_copy(vtm[:, tt2 * 512:(tt2 + 1) * 512], vev).then_inc(sev.h, 1)
        vvv = sev.inc(1)
        b.hist[("pbkv", 1, tt2 % 2)] = (sev, vvv)
        v.tensor_tensor(out=ksq[:, :], in0=kev, in1=kev, op=ALU.mult)
        v.tensor_reduce(out=st2[:, 0:4], in_=ksq[:, :].rearrange("p (h e) -> p h e", e=128),
                        axis=AX.X, op=ALU.add)
        vspace(R, b)
        v.tensor_scalar(out=st2[:, 4:8], in0=st2[:, 0:4], scalar1=1.0 / 128,
                        scalar2=L1_EPS, op0=ALU.mult, op1=ALU.add)
        vspace(R, b)
        v.reciprocal(out=st2[:, 0:4], in_=st2[:, 4:8])
        vr = vpub(R, b, sev)
        R.scalar.wait_ge(sev.h, vr)
        R.scalar.activation(out=st2[:, 4:8], in_=st2[:, 0:4], func=AF.Sqrt)
        va = spub(R, b, sact)
        v.wait_ge(sact.h, va)
        for h in range(4):
            v.tensor_scalar(out=ksq[:, h * 128:(h + 1) * 128],
                            in0=kev[:, h * 128:(h + 1) * 128],
                            scalar1=st2[:, 4 + h:5 + h], scalar2=None, op0=ALU.mult)
        v.tensor_tensor(out=kno[:, :].rearrange("p (h e) -> p h e", e=128),
                        in0=ksq[:, :].rearrange("p (h e) -> p h e", e=128),
                        in1=knb0.unsqueeze(1).broadcast_to([128, 4, 128]), op=ALU.mult)
        k = b.hist.get("csfree2")
        if k is not None:
            R.sync.wait_ge(k[0].h, k[1])
        R.sync.dma_start(out=kcs[:, 0:64], in_=b.inp["tab_cos"][trow:trow + 128, :]) \
            .then_inc(sx.h, 16)
        R.sync.dma_start(out=kcs[:, 64:128], in_=b.inp["tab_sin"][trow:trow + 128, :]) \
            .then_inc(sx.h, 16)
        vcs = sx.inc(32)
        v.wait_ge(sx.h, vcs)
        evk = kno[:, :][:, 0::2].rearrange("p (h j) -> p h j", j=64)
        odk = kno[:, :][:, 1::2].rearrange("p (h j) -> p h j", j=64)
        ck = kcs[:, 0:64].unsqueeze(1).broadcast_to([128, 4, 64])
        sk_ = kcs[:, 64:128].unsqueeze(1).broadcast_to([128, 4, 64])
        o_ev = ksq[:, :][:, 0::2].rearrange("p (h j) -> p h j", j=64)
        o_od = ksq[:, :][:, 1::2].rearrange("p (h j) -> p h j", j=64)
        tmp1 = ktmp1
        tmp2 = ktmp2
        v.tensor_tensor(out=tmp1[:, 0:256].rearrange("p (h j) -> p h j", j=64), in0=evk, in1=ck, op=ALU.mult)
        v.tensor_tensor(out=tmp2[:, 0:256].rearrange("p (h j) -> p h j", j=64), in0=odk, in1=sk_, op=ALU.mult)
        v.tensor_tensor(out=o_ev, in0=tmp1[:, 0:256].rearrange("p (h j) -> p h j", j=64),
                        in1=tmp2[:, 0:256].rearrange("p (h j) -> p h j", j=64), op=ALU.subtract)
        v.tensor_tensor(out=tmp1[:, 0:256].rearrange("p (h j) -> p h j", j=64), in0=evk, in1=sk_, op=ALU.mult)
        v.tensor_tensor(out=tmp2[:, 0:256].rearrange("p (h j) -> p h j", j=64), in0=odk, in1=ck, op=ALU.mult)
        v.tensor_tensor(out=o_od, in0=tmp1[:, 0:256].rearrange("p (h j) -> p h j", j=64),
                        in1=tmp2[:, 0:256].rearrange("p (h j) -> p h j", j=64), op=ALU.add)
        k = b.hist.get(("kbf", tt2 % 2))
        if k is not None:
            v.wait_ge(k[0].h, k[1])
        v.tensor_copy(kbf[:, (tt2 % 2) * 512:(tt2 % 2) * 512 + 512], ksq[:, :]) \
            .then_inc(sev.h, 1)
        vkb = sev.inc(1)
        b.hist["csfree2"] = (sev, vkb)
        b.hist[("pbkv", 0, tt2 % 2)] = (sev, vkb)
        b.hist[("kvev", tt2 % 2)] = (sev, vkb)
        R.tensor.wait_ge(sev.h, vkb)
        ptb = b.pt[tt2 % 2]
        k = b.hist.get(("pt", tt2 % 2))
        if k is not None:
            R.tensor.wait_ge(k[0].h, k[1])
        ins = None
        for jj in range(4):
            ins = R.tensor.transpose(ptb[:, jj * 128:(jj + 1) * 128],
                                     kbf[:, (tt2 % 2) * 512 + jj * 128:(tt2 % 2) * 512 + (jj + 1) * 128],
                                     b.ident_bf[:, :])
        ins.then_inc(strp.h, 1)
        vt = strp.inc(1)
        R.vector.wait_ge(strp.h, vt)
        dstk = kfm.rearrange("p (h t) -> p h t", h=4)[:, :, trow:trow + 128]
        R.vector.tensor_copy(dstk, ptb[:, 0:512].rearrange("p (h t) -> p h t", h=4)) \
            .then_inc(ste.h, 1)
        kev_ = ste.inc(1)
        b.hist[("pt", tt2 % 2)] = (ste, kev_)
        b.hist[("kbf", tt2 % 2)] = (ste, kev_)
    kv_done = b.sem["ste"].n
    if b.debug:
        R.sync.wait_ge(b.sem["ste"].h, kv_done)
        R.sync.dma_start(out=b.dbg_kfm, in_=kfm).then_inc(sst.h, 16)
        R.sync.dma_start(out=b.dbg_vtm, in_=vtm).then_inc(sst.h, 16)
        sst.inc(32)
    S = carve(0, 16384, F32)
    P = carve(16384, 8192)
    P2v = carve(24576, 8192)
    amask = carve(32768, 8192)
    kposb = carve(40960, 16384, F32)
    Sa = carve(24576, 2048, F32)
    btmp = carve(57344, 8192)
    qh0 = carve(131072, 8192)
    qtile = carve(139264, 512)
    T12 = carve(139776, 4096, F32)
    ctxst = carve(143872, 1024)
    PTb = carve(144896, 512)
    R.vector.wait_ge(b.sem["ste"].h, kv_done)
    R.gpsimd.wait_ge(b.sem["ste"].h, kv_done)
    R.gpsimd.wait_ge(sst.h, sst.n)
    R.gpsimd.dma_start(out=qh0, in_=b.qfm[0]).then_inc(sq.h, 16)
    vq0 = sq.inc(16)
    R.sync.dma_start(out=kposb, in_=b.inp["tab_kpos"][0:1, :].to_broadcast((128, T))) \
        .then_inc(sx.h, 16)
    vkp = sx.inc(16)
    R.vector.wait_ge(sx.h, vkp)
    R.tensor.wait_ge(b.sem["ste"].h, kv_done)
    for tq in range(NTT):
        L = (tq + 1) * 128
        nch = (L + 511) // 512
        if tq == 0:
            R.tensor.wait_ge(sq.h, vq0)
            R.tensor.wait_ge(sev.h, sev.n)
        for c in range(nch):
            pbo = b.pb[c % 2]
            k = b.hist.get(("pbS", c % 2))
            if k is not None:
                R.tensor.wait_ge(k[0].h, k[1])
            w = min(512, L - c * 512)
            R.tensor.matmul(pbo[:, 0:w], qh0[0:64, tq * 128:(tq + 1) * 128],
                            kfm[0:64, c * 512:c * 512 + w],
                            start=True, stop=True).then_inc(smm.h, 1)
            vm = smm.inc(1)
            if c == 0:
                k = b.hist.get("Sfree")
                if k is not None:
                    R.vector.wait_ge(k[0].h, k[1])
            R.vector.wait_ge(smm.h, vm)
            ndiag = w - 128 if (c == nch - 1) else w
            if c == nch - 1:
                if ndiag > 0:
                    R.vector.tensor_scalar(out=S[:, c * 512:c * 512 + ndiag],
                                           in0=pbo[:, 0:ndiag], scalar1=SSCALE,
                                           scalar2=None, op0=ALU.mult)
                R.vector.scalar_tensor_tensor(
                    out=S[:, c * 512 + ndiag:c * 512 + ndiag + 128],
                    in0=pbo[:, ndiag:ndiag + 128], scalar=SSCALE, in1=b.tri[:, :],
                    op0=ALU.mult, op1=ALU.add).then_inc(sev.h, 1)
            else:
                R.vector.tensor_scalar(out=S[:, c * 512:(c + 1) * 512],
                                       in0=pbo[:, :], scalar1=SSCALE,
                                       scalar2=None, op0=ALU.mult).then_inc(sev.h, 1)
            b.hist[("pbS", c % 2)] = (sev, sev.inc(1))
        v = R.vector
        v.tensor_reduce(out=st[:, 20:21], in_=S[:, 0:L], axis=AX.X, op=ALU.max)
        vspace(R, b)
        v.tensor_scalar(out=st[:, 21:22], in0=st[:, 20:21], scalar1=-64.0, scalar2=None,
                        op0=ALU.add)
        v.tensor_copy(st[:, 22:23], st[:, 20:21])
        vspace(R, b)
        for it in range(16):
            v.tensor_tensor(out=st[:, 23:24], in0=st[:, 21:22], in1=st[:, 22:23], op=ALU.add)
            vspace(R, b)
            v.tensor_scalar(out=st[:, 24:25], in0=st[:, 23:24], scalar1=0.5, scalar2=None,
                            op0=ALU.mult)
            vspace(R, b)
            v.tensor_scalar(out=btmp[:, 0:L], in0=S[:, 0:L], scalar1=st[:, 24:25],
                            scalar2=None, op0=ALU.is_ge)
            v.tensor_reduce(out=st[:, 25:26], in_=btmp[:, 0:L], axis=AX.X, op=ALU.add)
            vspace(R, b)
            v.tensor_scalar(out=st[:, 26:27], in0=st[:, 25:26], scalar1=64.0, scalar2=None,
                            op0=ALU.is_ge)
            vspace(R, b)
            v.tensor_tensor(out=st[:, 27:28], in0=st[:, 24:25], in1=st[:, 21:22], op=ALU.subtract)
            vspace(R, b)
            v.tensor_tensor(out=st[:, 28:29], in0=st[:, 26:27], in1=st[:, 27:28], op=ALU.mult)
            vspace(R, b)
            v.tensor_tensor(out=st[:, 21:22], in0=st[:, 21:22], in1=st[:, 28:29], op=ALU.add)
            v.tensor_tensor(out=st[:, 29:30], in0=st[:, 24:25], in1=st[:, 22:23], op=ALU.subtract)
            vspace(R, b)
            v.tensor_scalar(out=st[:, 30:31], in0=st[:, 26:27], scalar1=-1.0, scalar2=1.0,
                            op0=ALU.mult, op1=ALU.add)
            vspace(R, b)
            v.tensor_tensor(out=st[:, 31:32], in0=st[:, 30:31], in1=st[:, 29:30], op=ALU.mult)
            vspace(R, b)
            v.tensor_tensor(out=st[:, 22:23], in0=st[:, 22:23], in1=st[:, 31:32], op=ALU.add)
            vspace(R, b)
        if b.debug and tq == 31:
            R.sync.wait_ge(sev.h, sev.n)
            R.sync.dma_start(out=b.dbg_S[0], in_=S[:, 0:T]).then_inc(sst.h, 16)
            R.sync.dma_start(out=b.dbg_S[3][:, 0:32], in_=st[:, 0:32]).then_inc(sst.h, 16)
            sst.inc(32)
        k = b.hist.get("amfree")
        if k is not None:
            v.wait_ge(k[0].h, k[1])
        v.tensor_scalar(out=amask[:, 0:L], in0=S[:, 0:L], scalar1=st[:, 21:22],
                        scalar2=BIG, op0=ALU.is_lt, op1=ALU.mult)
        v.tensor_tensor(out=Sa[:, 0:128], in0=amask[:, L - 128:L], in1=b.diagboost[:, :],
                        op=ALU.mult)
        v.tensor_copy(amask[:, L - 128:L], Sa[:, 0:128]).then_inc(sev.h, 1)
        am_done = sev.inc(1)
        b.hist["Sfree"] = (sev, am_done)
        for h in range(16):
            sl = h % 2
            k = b.hist.get(("qtile", sl))
            if k is not None:
                R.gpsimd.wait_ge(k[0].h, k[1])
            R.gpsimd.dma_start(out=qtile[:, sl * 128:(sl + 1) * 128],
                               in_=b.qfm[h][:, tq * 128:(tq + 1) * 128]).then_inc(sq.h, 16)
            vqt = sq.inc(16)
            slope_h = float(2.0 ** (-0.5 * (h + 1)))
            kvh = h // 4
            R.tensor.wait_ge(sq.h, vqt)
            for c in range(nch):
                pbo = b.pb[2 + c % 2]
                k = b.hist.get(("pbS2", c % 2))
                if k is not None:
                    R.tensor.wait_ge(k[0].h, k[1])
                w = min(512, L - c * 512)
                R.tensor.matmul(pbo[:, 0:w], qtile[:, sl * 128:(sl + 1) * 128],
                                kfm.rearrange("p (g t) -> p g t", g=4)[:, kvh, c * 512:c * 512 + w],
                                start=True, stop=True).then_inc(smm.h, 1)
                vm = smm.inc(1)
                R.scalar.wait_ge(smm.h, vm)
                k = b.hist.get(("T12", c % 2))
                if k is not None:
                    R.scalar.wait_ge(k[0].h, k[1])
                R.scalar.activation(out=T12[:, (c % 2) * 512:(c % 2) * 512 + w],
                                    in_=pbo[:, 0:w], func=AF.Tanh,
                                    scale=TSCALE).then_inc(sact.h, 1)
                va = sact.inc(1)
                b.hist[("pbS2", c % 2)] = (sact, va)
                if c == 0:
                    R.vector.wait_ge(sev.h, am_done)
                    k = b.hist.get("Sfree2")
                    if k is not None:
                        R.vector.wait_ge(k[0].h, k[1])
                    k = b.hist.get("P2free")
                    if k is not None:
                        R.vector.wait_ge(k[0].h, k[1])
                R.vector.wait_ge(sact.h, va)
                R.vector.scalar_tensor_tensor(out=Sa[:, 0:w],
                                              in0=T12[:, (c % 2) * 512:(c % 2) * 512 + w],
                                              scalar=30.0,
                                              in1=amask[:, c * 512:c * 512 + w],
                                              op0=ALU.mult, op1=ALU.add)
                v2 = R.vector.scalar_tensor_tensor(out=S[:, c * 512:c * 512 + w],
                                                   in0=kposb[:, c * 512:c * 512 + w],
                                                   scalar=slope_h, in1=Sa[:, 0:w],
                                                   op0=ALU.mult, op1=ALU.add)
                v2.then_inc(sev.h, 1)
                b.hist[("T12", c % 2)] = (sev, sev.inc(1))
            if b.debug and tq == 31 and h == 0:
                R.sync.wait_ge(sev.h, sev.n)
                R.sync.dma_start(out=b.dbg_S[1], in_=S[:, 0:T]).then_inc(sst.h, 16)
                sst.inc(16)
            v = R.vector
            v.tensor_reduce(out=st[:, 10:11], in_=S[:, 0:L], axis=AX.X, op=ALU.max,
                            negate=True)
            vspace(R, b)
            v.tensor_copy(st[:, 11:12], st[:, 10:11])
            vneg = vpub(R, b, sev)
            R.scalar.wait_ge(sev.h, vneg)
            k = b.hist.get("Pfree")
            if k is not None:
                R.scalar.wait_ge(k[0].h, k[1])
            R.scalar.activation(out=P[:, 0:L], in_=S[:, 0:L], func=AF.Exp,
                                bias=st[:, 11:12], accum_out=st[:, 12:13])
            sspace(R, b, 4)
            R.scalar.activation(out=st[:, 13:14], in_=st[:, 12:13], func=AF.Copy, bias=0.0)
            sspace(R, b, 2)
            R.scalar.activation(out=st[:, 15:16], in_=st[:, 13:14], func=AF.Copy, bias=0.0)
            va = spub(R, b, sact)
            b.hist["Sfree2"] = (sact, va)
            v.wait_ge(sact.h, va)
            v.reciprocal(out=st[:, 14:15], in_=st[:, 15:16])
            vspace(R, b, 3)
            k = b.hist.get("P2free")
            if k is not None:
                v.wait_ge(k[0].h, k[1])
            v.tensor_scalar(out=P2v[:, 0:L], in0=P[:, 0:L], scalar1=st[:, 14:15],
                            scalar2=None, op0=ALU.mult).then_inc(sev.h, 1)
            vp2 = sev.inc(1)
            b.hist["Pfree"] = (sev, vp2)
            R.tensor.wait_ge(sev.h, vp2)
            nkb = tq + 1
            for kb in range(nkb):
                ptb = b.pt[kb % 2]
                k = b.hist.get(("pt", kb % 2))
                if k is not None:
                    R.tensor.wait_ge(k[0].h, k[1])
                R.tensor.transpose(ptb[:, 0:128], P2v[:, kb * 128:(kb + 1) * 128],
                                   b.ident_bf[:, :]).then_inc(strp.h, 1)
                vt = strp.inc(1)
                k = b.hist.get(("PTs", kb % 2))
                if k is not None:
                    R.vector.wait_ge(k[0].h, k[1])
                R.vector.wait_ge(strp.h, vt)
                R.vector.tensor_copy(PTb[:, (kb % 2) * 128:(kb % 2) * 128 + 128],
                                     ptb[:, 0:128]).then_inc(ste.h, 1)
                vte = ste.inc(1)
                b.hist[("pt", kb % 2)] = (ste, vte)
                R.tensor.wait_ge(ste.h, vte)
                if kb == 0:
                    k = b.hist.get("pbctx")
                    if k is not None:
                        R.tensor.wait_ge(k[0].h, k[1])
                R.tensor.matmul(b.pb[4][:, 0:128],
                                vtm[:, kb * 512 + kvh * 128:kb * 512 + (kvh + 1) * 128],
                                PTb[:, (kb % 2) * 128:(kb % 2) * 128 + 128],
                                start=(kb == 0), stop=(kb == nkb - 1)).then_inc(smm.h, 1)
                b.hist[("PTs", kb % 2)] = (smm, smm.inc(1))
            b.hist["P2free"] = (smm, smm.n)
            R.scalar.wait_ge(smm.h, smm.n)
            k = b.hist.get("ctxst")
            if k is not None:
                R.scalar.wait_ge(k[0].h, k[1])
            R.scalar.activation(out=ctxst[:, (tq % 4) * 128:(tq % 4) * 128 + 128],
                                in_=b.pb[4][:, 0:128], func=AF.Copy,
                                bias=0.0).then_inc(sact.h, 1)
            b.hist["pbctx"] = (sact, sact.inc(1))
            b.hist[("qtile", sl)] = (sact, sact.n)
            if True:
                R.sync.wait_ge(sact.h, sact.n)
                R.sync.dma_start(out=b.ctx[h][:, tq * 128:(tq + 1) * 128],
                                 in_=ctxst[:, (tq % 4) * 128:(tq % 4) * 128 + 128]) \
                    .then_inc(sst.h, 16)
                b.hist["ctxst"] = (sst, sst.inc(16))
        b.hist["amfree"] = (sev, sev.n)
    bgw = carve(0, 4096)
    crh = carve(4096, 2048)
    gbrow = carve(6144, 8192, F32)
    ydst = carve(14336, 2048)
    R.gpsimd.wait_ge(sst.h, sst.n)
    R.sync.wait_ge(sst.h, sst.n)
    for half in range(2):
        R.gpsimd.dma_start(out=bgw[:, half * 1024:(half + 1) * 1024]
                           .rearrange("d (h e) -> d h e", e=128),
                           in_=b.inp["L1_attn_gate_w"][half * 8:(half + 1) * 8]
                           .rearrange("h d e -> d h e"),
                           max_dma_last_dim=8192).then_inc(sw.h, 16)
        sw.inc(16)
    gw_done = sw.n
    R.sync.dma_start(out=gbrow[0:16, 0:128], in_=b.inp["L1_attn_gate_b"][:, :]) \
        .then_inc(sx.h, 16)
    vgb = sx.inc(16)
    R.tensor.wait_ge(sx.h, vgb)
    k = b.hist.get("pbT")
    if k is not None:
        R.tensor.wait_ge(k[0].h, k[1])
    R.tensor.transpose(b.pbT[:, 0:16], gbrow[0:16, 0:128], b.ident_f32[0:16, 0:16]) \
        .then_inc(strp.h, 1)
    vt = strp.inc(1)
    R.vector.wait_ge(strp.h, vt)
    R.vector.tensor_copy(st2[:, 32:48], b.pbT[:, 0:16])
    vgb2 = vpub(R, b, sev)
    R.scalar.wait_ge(sev.h, vgb2)
    R.tensor.wait_ge(sw.h, gw_done)
    for h in range(16):
        for tg in range(NTG):
            sl = (h * NTG + tg) % 2
            k = b.hist.get(("crh", sl))
            if k is not None:
                R.gpsimd.wait_ge(k[0].h, k[1])
            R.gpsimd.dma_start(out=crh[:, sl * 512:(sl + 1) * 512],
                               in_=b.ctx[h][:, tg * 512:(tg + 1) * 512]).then_inc(sq.h, 16)
            vc = sq.inc(16)
            pbo = b.pb[sl]
            k = b.hist.get(("pbg", sl))
            if k is not None:
                R.tensor.wait_ge(k[0].h, k[1])
            R.tensor.wait_ge(sq.h, vc)
            R.tensor.matmul(pbo[:, :], bgw[:, h * 128:(h + 1) * 128],
                            crh[:, sl * 512:(sl + 1) * 512],
                            start=True, stop=True).then_inc(smm.h, 1)
            vm = smm.inc(1)
            R.scalar.wait_ge(smm.h, vm)
            k = b.hist.get(("T12", sl))
            if k is not None:
                R.scalar.wait_ge(k[0].h, k[1])
            R.scalar.activation(out=T12[:, sl * 512:(sl + 1) * 512], in_=pbo[:, :],
                                func=AF.Sigmoid, bias=st2[:, 32 + h:33 + h]) \
                .then_inc(sact.h, 1)
            va = sact.inc(1)
            b.hist[("pbg", sl)] = (sact, va)
            k = b.hist.get(("ydst", sl))
            if k is not None:
                R.vector.wait_ge(k[0].h, k[1])
            R.vector.wait_ge(sact.h, va)
            R.vector.tensor_tensor(out=ydst[:, sl * 512:(sl + 1) * 512],
                                   in0=T12[:, sl * 512:(sl + 1) * 512],
                                   in1=crh[:, sl * 512:(sl + 1) * 512], op=ALU.mult) \
                .then_inc(sev.h, 1)
            vy = sev.inc(1)
            b.hist[("T12", sl)] = (sev, vy)
            b.hist[("crh", sl)] = (sev, vy)
            R.sync.wait_ge(sev.h, vy)
            R.sync.dma_start(out=b.yp[h][:, tg * 512:(tg + 1) * 512],
                             in_=ydst[:, sl * 512:(sl + 1) * 512]).then_inc(sst.h, 16)
            b.hist[("ydst", sl)] = (sst, sst.inc(16))
    wslab = carve(0, 65536)
    yrhs = carve(65536, 16384)
    outfm = carve(81920, 16384)
    borow = carve(98304, 8192, F32)[0:1, :]
    R.gpsimd.wait_ge(sst.h, sst.n)
    R.sync.wait_ge(sst.h, sst.n)
    R.sync.dma_start(out=borow[0:1, 0:2048], in_=b.inp["L1_attn_b_o"].unsqueeze(0)) \
        .then_inc(sx.h, 16)
    vb = sx.inc(16)
    R.tensor.wait_ge(sx.h, vb)
    k = b.hist.get("pbT")
    if k is not None:
        R.tensor.wait_ge(k[0].h, k[1])
    for q4 in range(4):
        ins = None
        for kk in range(4):
            i = q4 * 4 + kk
            ins = R.tensor.transpose(b.pbT[:, i:i + 1],
                                     borow[0:1, i * 128:(i + 1) * 128],
                                     b.ones1f[0:1, 0:1])
    ins.then_inc(strp.h, 1)
    vt = strp.inc(1)
    R.vector.wait_ge(strp.h, vt)
    R.vector.tensor_copy(b.stats2[:, 16:32], b.pbT[:, 0:16])
    bocol_ready = vpub(R, b, sev)
    for oc in range(ICH):
        emit_strip_load(R, b, b.inp["L1_attn_w_o"], wslab, oc, sw)
    wload = sw.n
    for tg in range(NTG):
        k = b.hist.get("crhs")
        if k is not None:
            R.sync.wait_ge(k[0].h, k[1])
        for i in range(ICH):
            R.sync.dma_start(out=yrhs[:, i * 512:(i + 1) * 512],
                             in_=b.yp[i][:, tg * 512:(tg + 1) * 512]).then_inc(sx.h, 16)
        vls = sx.inc(16 * ICH)
        R.tensor.wait_ge(sx.h, vls)
        if tg == 0:
            R.tensor.wait_ge(sw.h, wload)
            R.scalar.wait_ge(sev.h, bocol_ready)
        for oc in range(ICH):
            pbo = b.pb[oc % 2]
            k = b.hist.get(("pbqkv", oc % 2))
            if k is not None:
                R.tensor.wait_ge(k[0].h, k[1])
            ins = None
            for i in range(ICH):
                ins = R.tensor.matmul(pbo[:, :],
                                      wslab[:, oc * 2048 + i * 128:oc * 2048 + (i + 1) * 128],
                                      yrhs[:, i * 512:(i + 1) * 512],
                                      start=(i == 0), stop=(i == ICH - 1))
            ins.then_inc(smm.h, 1)
            vm = smm.inc(1)
            R.scalar.wait_ge(smm.h, vm)
            if oc == 0:
                k = b.hist.get("outfm")
                if k is not None:
                    R.scalar.wait_ge(k[0].h, k[1])
            R.scalar.activation(out=outfm[:, oc * 512:(oc + 1) * 512], in_=pbo[:, :],
                                func=AF.Identity, bias=b.stats2[:, 16 + oc:17 + oc]) \
                .then_inc(sact.h, 1)
            b.hist[("pbqkv", oc % 2)] = (sact, sact.inc(1))
        outfm_done = sact.n
        b.hist["crhs"] = (smm, smm.n)
        R.tensor.wait_ge(sact.h, outfm_done)
        for tt in range(4):
            row0 = tg * 512 + tt * 128
            xs = b.xt[:, (tt % 2) * D:(tt % 2) * D + D]
            k = b.hist.get(("ln_x", tt % 2))
            if k is not None:
                R.sync.wait_ge(k[0].h, k[1])
            R.sync.dma_start(out=xs, in_=src[row0:row0 + 128, :]).then_inc(sx.h, 16)
            vx = sx.inc(16)
            R.vector.wait_ge(sx.h, vx)
            res = b.res[:, (tt % 2) * D:(tt % 2) * D + D]
            k = b.hist.get(("res", tt % 2))
            if k is not None:
                R.vector.wait_ge(k[0].h, k[1])
            for r in range(4):
                ptb = b.pt[r % 2]
                k = b.hist.get(("pt", r % 2))
                if k is not None:
                    R.tensor.wait_ge(k[0].h, k[1])
                ins = None
                for kk in range(4):
                    oc = r * 4 + kk
                    ins = R.tensor.transpose(
                        ptb[:, kk * 128:(kk + 1) * 128],
                        outfm[:, oc * 512 + tt * 128:oc * 512 + (tt + 1) * 128],
                        b.ident_bf[:, :])
                ins.then_inc(strp.h, 1)
                vt = strp.inc(1)
                R.vector.wait_ge(strp.h, vt)
                R.vector.scalar_tensor_tensor(
                    out=res[:, r * 512:(r + 1) * 512], in0=ptb[:, 0:512],
                    scalar=1.0, in1=xs[:, r * 512:(r + 1) * 512],
                    op0=ALU.mult, op1=ALU.add).then_inc(sev.h, 1)
                b.hist[("pt", r % 2)] = (sev, sev.inc(1))
            b.hist[("ln_x", tt % 2)] = (sev, sev.n)
            R.sync.wait_ge(sev.h, sev.n)
            R.sync.dma_start(out=dst[row0:row0 + 128, :], in_=res).then_inc(sst.h, 16)
            b.hist[("res", tt % 2)] = (sst, sst.inc(16))
        b.hist["outfm"] = (b.sem["str"], b.sem["str"].n)


# version 03 (final)
# speedup vs baseline: 1.1530x; 1.1530x over previous
import numpy as np
import ml_dtypes
from contextlib import ExitStack

import concourse.bass as bass
import concourse.mybir as mybir
from concourse.bass_utils import run_bass_kernel_spmd

F32 = mybir.dt.float32
BF16 = mybir.dt.bfloat16
ALU = mybir.AluOpType
AF = mybir.ActivationFunctionType
AX = mybir.AxisListType

T = 4096
D = 2048
NTT = T // 128
NTG = T // 512
ICH = D // 128
BIG = -1e30
MAGIC = 12582912.0
LN_EPS = 1e-5
RMS_EPS_F32 = float(np.finfo(np.float32).eps)
L1_EPS = 1e-6

ENG = ("sync", "gpsimd", "tensor", "vector", "scalar")


class _Instr:
    __slots__ = ("incs",)

    def __init__(self):
        self.incs = []

    def then_inc(self, sem, n=1):
        self.incs.append((sem, n))
        return self


class _EngRec:
    def __init__(self, rec, eng):
        self._rec = rec
        self._eng = eng

    def __getattr__(self, name):
        rec, eng = self._rec, self._eng

        def call(*a, **k):
            inst = _Instr()
            rec.q[eng].append((name, a, k, inst))
            return inst

        return call


class Rec:
    def __init__(self):
        self.q = {e: [] for e in ENG}
        self.sync = _EngRec(self, "sync")
        self.gpsimd = _EngRec(self, "gpsimd")
        self.tensor = _EngRec(self, "tensor")
        self.vector = _EngRec(self, "vector")
        self.scalar = _EngRec(self, "scalar")

    def replay(self, nc):
        with nc.Block() as block:
            for e in ENG:
                ops = self.q[e]
                if not ops:
                    continue

                def body(eng, ops=ops):
                    for name, a, k, inst in ops:
                        r = getattr(eng, name)(*a, **k)
                        for (s, n) in inst.incs:
                            r = r.then_inc(s, n)

                getattr(block, e)(body)
        self.q = {e: [] for e in ENG}


class Sem:
    __slots__ = ("h", "n")

    def __init__(self, h):
        self.h = h
        self.n = 0

    def inc(self, k=1):
        self.n += k
        return self.n


def host_tables():
    t = {}
    t["tab_ident_bf"] = np.eye(128, dtype=ml_dtypes.bfloat16)
    t["tab_ident_f32"] = np.eye(128, dtype=np.float32)
    tri = np.where(np.triu(np.ones((128, 128), np.float32), 1) > 0, np.float32(BIG), np.float32(0))
    t["tab_tri"] = tri
    db = np.ones((128, 128), np.float32)
    np.fill_diagonal(db, np.float32(0))
    t["tab_diagboost"] = db
    inv = 1.0 / (10000.0 ** (np.arange(0, 128, 2, dtype=np.float64) / 128.0))
    j = np.arange(64)
    ang = np.arange(T, dtype=np.float64)[:, None] * inv[(2 * j) % 64][None, :]
    t["tab_cos"] = np.cos(ang).astype(np.float32)
    t["tab_sin"] = np.sin(ang).astype(np.float32)
    slopes = np.array([2.0 ** (-0.5 * (i + 1)) for i in range(16)], np.float64)
    t["tab_alibi"] = (slopes[:, None] * np.arange(T, dtype=np.float64)[None, :]).astype(np.float32)
    t["tab_kpos"] = np.arange(T, dtype=np.float32)[None, :]
    return t


TABLE_SPECS = {
    "tab_ident_bf": ([128, 128], BF16),
    "tab_ident_f32": ([128, 128], F32),
    "tab_tri": ([128, 128], F32),
    "tab_diagboost": ([128, 128], F32),
    "tab_cos": ([T, 64], F32),
    "tab_sin": ([T, 64], F32),
    "tab_alibi": ([16, T], F32),
    "tab_kpos": ([1, T], F32),
}

INPUT_SPECS = {
    "L0_attn_Wq": ([D, D], F32), "L0_attn_Wk": ([D, D], F32),
    "L0_attn_Wv": ([D, D], F32), "L0_attn_Wo": ([D, D], F32),
    "L0_attn_bo": ([D], F32),
    "L0_ffn_w_gate_up": ([16384, D], F32), "L0_ffn_w_proj": ([D, 8192], F32),
    "L1_attn_w_q": ([D, D], F32), "L1_attn_b_q": ([D], F32),
    "L1_attn_w_down": ([D, 512], F32), "L1_attn_b_down": ([512], F32),
    "L1_attn_w_up": ([512, 1024], F32), "L1_attn_b_up": ([1024], F32),
    "L1_attn_w_o": ([D, D], F32), "L1_attn_b_o": ([D], F32),
    "L1_attn_qn_w": ([128], F32), "L1_attn_kn_w": ([128], F32),
    "L1_attn_gate_w": ([16, 128, 128], F32), "L1_attn_gate_b": ([16, 128], F32),
    "L1_ffn_gate_weight": ([8, D], F32),
    "L1_ffn_gate_w": ([8, 1408, D], F32), "L1_ffn_up_w": ([8, 1408, D], F32),
    "L1_ffn_down_w": ([8, D, 1408], F32),
    "L1_ffn_sh_gate_w": ([2816, D], F32), "L1_ffn_sh_up_w": ([2816, D], F32),
    "L1_ffn_sh_down_w": ([D, 2816], F32),
}


class B:
    pass


def make_context(es, debug=False):
    nc = bass.Bass("TRN2", target_bir_lowering=False)
    b = B()
    b.nc = nc
    b.x = nc.dram_tensor("x", [T, D], F32, kind="ExternalInput").ap()
    b.inp = {}
    for name, (shape, dt) in INPUT_SPECS.items():
        b.inp[name] = nc.dram_tensor(name, shape, dt, kind="ExternalInput").ap()
    for name, (shape, dt) in TABLE_SPECS.items():
        b.inp[name] = nc.dram_tensor(name, shape, dt, kind="ExternalInput").ap()
    b.out = nc.dram_tensor("out", [T, D], F32, kind="ExternalOutput").ap()
    kd = dict(kind="ExternalOutput") if debug else {}
    b.resid = nc.dram_tensor("resid", [T, D], F32, **kd).ap()
    b.lnscr = nc.dram_tensor("lnscr", [ICH, 128, T], BF16, **kd).ap()
    b.qfm = nc.dram_tensor("qfm", [ICH, 128, T], BF16, **kd).ap()
    b.kfm = nc.dram_tensor("kfm", [ICH, 128, T], BF16, **kd).ap()
    b.vtm = nc.dram_tensor("vtm", [16, T // 128, 128, 128], BF16, **kd).ap()
    b.ctx = nc.dram_tensor("ctxs", [16, 128, T], BF16, **kd).ap()
    b.yp = nc.dram_tensor("yps", [16, 128, T], BF16, **kd).ap()
    b.dbg_wmat = nc.dram_tensor("dbg_wmat", [NTG, 128, 32], F32, **kd).ap() if debug else None
    b.dbg_a = nc.dram_tensor("dbg_a", [128, 56320], BF16, **kd).ap() if debug else None
    b.dbg_ln = nc.dram_tensor("dbg_ln", [128, ICH * 512], BF16, **kd).ap() if debug else None
    b.dbg_kfm = nc.dram_tensor("dbg_kfm", [128, 4 * T], BF16, **kd).ap() if debug else None
    b.dbg_vtm = nc.dram_tensor("dbg_vtm", [128, T // 128 * 512], BF16, **kd).ap() if debug else None
    b.dbg_S = nc.dram_tensor("dbg_S", [4, 128, T], F32, **kd).ap() if debug else None
    b.dbg_wbc = nc.dram_tensor("dbg_wbc", [NTG, 128, 8 * 512], BF16, **kd).ap() if debug else None
    b.debug = debug
    b.c_wgu = nc.dram_tensor("c_wgu", [128, 128, ICH * 128], BF16).ap()
    b.c_wp = nc.dram_tensor("c_wp", [16, 128, 64 * 128], BF16).ap()
    b.c_g = nc.dram_tensor("c_g", [8, 11, 128, ICH * 128], BF16).ap()
    b.c_u = nc.dram_tensor("c_u", [8, 11, 128, ICH * 128], BF16).ap()
    b.c_d = nc.dram_tensor("c_d", [8, 16, 128, 11 * 128], BF16).ap()
    b.c_shg = nc.dram_tensor("c_shg", [22, 128, ICH * 128], BF16).ap()
    b.c_shu = nc.dram_tensor("c_shu", [22, 128, ICH * 128], BF16).ap()
    b.c_shd = nc.dram_tensor("c_shd", [16, 128, 22 * 128], BF16).ap()

    sb = lambda name, shape, dt: es.enter_context(nc.sbuf_tensor(name, shape, dt))
    ps = lambda name, shape, dt: es.enter_context(nc.psum_tensor(name, shape, dt))
    b.ident_bf = sb("ident_bf", [128, 128], BF16)
    b.ident_f32 = sb("ident_f32", [128, 128], F32)
    b.tri = sb("tri", [128, 128], F32)
    b.diagboost = sb("diagboost", [128, 128], F32)
    b.ones1 = sb("ones1", [1, 128], BF16)
    b.ones1f = sb("ones1f", [1, 128], F32)
    b.onescol = sb("onescol", [128, 1], BF16)
    b.stats = sb("stats", [128, 64], F32)
    b.stats2 = sb("stats2", [128, 64], F32)
    b.xt = sb("xt", [128, 2 * D], F32)
    b.lnt = sb("lnt", [128, 2 * D], BF16)
    b.lnfm = sb("lnfm", [128, ICH * 512], BF16)
    b.dump = sb("dumpb", [128, D], BF16)
    b.res = sb("res", [128, 2 * D], F32)
    b.istd = sb("istdb", [128, 8], F32)
    b.qs32 = sb("qs32", [128, 128], F32)
    b.qsb = sb("qsb", [128, 128], BF16)
    b.qr = sb("qr", [128, 128], F32)
    b.big = sb("big", [128, 73472], BF16)

    def carve(off_bytes, nbytes, dt=BF16):
        a = off_bytes // 2
        v = b.big[:, a:a + nbytes // 2]
        return v if dt == BF16 else v.bitcast(dt)
    b.carve = carve
    b.h = carve(0, 65536)
    b.qa = b.h[:, 0:2048]
    b.qb = b.h[:, 2048:4096]
    b.wnat = b.h[:, 4096:8192]
    b.wT = carve(65536, 32768)
    b.wT2 = carve(98304, 16384)
    b.outfm = carve(114688, 16384)
    b.silu = carve(131072, 4096, F32)
    b.hsq = carve(135168, 2048)
    b.row1 = carve(137216, 8192, F32)[0:1, :]
    b.pb = [ps(f"pb{i}", [128, 512], F32) for i in range(5)]
    b.pbT = ps("pbT", [128, 512], F32)
    b.pt = [ps(f"pt{i}", [128, 1024], BF16) for i in range(2)]
    b.sem = {}
    for name in ["sx", "sln", "slnT", "sw", "sw2", "str", "ste", "smm", "sev",
                 "sact", "sst", "sq", "sk", "ssel", "sbis", "sctx", "smisc",
                 "sr1", "sr2", "sr3", "sr4", "sr5"]:
        b.sem[name] = Sem(es.enter_context(nc.semaphore(name)))
    b.hist = {}
    return b


def emit_init(R, b):
    s = b.sem["smisc"]
    g = R.gpsimd
    g.memset(b.ones1[:, :], 1.0)
    g.memset(b.ones1f[:, :], 1.0)
    g.memset(b.onescol[:, :], 1.0).then_inc(s.h, 1)
    s.inc(1)
    sy = R.sync
    v = b.sem["sx"]
    sy.dma_start(out=b.ident_bf[:, :], in_=b.inp["tab_ident_bf"][:, :]).then_inc(v.h, 16)
    sy.dma_start(out=b.ident_f32[:, :], in_=b.inp["tab_ident_f32"][:, :]).then_inc(v.h, 16)
    sy.dma_start(out=b.tri[:, :], in_=b.inp["tab_tri"][:, :]).then_inc(v.h, 16)
    sy.dma_start(out=b.diagboost[:, :], in_=b.inp["tab_diagboost"][:, :]).then_inc(v.h, 16)
    v.inc(64)
    b.init_done = (b.sem["smisc"].n, b.sem["sx"].n)


def wait_init(R, b, engs=ENG):
    mis, sx = b.init_done
    for e in engs:
        getattr(R, e).wait_ge(b.sem["smisc"].h, mis)
        getattr(R, e).wait_ge(b.sem["sx"].h, sx)


FLOW_SEMS = ["sx", "sw", "sw2", "str", "ste", "smm", "sev", "sact", "sst",
             "sln", "slnT", "sq", "sk", "ssel", "sbis", "sctx", "smisc"]


def emit_barrier(R, b):
    for e in ENG:
        eng = getattr(R, e)
        for name in FLOW_SEMS:
            s = b.sem[name]
            if s.n:
                eng.wait_ge(s.h, s.n)
    keep = {k: v for k, v in b.hist.items() if k in ("l0pp_done", "l1pp_done")}
    b.hist = keep


def emit_ln_group(R, b, src, tg, buf, war=None):
    sx, sln, slnT = b.sem["sx"], b.sem["slnT"], None
    sln = b.sem["sln"]
    slnT = b.sem["slnT"]
    sact = b.sem["sact"]
    strp = b.sem["str"]
    sev = b.sem["sev"]
    if war is not None:
        R.vector.wait_ge(war[0].h, war[1])
    for tt in range(4):
        B0 = (tt % 2) * 10
        st = b.stats
        row0 = tg * 512 + tt * 128
        xv = b.xt[:, (tt % 2) * D:(tt % 2 + 1) * D]
        lv = b.lnt[:, (tt % 2) * D:(tt % 2 + 1) * D]
        k = b.hist.get(("ln_x", tt % 2))
        if k is not None:
            R.sync.wait_ge(k[0].h, k[1])
        R.sync.dma_start(out=xv, in_=src[row0:row0 + 128, :]).then_inc(sx.h, 16)
        vx = sx.inc(16)
        R.vector.wait_ge(sx.h, vx)
        R.scalar.wait_ge(sx.h, vx)
        R.vector.wait_ge(sx.h, vx)
        for c4 in range(4):
            R.vector.bn_stats(out=b.stats2[:, 32 + 6 * c4:38 + 6 * c4],
                              in_=xv[:, c4 * 512:(c4 + 1) * 512])
        vspace(R, b)
        R.vector.bn_aggr(out=st[:, B0 + 0:B0 + 2], in_=b.stats2[:, 32:56])
        vspace(R, b)
        R.vector.tensor_scalar(out=st[:, B0 + 2:B0 + 3], in0=st[:, B0 + 0:B0 + 1],
                               scalar1=-1.0, scalar2=None, op0=ALU.mult)
        R.vector.tensor_scalar(out=st[:, B0 + 3:B0 + 4], in0=st[:, B0 + 1:B0 + 2],
                               scalar1=1.0, scalar2=LN_EPS, op0=ALU.mult, op1=ALU.add)
        vspace(R, b)
        R.vector.reciprocal(out=st[:, B0 + 4:B0 + 5], in_=st[:, B0 + 3:B0 + 4])
        vr = vpub(R, b, sev)
        R.scalar.wait_ge(sev.h, vr)
        R.scalar.activation(out=st[:, B0 + 5:B0 + 6], in_=st[:, B0 + 4:B0 + 5], func=AF.Sqrt)
        va2 = spub(R, b, sact)
        k = b.hist.get(("ln_t", tt % 2))
        if k is not None:
            R.vector.wait_ge(k[0].h, k[1])
        R.vector.wait_ge(sact.h, va2)
        R.vector.tensor_scalar(out=lv, in0=xv, scalar1=st[:, B0 + 2:B0 + 3],
                               scalar2=st[:, B0 + 5:B0 + 6], op0=ALU.add, op1=ALU.mult).then_inc(sln.h, 1)
        b.hist[("ln_x", tt % 2)] = (sln, sln.inc(1))
        R.tensor.wait_ge(sln.h, b.hist[("ln_x", tt % 2)][1])
        for g4 in range(4):
            ptb = b.pt[g4 % 2]
            k = b.hist.get(("pt", g4 % 2))
            if k is not None:
                R.tensor.wait_ge(k[0].h, k[1])
            ins = None
            for jj in range(4):
                i = g4 * 4 + jj
                ins = R.tensor.transpose(ptb[:, jj * 128:(jj + 1) * 128],
                                         lv[:, i * 128:(i + 1) * 128],
                                         b.ident_bf[:, :])
            ins.then_inc(strp.h, 1)
            vt = strp.inc(1)
            R.vector.wait_ge(strp.h, vt)
            dst = b.lnfm[:, buf * ICH * 512:(buf + 1) * ICH * 512] \
                .rearrange("p (i t) -> p i t", i=ICH)[:, g4 * 4:(g4 + 1) * 4, tt * 128:(tt + 1) * 128]
            R.vector.tensor_copy(dst, ptb[:, 0:512].rearrange("p (i t) -> p i t", i=4)) \
                .then_inc(slnT.h, 1)
            b.hist[("pt", g4 % 2)] = (slnT, slnT.inc(1))
        b.hist[("ln_t", tt % 2)] = (strp, strp.n)
    return slnT.n


def lnfm_tile(b, buf, i, lo, width):
    base = buf * ICH * 512 + i * 512 + lo
    return b.lnfm[:, base:base + width]


def vspace(R, b, n=2):
    for _ in range(n):
        R.vector.tensor_copy(b.stats[:, 40:41], b.stats[:, 41:42])


def vpub(R, b, sem):
    vspace(R, b, 4)
    R.vector.tensor_copy(b.stats[:, 42:43], b.stats[:, 43:44]).then_inc(sem.h, 1)
    return sem.inc(1)


def spub(R, b, sem):
    sspace(R, b, 4)
    R.scalar.activation(out=b.stats[:, 48:49], in_=b.ident_f32[:, 0:1],
                        func=AF.Copy, bias=0.0).then_inc(sem.h, 1)
    return sem.inc(1)


def sspace(R, b, n=1):
    for _ in range(n):
        R.scalar.activation(out=b.stats[:, 44:45], in_=b.ident_f32[:, 0:1],
                            func=AF.Copy, bias=0.0)


def emit_quant_chunk(R, b, ngroups):
    w = ngroups * 64
    v = R.vector
    wv = b.wnat[:, 0:w].rearrange("p (g e) -> p g e", e=64)
    v.tensor_reduce(out=b.qs32[:, 0:ngroups], in_=wv, axis=AX.X, op=ALU.add,
                    apply_absolute_value=True)
    vspace(R, b)
    v.tensor_scalar(out=b.qsb[:, 0:ngroups], in0=b.qs32[:, 0:ngroups],
                    scalar1=1.0 / 64, scalar2=1e-8, op0=ALU.mult, op1=ALU.max)
    vspace(R, b)
    v.reciprocal(out=b.qr[:, 0:ngroups], in_=b.qsb[:, 0:ngroups])
    vspace(R, b)
    rb = b.qr[:, 0:ngroups].unsqueeze(2).broadcast_to([128, ngroups, 64])
    v.tensor_tensor(out=b.qa[:, 0:w].rearrange("p (g e) -> p g e", e=64), in0=wv, in1=rb,
                    op=ALU.mult)
    v.tensor_scalar(out=b.qb[:, 0:w], in0=b.qa[:, 0:w], scalar1=MAGIC, scalar2=-MAGIC,
                    op0=ALU.add, op1=ALU.add)
    v.tensor_scalar(out=b.qa[:, 0:w], in0=b.qb[:, 0:w], scalar1=1.0, scalar2=-1.0,
                    op0=ALU.min, op1=ALU.max)
    sb2 = b.qsb[:, 0:ngroups].unsqueeze(2).broadcast_to([128, ngroups, 64])
    v.tensor_tensor(out=b.qb[:, 0:w].rearrange("p (g e) -> p g e", e=64),
                    in0=b.qa[:, 0:w].rearrange("p (g e) -> p g e", e=64), in1=sb2,
                    op=ALU.mult)


def emit_transpose16(R, b, src_bf, dst, n, war_key):
    strp, ste = b.sem["str"], b.sem["ste"]
    ng = (n + 3) // 4
    for g4 in range(ng):
        ptb = b.pt[g4 % 2]
        k = b.hist.get(("pt", g4 % 2))
        if k is not None:
            R.tensor.wait_ge(k[0].h, k[1])
        cnt = min(4, n - g4 * 4)
        ins = None
        for jj in range(cnt):
            i = g4 * 4 + jj
            ins = R.tensor.transpose(ptb[:, jj * 128:(jj + 1) * 128],
                                     src_bf[:, i * 128:(i + 1) * 128],
                                     b.ident_bf[:, :])
        ins.then_inc(strp.h, 1)
        vt = strp.inc(1)
        R.vector.wait_ge(strp.h, vt)
        R.vector.tensor_copy(dst[:, g4 * 512:g4 * 512 + cnt * 128], ptb[:, 0:cnt * 128]) \
            .then_inc(ste.h, 1)
        b.hist[("pt", g4 % 2)] = (ste, ste.inc(1))
    return ste.n


def emit_l0ffn_prepass(R, b):
    wgu = b.inp["L0_ffn_w_gate_up"]
    wp = b.inp["L0_ffn_w_proj"]
    sw, sev, sst = b.sem["sw"], b.sem["sev"], b.sem["sst"]
    smm = b.sem["smm"]
    for fc in range(128):
        k = b.hist.get("qwnat")
        if k is not None:
            R.gpsimd.wait_ge(k[0].h, k[1])
        R.gpsimd.dma_start(out=b.wnat[:, 0:D], in_=wgu[fc * 128:(fc + 1) * 128, :],
                           max_dma_last_dim=8192).then_inc(sw.h, 16)
        vw = sw.inc(16)
        R.vector.wait_ge(sw.h, vw)
        k = b.hist.get("qb_free")
        if k is not None:
            R.vector.wait_ge(k[0].h, k[1])
        emit_quant_chunk(R, b, 32)
        R.vector.tensor_copy(b.stats[:, 45:46], b.stats[:, 46:47]).then_inc(sev.h, 1)
        b.hist["qwnat"] = (sev, sev.inc(1))
        R.tensor.wait_ge(sev.h, b.hist["qwnat"][1])
        slot = b.wT[:, (fc % 2) * 2048:(fc % 2) * 2048 + 2048]
        k = b.hist.get(("wTfree", fc % 2))
        if k is not None:
            R.vector.wait_ge(k[0].h, k[1])
        evdone = emit_transpose16(R, b, b.qb, slot, 16, "pp")
        b.hist["qb_free"] = (b.sem["str"], b.sem["str"].n)
        R.sync.wait_ge(b.sem["ste"].h, evdone)
        R.sync.dma_start(out=b.c_wgu[fc], in_=slot).then_inc(sst.h, 16)
        b.hist[("wTfree", fc % 2)] = (sst, sst.inc(16))
    R.vector.wait_ge(sst.h, sst.n)
    for oc in range(16):
        for pc in range(4):
            k = b.hist.get("qwnat")
            if k is not None:
                R.gpsimd.wait_ge(k[0].h, k[1])
            R.gpsimd.dma_start(out=b.wnat[:, 0:D],
                               in_=wp[oc * 128:(oc + 1) * 128, pc * 2048:(pc + 1) * 2048],
                               max_dma_last_dim=8192).then_inc(sw.h, 16)
            vw = sw.inc(16)
            R.vector.wait_ge(sw.h, vw)
            k = b.hist.get("qb_free")
            if k is not None:
                R.vector.wait_ge(k[0].h, k[1])
            emit_quant_chunk(R, b, 32)
            R.vector.tensor_copy(b.stats[:, 45:46], b.stats[:, 46:47]).then_inc(sev.h, 1)
            b.hist["qwnat"] = (sev, sev.inc(1))
            R.tensor.wait_ge(sev.h, b.hist["qwnat"][1])
            slab = b.wT[:, (oc % 2) * 8192:(oc % 2) * 8192 + 8192]
            if pc == 0:
                k = b.hist.get(("wpfree", oc % 2))
                if k is not None:
                    R.vector.wait_ge(k[0].h, k[1])
            evdone = emit_transpose16(R, b, b.qb, slab[:, pc * 2048:(pc + 1) * 2048], 16, "pp")
            b.hist["qb_free"] = (b.sem["str"], b.sem["str"].n)
        R.sync.wait_ge(b.sem["ste"].h, evdone)
        R.sync.dma_start(out=b.c_wp[oc], in_=slab).then_inc(sst.h, 16)
        b.hist[("wpfree", oc % 2)] = (sst, sst.inc(16))
    b.hist["l0pp_done"] = (sst, sst.n)


def emit_l0ffn(R, b, src, dst, first_call):
    emit_barrier(R, b)
    if first_call:
        emit_l0ffn_prepass(R, b)
    sw, sw2 = b.sem["sw"], b.sem["sw2"]
    smm, sev, sact = b.sem["smm"], b.sem["sev"], b.sem["sact"]
    sst, sx = b.sem["sst"], b.sem["sx"]
    ppd = b.hist.get("l0pp_done")
    if ppd is not None:
        R.gpsimd.wait_ge(ppd[0].h, ppd[1])
    for tg in range(NTG):
        buf = 0
        war = b.hist.get(("lnfm", buf))
        emit_ln_group(R, b, src, tg, buf, war=war)
        ln_ready = b.sem["slnT"].n
        R.tensor.wait_ge(b.sem["slnT"].h, ln_ready)
        for j in range(64):
            gslot = b.wT2[:, (j % 2) * 2048:(j % 2) * 2048 + 2048]
            uslot = b.wT2[:, 4096 + (j % 2) * 2048:4096 + (j % 2) * 2048 + 2048]
            k = b.hist.get(("gw", j % 2))
            if k is not None:
                R.gpsimd.wait_ge(k[0].h, k[1])
            R.gpsimd.dma_start(out=gslot, in_=b.c_wgu[j]).then_inc(sw.h, 16)
            R.gpsimd.dma_start(out=uslot, in_=b.c_wgu[64 + j]).then_inc(sw.h, 16)
            vw = sw.inc(32)
            pbg = b.pb[(j % 2) * 2]
            pbu = b.pb[(j % 2) * 2 + 1]
            k = b.hist.get(("pbgu", j % 2))
            if k is not None:
                R.tensor.wait_ge(k[0].h, k[1])
            R.tensor.wait_ge(sw.h, vw)
            insg = insu = None
            for i in range(ICH):
                insg = R.tensor.matmul(pbg[:, :], gslot[:, i * 128:(i + 1) * 128],
                                       lnfm_tile(b, buf, i, 0, 512),
                                       start=(i == 0), stop=(i == ICH - 1))
            insg.then_inc(smm.h, 1)
            vg = smm.inc(1)
            for i in range(ICH):
                insu = R.tensor.matmul(pbu[:, :], uslot[:, i * 128:(i + 1) * 128],
                                       lnfm_tile(b, buf, i, 0, 512),
                                       start=(i == 0), stop=(i == ICH - 1))
            insu.then_inc(smm.h, 1)
            vu = smm.inc(1)
            b.hist[("gw", j % 2)] = (smm, vu)
            sislot = b.silu[:, (j % 2) * 512:(j % 2) * 512 + 512]
            R.scalar.wait_ge(smm.h, vg)
            k = b.hist.get(("silu", j % 2))
            if k is not None:
                R.scalar.wait_ge(k[0].h, k[1])
            R.scalar.activation(out=sislot, in_=pbg[:, :], func=AF.Silu).then_inc(sact.h, 1)
            va = sact.inc(1)
            R.vector.wait_ge(sact.h, va)
            R.vector.wait_ge(smm.h, vu)
            hj = b.h[:, j * 512:(j + 1) * 512]
            R.vector.tensor_tensor(out=hj, in0=sislot, in1=pbu[:, :], op=ALU.mult) \
                .then_inc(sev.h, 1)
            hval = sev.inc(1)
            b.hist[("pbgu", j % 2)] = (sev, hval)
            b.hist[("silu", j % 2)] = (sev, hval)
            hsq = b.hsq[:, (j % 2) * 512:(j % 2) * 512 + 512]
            k = b.hist.get(("hsq", j % 2))
            if k is not None:
                R.vector.wait_ge(k[0].h, k[1])
            R.vector.tensor_tensor(out=hsq, in0=hj, in1=hj, op=ALU.mult).then_inc(sev.h, 1)
            vq = sev.inc(1)
            R.tensor.wait_ge(sev.h, vq)
            if j == 0:
                k = b.hist.get("pb4")
                if k is not None:
                    R.tensor.wait_ge(k[0].h, k[1])
            insq = R.tensor.matmul(b.pb[4][0:1, :], b.onescol[:, :], hsq,
                                   start=(j == 0), stop=(j == 63))
            insq.then_inc(smm.h, 1)
            b.hist[("hsq", j % 2)] = (smm, smm.inc(1))
        sumsq_done = smm.n
        v = R.vector
        v.wait_ge(smm.h, sumsq_done)
        v.tensor_scalar(out=b.row1[0:1, 0:512], in0=b.pb[4][0:1, :], scalar1=1.0 / 8192,
                        scalar2=RMS_EPS_F32, op0=ALU.mult, op1=ALU.add).then_inc(sev.h, 1)
        b.hist["pb4"] = (sev, sev.inc(1))
        vspace(R, b)
        v.reciprocal(out=b.row1[0:1, 512:1024], in_=b.row1[0:1, 0:512])
        vr = vpub(R, b, sev)
        R.scalar.wait_ge(sev.h, vr)
        R.scalar.activation(out=b.row1[0:1, 1024:1536], in_=b.row1[0:1, 512:1024],
                            func=AF.Sqrt)
        va = spub(R, b, sact)
        v.wait_ge(sact.h, va)
        v.tensor_scalar(out=b.row1[0:1, 1536:2048], in0=b.row1[0:1, 1024:1536],
                        scalar1=0.5, scalar2=None, op0=ALU.mult)
        vh = vpub(R, b, sev)
        R.tensor.wait_ge(sev.h, vh)
        k = b.hist.get("pbT")
        if k is not None:
            R.tensor.wait_ge(k[0].h, k[1])
        ins = None
        for tt in range(4):
            ins = R.tensor.transpose(b.pbT[:, tt:tt + 1],
                                     b.row1[0:1, 1536 + tt * 128:1536 + (tt + 1) * 128],
                                     b.ones1f[0:1, 0:1])
        ins.then_inc(b.sem["str"].h, 1)
        vt = b.sem["str"].inc(1)
        v.wait_ge(b.sem["str"].h, vt)
        v.tensor_copy(b.istd[:, 0:4], b.pbT[:, 0:4])
        vspace(R, b, 2)
        v.tensor_copy(b.stats[:, 42:43], b.stats[:, 43:44]).then_inc(sev.h, 1)
        b.hist["pbT"] = (sev, sev.inc(1))
        for oc in range(16):
            slab = b.wT[:, (oc % 2) * 8192:(oc % 2) * 8192 + 8192]
            k = b.hist.get(("wp", oc % 2))
            if k is not None:
                R.gpsimd.wait_ge(k[0].h, k[1])
            R.gpsimd.dma_start(out=slab, in_=b.c_wp[oc]).then_inc(sw2.h, 16)
            vw = sw2.inc(16)
            pbo = b.pb[oc % 2]
            k = b.hist.get(("pbdown", oc % 2))
            if k is not None:
                R.tensor.wait_ge(k[0].h, k[1])
            R.tensor.wait_ge(sw2.h, vw)
            ins = None
            for f in range(64):
                ins = R.tensor.matmul(pbo[:, :], slab[:, f * 128:(f + 1) * 128],
                                      b.h[:, f * 512:(f + 1) * 512],
                                      start=(f == 0), stop=(f == 63))
            ins.then_inc(smm.h, 1)
            vm = smm.inc(1)
            b.hist[("wp", oc % 2)] = (smm, vm)
            R.scalar.wait_ge(smm.h, vm)
            if oc == 0:
                k = b.hist.get("outfm")
                if k is not None:
                    R.scalar.wait_ge(k[0].h, k[1])
            R.scalar.activation(out=b.outfm[:, oc * 512:(oc + 1) * 512], in_=pbo[:, :],
                                func=AF.Copy, bias=0.0).then_inc(sact.h, 1)
            b.hist[("pbdown", oc % 2)] = (sact, sact.inc(1))
        outfm_done = sact.n
        R.tensor.wait_ge(sact.h, outfm_done)
        for tt in range(4):
            row0 = tg * 512 + tt * 128
            xs = b.xt[:, (tt % 2) * D:(tt % 2) * D + D]
            k = b.hist.get(("ln_x", tt % 2))
            if k is not None:
                R.sync.wait_ge(k[0].h, k[1])
            R.sync.dma_start(out=xs, in_=src[row0:row0 + 128, :]).then_inc(sx.h, 16)
            vx = sx.inc(16)
            R.vector.wait_ge(sx.h, vx)
            res = b.res[:, (tt % 2) * D:(tt % 2) * D + D]
            k = b.hist.get(("res", tt % 2))
            if k is not None:
                R.vector.wait_ge(k[0].h, k[1])
            for r in range(4):
                ptb = b.pt[r % 2]
                k = b.hist.get(("pt", r % 2))
                if k is not None:
                    R.tensor.wait_ge(k[0].h, k[1])
                ins = None
                for kk in range(4):
                    oc = r * 4 + kk
                    ins = R.tensor.transpose(
                        ptb[:, kk * 128:(kk + 1) * 128],
                        b.outfm[:, oc * 512 + tt * 128:oc * 512 + (tt + 1) * 128],
                        b.ident_bf[:, :])
                ins.then_inc(b.sem["str"].h, 1)
                vt = b.sem["str"].inc(1)
                R.vector.wait_ge(b.sem["str"].h, vt)
                R.vector.scalar_tensor_tensor(
                    out=res[:, r * 512:(r + 1) * 512], in0=ptb[:, 0:512],
                    scalar=b.istd[:, tt:tt + 1], in1=xs[:, r * 512:(r + 1) * 512],
                    op0=ALU.mult, op1=ALU.add).then_inc(sev.h, 1)
                b.hist[("pt", r % 2)] = (sev, sev.inc(1))
            b.hist[("ln_x", tt % 2)] = (sev, sev.n)
            R.sync.wait_ge(sev.h, sev.n)
            R.sync.dma_start(out=dst[row0:row0 + 128, :], in_=res).then_inc(sst.h, 16)
            b.hist[("res", tt % 2)] = (sst, sst.inc(16))
        b.hist[("lnfm", buf)] = (smm, smm.n)
        b.hist["outfm"] = (b.sem["str"], b.sem["str"].n)


def emit_final_sync(R, b):
    for name in ["sst", "sx"]:
        R.sync.wait_ge(b.sem[name].h, b.sem[name].n)


def build_program(parts, debug=False):
    es = ExitStack()
    with es:
        b = make_context(es, debug=debug)
        R = Rec()
        emit_init(R, b)
        wait_init(R, b)
        ffn_calls = {"l0ffn": 0, "l1ffn": 0}
        for name, src, dst in parts:
            s = {"x": b.x, "resid": b.resid, "out": b.out}[src]
            d = {"x": b.x, "resid": b.resid, "out": b.out}[dst]
            if name == "l0ffn":
                emit_l0ffn(R, b, s, d, first_call=(ffn_calls["l0ffn"] == 0))
                ffn_calls["l0ffn"] += 1
            elif name == "l1ffn":
                emit_l1ffn(R, b, s, d, first_call=(ffn_calls["l1ffn"] == 0))
                ffn_calls["l1ffn"] += 1
            elif name == "l0attn":
                emit_l0attn(R, b, s, d)
            elif name == "l1attn":
                emit_l1attn(R, b, s, d)
            else:
                raise ValueError(name)
        emit_final_sync(R, b)
        R.replay(b.nc)
    return b.nc


FULL_CHAIN = [
    ("l0ffn", "x", "resid"),
    ("l0attn", "resid", "resid"),
    ("l0ffn", "resid", "resid"),
    ("l1ffn", "resid", "resid"),
    ("l1attn", "resid", "resid"),
    ("l1ffn", "resid", "out"),
]


def kernel(**inputs):
    tabs = host_tables()
    nc = build_program(FULL_CHAIN)
    in_maps = []
    for c in range(8):
        m = {"x": np.ascontiguousarray(inputs["L0_attn_x"][c])}
        for name in INPUT_SPECS:
            m[name] = np.asarray(inputs[name])
        m.update(tabs)
        in_maps.append(m)
    res = run_bass_kernel_spmd(nc, in_maps, core_ids=list(range(8)))
    return np.stack([r["out"] for r in res.results], axis=0)


HSCALE = float(1.0 / np.sqrt(np.float32(128.0)))


def emit_strip_load(R, b, w_ap, slab, oc, sem):
    for q4 in range(4):
        src = w_ap[:, oc * 128:(oc + 1) * 128] \
            .rearrange("(i p) c -> p i c", p=128)[:, q4 * 4:(q4 + 1) * 4, :]
        dst = slab[:, oc * 2048 + q4 * 512:oc * 2048 + (q4 + 1) * 512] \
            .rearrange("p (i c) -> p i c", c=128)
        R.gpsimd.dma_start(out=dst, in_=src, max_dma_last_dim=8192).then_inc(sem.h, 16)
        sem.inc(16)
    return sem.n


def emit_l0attn(R, b, src, dst):
    emit_barrier(R, b)
    sx, sw, sw2 = b.sem["sx"], b.sem["sw"], b.sem["sw2"]
    smm, sev, sact = b.sem["smm"], b.sem["sev"], b.sem["sact"]
    sst, sq = b.sem["sst"], b.sem["sq"]
    strp, ste = b.sem["str"], b.sem["ste"]
    sln, slnT = b.sem["sln"], b.sem["slnT"]
    carve = b.carve
    wslab = carve(0, 65536)
    stage = carve(65536, 8192)
    for tg in range(NTG):
        war = b.hist.get(("lnfm", 0))
        emit_ln_group(R, b, src, tg, 0, war=war)
        ln_ready = b.sem["slnT"].n
        R.sync.wait_ge(slnT.h, ln_ready)
        for i in range(ICH):
            R.sync.dma_start(out=b.lnscr[i][:, tg * 512:(tg + 1) * 512],
                             in_=b.lnfm[:, i * 512:(i + 1) * 512]).then_inc(sst.h, 16)
            sst.inc(16)
        b.hist[("lnfm", 0)] = (sst, sst.n)
    for wi, (wname, dstscr, is_v) in enumerate([
            ("L0_attn_Wq", b.qfm, False),
            ("L0_attn_Wk", b.kfm, False),
            ("L0_attn_Wv", None, True)]):
        k = b.hist.get("wslab")
        if k is not None:
            R.gpsimd.wait_ge(k[0].h, k[1])
        for oc in range(ICH):
            emit_strip_load(R, b, b.inp[wname], wslab, oc, sw)
        wload = sw.n
        for tg in range(NTG):
            k = b.hist.get(("lnfm", 0))
            if k is not None:
                R.sync.wait_ge(k[0].h, k[1])
            for i in range(ICH):
                R.sync.dma_start(out=b.lnfm[:, i * 512:(i + 1) * 512],
                                 in_=b.lnscr[i][:, tg * 512:(tg + 1) * 512]).then_inc(sx.h, 16)
            vls = sx.inc(16 * ICH)
            R.tensor.wait_ge(sx.h, vls)
            if tg == 0:
                R.tensor.wait_ge(sw.h, wload)
            for oc in range(ICH):
                pbo = b.pb[oc % 2]
                k = b.hist.get(("pbqkv", oc % 2))
                if k is not None:
                    R.tensor.wait_ge(k[0].h, k[1])
                ins = None
                for i in range(ICH):
                    ins = R.tensor.matmul(pbo[:, :],
                                          wslab[:, oc * 2048 + i * 128:oc * 2048 + (i + 1) * 128],
                                          b.lnfm[:, i * 512:(i + 1) * 512],
                                          start=(i == 0), stop=(i == ICH - 1))
                ins.then_inc(smm.h, 1)
                vm = smm.inc(1)
                if not is_v:
                    sslot = stage[:, (oc % 4) * 1024:(oc % 4) * 1024 + 512]
                    k = b.hist.get(("qkst", oc % 4))
                    if k is not None:
                        R.scalar.wait_ge(k[0].h, k[1])
                    R.scalar.wait_ge(smm.h, vm)
                    R.scalar.activation(out=sslot, in_=pbo[:, :], func=AF.Copy,
                                        bias=0.0).then_inc(sact.h, 1)
                    va = sact.inc(1)
                    b.hist[("pbqkv", oc % 2)] = (sact, va)
                    R.sync.wait_ge(sact.h, va)
                    R.sync.dma_start(out=dstscr[oc][:, tg * 512:(tg + 1) * 512],
                                     in_=sslot).then_inc(sst.h, 16)
                    b.hist[("qkst", oc % 4)] = (sst, sst.inc(16))
                else:
                    sslot = stage[:, (oc % 2) * 2048:(oc % 2) * 2048 + 512]
                    tslot = stage[:, (oc % 2) * 2048 + 1024:(oc % 2) * 2048 + 1536]
                    k = b.hist.get(("vst", oc % 2))
                    if k is not None:
                        R.scalar.wait_ge(k[0].h, k[1])
                    R.scalar.wait_ge(smm.h, vm)
                    R.scalar.activation(out=sslot, in_=pbo[:, :], func=AF.Copy,
                                        bias=0.0).then_inc(sact.h, 1)
                    va = sact.inc(1)
                    b.hist[("pbqkv", oc % 2)] = (sact, va)
                    R.tensor.wait_ge(sact.h, va)
                    ptb = b.pt[oc % 2]
                    k = b.hist.get(("pt", oc % 2))
                    if k is not None:
                        R.tensor.wait_ge(k[0].h, k[1])
                    ins = None
                    for kk in range(4):
                        ins = R.tensor.transpose(ptb[:, kk * 128:(kk + 1) * 128],
                                                 sslot[:, kk * 128:(kk + 1) * 128],
                                                 b.ident_bf[:, :])
                    ins.then_inc(strp.h, 1)
                    vt = strp.inc(1)
                    k = b.hist.get(("vts", oc % 2))
                    if k is not None:
                        R.vector.wait_ge(k[0].h, k[1])
                    R.vector.wait_ge(strp.h, vt)
                    R.vector.tensor_copy(tslot, ptb[:, 0:512]).then_inc(ste.h, 1)
                    vte = ste.inc(1)
                    b.hist[("pt", oc % 2)] = (ste, vte)
                    R.sync.wait_ge(ste.h, vte)
                    vdst = b.vtm[oc][tg * 4:(tg + 1) * 4].rearrange("a q f -> q a f")
                    R.sync.dma_start(out=vdst,
                                     in_=tslot.rearrange("p (a f) -> p a f", f=128)) \
                        .then_inc(sst.h, 16)
                    vs = sst.inc(16)
                    b.hist[("vst", oc % 2)] = (sst, vs)
                    b.hist[("vts", oc % 2)] = (sst, vs)
            b.hist[("lnfm", 0)] = (smm, smm.n)
        b.hist["wslab"] = (smm, smm.n)
    S = carve(0, 16384, F32)
    P = carve(16384, 8192)
    P2 = carve(24576, 8192)
    qh = carve(32768, 8192)
    kh = carve(40960, 8192)
    vh = carve(49152, 8192)
    PT = carve(57344, 2048)
    ctxst = carve(59392, 1024)
    R.sync.wait_ge(sst.h, sst.n)
    R.gpsimd.wait_ge(sst.h, sst.n)
    for p_ in range(2):
        if b.hist.get(("pbqkv", p_)) is not None:
            b.hist[("pbS", p_)] = b.hist[("pbqkv", p_)]
    for h in range(16):
        k = b.hist.get("headbufs")
        if k is not None:
            R.gpsimd.wait_ge(k[0].h, k[1])
        R.gpsimd.dma_start(out=qh, in_=b.qfm[h]).then_inc(sq.h, 16)
        R.gpsimd.dma_start(out=kh, in_=b.kfm[h]).then_inc(sq.h, 16)
        R.gpsimd.dma_start(out=vh.rearrange("p (a f) -> p a f", f=128),
                           in_=b.vtm[h].rearrange("a q f -> q a f")).then_inc(sq.h, 16)
        vq = sq.inc(48)
        R.tensor.wait_ge(sq.h, vq)
        for tq in range(NTT):
            L = (tq + 1) * 128
            nch = (L + 511) // 512
            ncols = 0
            for c in range(nch):
                pbo = b.pb[c % 2]
                k = b.hist.get(("pbS", c % 2))
                if k is not None:
                    R.tensor.wait_ge(k[0].h, k[1])
                R.tensor.matmul(pbo[:, :], qh[:, tq * 128:(tq + 1) * 128],
                                kh[:, c * 512:(c + 1) * 512],
                                start=True, stop=True).then_inc(smm.h, 1)
                vm = smm.inc(1)
                if c == 0:
                    k = b.hist.get("Sfree")
                    if k is not None:
                        R.vector.wait_ge(k[0].h, k[1])
                R.vector.wait_ge(smm.h, vm)
                ndiag = min(512, L - c * 512) - 128 if (c == nch - 1) else 512
                if c == nch - 1:
                    if ndiag > 0:
                        R.vector.tensor_scalar(out=S[:, c * 512:c * 512 + ndiag],
                                               in0=pbo[:, 0:ndiag], scalar1=HSCALE,
                                               scalar2=None, op0=ALU.mult, op1=ALU.max,
                                               accum_out=b.stats[:, 16 + ncols:17 + ncols])
                        ncols += 1
                    R.vector.scalar_tensor_tensor(
                        out=S[:, c * 512 + ndiag:c * 512 + ndiag + 128],
                        in0=pbo[:, ndiag:ndiag + 128], scalar=HSCALE, in1=b.tri[:, :],
                        op0=ALU.mult, op1=ALU.add).then_inc(sev.h, 1)
                else:
                    R.vector.tensor_scalar(out=S[:, c * 512:(c + 1) * 512],
                                           in0=pbo[:, :], scalar1=HSCALE,
                                           scalar2=None, op0=ALU.mult, op1=ALU.max,
                                           accum_out=b.stats[:, 16 + ncols:17 + ncols]) \
                        .then_inc(sev.h, 1)
                    ncols += 1
                b.hist[("pbS", c % 2)] = (sev, sev.inc(1))
            v = R.vector
            v.tensor_reduce(out=b.stats[:, 16 + ncols:17 + ncols], in_=S[:, L - 128:L],
                            axis=AX.X, op=ALU.max)
            ncols += 1
            vspace(R, b)
            v.tensor_reduce(out=b.stats[:, 10:11], in_=b.stats[:, 16:16 + ncols],
                            axis=AX.X, op=ALU.max, negate=True)
            vspace(R, b)
            v.tensor_copy(b.stats[:, 11:12], b.stats[:, 10:11])
            vneg = vpub(R, b, sev)
            R.scalar.wait_ge(sev.h, vneg)
            k = b.hist.get("Pfree")
            if k is not None:
                R.scalar.wait_ge(k[0].h, k[1])
            R.scalar.activation(out=P[:, 0:L], in_=S[:, 0:L], func=AF.Exp,
                                bias=b.stats[:, 11:12], accum_out=b.stats[:, 12:13])
            sspace(R, b, 4)
            R.scalar.activation(out=b.stats[:, 13:14], in_=b.stats[:, 12:13],
                                func=AF.Copy, bias=0.0)
            sspace(R, b, 2)
            R.scalar.activation(out=b.stats[:, 15:16], in_=b.stats[:, 13:14],
                                func=AF.Copy, bias=0.0)
            va = spub(R, b, sact)
            b.hist["Sfree"] = (sact, va)
            v.wait_ge(sact.h, va)
            v.reciprocal(out=b.stats[:, 14:15], in_=b.stats[:, 15:16])
            vspace(R, b, 3)
            k = b.hist.get("P2free")
            if k is not None:
                v.wait_ge(k[0].h, k[1])
            v.tensor_scalar(out=P2[:, 0:L], in0=P[:, 0:L], scalar1=b.stats[:, 14:15],
                            scalar2=None, op0=ALU.mult).then_inc(sev.h, 1)
            vp2 = sev.inc(1)
            b.hist["Pfree"] = (sev, vp2)
            R.tensor.wait_ge(sev.h, vp2)
            nkb = tq + 1
            ng4 = (nkb + 3) // 4
            for g4 in range(ng4):
                cnt = min(4, nkb - g4 * 4)
                ptb = b.pt[g4 % 2]
                k = b.hist.get(("pt", g4 % 2))
                if k is not None:
                    R.tensor.wait_ge(k[0].h, k[1])
                ins = None
                for jj in range(cnt):
                    kb = g4 * 4 + jj
                    ins = R.tensor.transpose(ptb[:, jj * 128:(jj + 1) * 128],
                                             P2[:, kb * 128:(kb + 1) * 128],
                                             b.ident_bf[:, :])
                ins.then_inc(strp.h, 1)
                vt = strp.inc(1)
                k = b.hist.get(("PTs", g4 % 2))
                if k is not None:
                    R.vector.wait_ge(k[0].h, k[1])
                R.vector.wait_ge(strp.h, vt)
                R.vector.tensor_copy(PT[:, (g4 % 2) * 512:(g4 % 2) * 512 + cnt * 128],
                                     ptb[:, 0:cnt * 128]).then_inc(ste.h, 1)
                vte = ste.inc(1)
                b.hist[("pt", g4 % 2)] = (ste, vte)
                R.tensor.wait_ge(ste.h, vte)
                if g4 == 0:
                    k = b.hist.get("pbctx")
                    if k is not None:
                        R.tensor.wait_ge(k[0].h, k[1])
                ins = None
                for jj in range(cnt):
                    kb = g4 * 4 + jj
                    ins = R.tensor.matmul(b.pb[2][:, 0:128], vh[:, kb * 128:(kb + 1) * 128],
                                          PT[:, (g4 % 2) * 512 + jj * 128:(g4 % 2) * 512 + (jj + 1) * 128],
                                          start=(kb == 0), stop=(kb == nkb - 1))
                ins.then_inc(smm.h, 1)
                b.hist[("PTs", g4 % 2)] = (smm, smm.inc(1))
            b.hist["P2free"] = (smm, smm.n)
            R.scalar.wait_ge(smm.h, smm.n)
            k = b.hist.get("ctxst")
            if k is not None:
                R.scalar.wait_ge(k[0].h, k[1])
            R.scalar.activation(out=ctxst[:, (tq % 4) * 128:(tq % 4) * 128 + 128],
                                in_=b.pb[2][:, 0:128], func=AF.Copy,
                                bias=0.0).then_inc(sact.h, 1)
            b.hist["pbctx"] = (sact, sact.inc(1))
            if tq % 4 == 3:
                R.sync.wait_ge(sact.h, sact.n)
                R.sync.dma_start(out=b.ctx[h][:, (tq - 3) * 128:(tq + 1) * 128],
                                 in_=ctxst).then_inc(sst.h, 16)
                b.hist["ctxst"] = (sst, sst.inc(16))
        b.hist["headbufs"] = (smm, smm.n)
    crhs = carve(65536, 16384)
    outfm = carve(81920, 16384)
    borow = carve(98304, 8192, F32)[0:1, :]
    R.sync.wait_ge(sst.h, sst.n)
    R.gpsimd.wait_ge(sst.h, sst.n)
    R.sync.dma_start(out=borow[0:1, 0:2048], in_=b.inp["L0_attn_bo"].unsqueeze(0)) \
        .then_inc(sx.h, 16)
    vb = sx.inc(16)
    R.tensor.wait_ge(sx.h, vb)
    k = b.hist.get("pbT")
    if k is not None:
        R.tensor.wait_ge(k[0].h, k[1])
    for q4 in range(4):
        ins = None
        for kk in range(4):
            i = q4 * 4 + kk
            ins = R.tensor.transpose(b.pbT[:, i:i + 1],
                                     borow[0:1, i * 128:(i + 1) * 128],
                                     b.ones1f[0:1, 0:1])
    ins.then_inc(strp.h, 1)
    vt = strp.inc(1)
    R.vector.wait_ge(strp.h, vt)
    R.vector.tensor_copy(b.stats2[:, 16:32], b.pbT[:, 0:16]).then_inc(sev.h, 1)
    b.hist["pbT"] = (sev, sev.inc(1))
    bocol_ready = sev.n
    for oc in range(ICH):
        emit_strip_load(R, b, b.inp["L0_attn_Wo"], wslab, oc, sw)
    wload = sw.n
    for tg in range(NTG):
        k = b.hist.get("crhs")
        if k is not None:
            R.sync.wait_ge(k[0].h, k[1])
        for i in range(ICH):
            R.sync.dma_start(out=crhs[:, i * 512:(i + 1) * 512],
                             in_=b.ctx[i][:, tg * 512:(tg + 1) * 512]).then_inc(sx.h, 16)
        vls = sx.inc(16 * ICH)
        R.tensor.wait_ge(sx.h, vls)
        if tg == 0:
            R.tensor.wait_ge(sw.h, wload)
            R.scalar.wait_ge(sev.h, bocol_ready)
        for oc in range(ICH):
            pbo = b.pb[oc % 2]
            k = b.hist.get(("pbqkv", oc % 2))
            if k is not None:
                R.tensor.wait_ge(k[0].h, k[1])
            ins = None
            for i in range(ICH):
                ins = R.tensor.matmul(pbo[:, :],
                                      wslab[:, oc * 2048 + i * 128:oc * 2048 + (i + 1) * 128],
                                      crhs[:, i * 512:(i + 1) * 512],
                                      start=(i == 0), stop=(i == ICH - 1))
            ins.then_inc(smm.h, 1)
            vm = smm.inc(1)
            R.scalar.wait_ge(smm.h, vm)
            if oc == 0:
                k = b.hist.get("outfm")
                if k is not None:
                    R.scalar.wait_ge(k[0].h, k[1])
            R.scalar.activation(out=outfm[:, oc * 512:(oc + 1) * 512], in_=pbo[:, :],
                                func=AF.Identity, bias=b.stats2[:, 16 + oc:17 + oc]) \
                .then_inc(sact.h, 1)
            b.hist[("pbqkv", oc % 2)] = (sact, sact.inc(1))
        outfm_done = sact.n
        b.hist["crhs"] = (smm, smm.n)
        R.tensor.wait_ge(sact.h, outfm_done)
        for tt in range(4):
            row0 = tg * 512 + tt * 128
            xs = b.xt[:, (tt % 2) * D:(tt % 2) * D + D]
            k = b.hist.get(("ln_x", tt % 2))
            if k is not None:
                R.sync.wait_ge(k[0].h, k[1])
            R.sync.dma_start(out=xs, in_=src[row0:row0 + 128, :]).then_inc(sx.h, 16)
            vx = sx.inc(16)
            R.vector.wait_ge(sx.h, vx)
            res = b.res[:, (tt % 2) * D:(tt % 2) * D + D]
            k = b.hist.get(("res", tt % 2))
            if k is not None:
                R.vector.wait_ge(k[0].h, k[1])
            for r in range(4):
                ptb = b.pt[r % 2]
                k = b.hist.get(("pt", r % 2))
                if k is not None:
                    R.tensor.wait_ge(k[0].h, k[1])
                ins = None
                for kk in range(4):
                    oc = r * 4 + kk
                    ins = R.tensor.transpose(
                        ptb[:, kk * 128:(kk + 1) * 128],
                        outfm[:, oc * 512 + tt * 128:oc * 512 + (tt + 1) * 128],
                        b.ident_bf[:, :])
                ins.then_inc(strp.h, 1)
                vt = strp.inc(1)
                R.vector.wait_ge(strp.h, vt)
                R.vector.scalar_tensor_tensor(
                    out=res[:, r * 512:(r + 1) * 512], in0=ptb[:, 0:512],
                    scalar=1.0, in1=xs[:, r * 512:(r + 1) * 512],
                    op0=ALU.mult, op1=ALU.add).then_inc(sev.h, 1)
                b.hist[("pt", r % 2)] = (sev, sev.inc(1))
            b.hist[("ln_x", tt % 2)] = (sev, sev.n)
            R.sync.wait_ge(sev.h, sev.n)
            R.sync.dma_start(out=dst[row0:row0 + 128, :], in_=res).then_inc(sst.h, 16)
            b.hist[("res", tt % 2)] = (sst, sst.inc(16))
        b.hist["outfm"] = (b.sem["str"], b.sem["str"].n)


def emit_l1ffn_prepass(R, b):
    sw, sev, sst = b.sem["sw"], b.sem["sev"], b.sem["sst"]
    jobs = []
    gw, uw, dw = b.inp["L1_ffn_gate_w"], b.inp["L1_ffn_up_w"], b.inp["L1_ffn_down_w"]
    sg, su, sd = b.inp["L1_ffn_sh_gate_w"], b.inp["L1_ffn_sh_up_w"], b.inp["L1_ffn_sh_down_w"]
    for n in range(8):
        for j in range(11):
            jobs.append((gw[n][j * 128:(j + 1) * 128, :], 16, b.c_g[n][j]))
            jobs.append((uw[n][j * 128:(j + 1) * 128, :], 16, b.c_u[n][j]))
        for oc in range(16):
            jobs.append((dw[n][oc * 128:(oc + 1) * 128, :], 11, b.c_d[n][oc]))
    for j in range(22):
        jobs.append((sg[j * 128:(j + 1) * 128, :], 16, b.c_shg[j]))
        jobs.append((su[j * 128:(j + 1) * 128, :], 16, b.c_shu[j]))
    for oc in range(16):
        jobs.append((sd[oc * 128:(oc + 1) * 128, :], 22, b.c_shd[oc]))
    slabw = b.carve(110 * 1024, 2 * 2816 * 2)
    for idx, (srcap, ntile, cdst) in enumerate(jobs):
        w = ntile * 128
        k = b.hist.get("qwnat")
        if k is not None:
            R.gpsimd.wait_ge(k[0].h, k[1])
        R.gpsimd.dma_start(out=b.wnat[:, 0:w], in_=srcap,
                           max_dma_last_dim=8192).then_inc(sw.h, 16)
        vw = sw.inc(16)
        R.tensor.wait_ge(sw.h, vw)
        slot = slabw[:, (idx % 2) * 2816:(idx % 2) * 2816 + w]
        k = b.hist.get(("l1slab", idx % 2))
        if k is not None:
            R.vector.wait_ge(k[0].h, k[1])
        evdone = emit_transpose16(R, b, b.wnat[:, 0:w], slot, ntile, "pp")
        b.hist["qwnat"] = (b.sem["str"], b.sem["str"].n)
        R.sync.wait_ge(b.sem["ste"].h, evdone)
        R.sync.dma_start(out=cdst, in_=slot).then_inc(sst.h, 16)
        b.hist[("l1slab", idx % 2)] = (sst, sst.inc(16))
    b.hist["l1pp_done"] = (sst, sst.n)


def emit_l1ffn(R, b, src, dst, first_call):
    emit_barrier(R, b)
    if first_call:
        emit_l1ffn_prepass(R, b)
        emit_barrier(R, b)
    carve = b.carve
    sx, sw, sw2 = b.sem["sx"], b.sem["sw"], b.sem["sw2"]
    smm, sev, sact = b.sem["smm"], b.sem["sev"], b.sem["sact"]
    sst, strp, ste = b.sem["sst"], b.sem["str"], b.sem["ste"]
    st, st2 = b.stats, b.stats2
    a_of = lambda n, j: carve(n * 11264 + j * 1024, 1024)
    ash_of = lambda j: carve(90112 + j * 1024, 1024)
    slabg = carve(112640, 8192)
    slabu = carve(120832, 8192)
    silu2 = carve(129024, 4096, F32)
    wbc = carve(133120, 8192)
    slabd = carve(141312, 5632)
    stage4 = carve(112640, 4096)
    wrow = carve(0, 8192)[0:1, :]
    gwT = b.qsb
    R.gpsimd.dma_start(out=b.wnat[0:8, 0:2048], in_=b.inp["L1_ffn_gate_weight"][:, :],
                       max_dma_last_dim=8192).then_inc(sw.h, 16)
    vw = sw.inc(16)
    R.tensor.wait_ge(sw.h, vw)
    k = b.hist.get("pbT")
    if k is not None:
        R.tensor.wait_ge(k[0].h, k[1])
    ins = None
    for i in range(ICH):
        ins = R.tensor.transpose(b.pbT.bitcast(BF16)[:, i * 8:(i + 1) * 8],
                                 b.wnat[0:8, i * 128:(i + 1) * 128],
                                 b.ident_bf[0:8, 0:8])
    ins.then_inc(strp.h, 1)
    vt = strp.inc(1)
    R.vector.wait_ge(strp.h, vt)
    R.vector.tensor_copy(gwT[:, :], b.pbT.bitcast(BF16)[:, 0:128]).then_inc(sev.h, 1)
    b.hist["pbT"] = (sev, sev.inc(1))
    gwT_ready = sev.n
    R.tensor.wait_ge(sev.h, gwT_ready)
    for tg in range(NTG):
        war = b.hist.get(("lnfm", 0))
        emit_ln_group(R, b, src, tg, 0, war=war)
        ln_ready = b.sem["slnT"].n
        R.tensor.wait_ge(b.sem["slnT"].h, ln_ready)
        if b.debug and tg == 0:
            R.sync.wait_ge(b.sem["slnT"].h, ln_ready)
            R.sync.dma_start(out=b.dbg_ln, in_=b.lnfm[:, :]).then_inc(b.sem["sst"].h, 16)
            b.sem["sst"].inc(16)
        rs = carve(8192, 2048, F32)
        k = b.hist.get("wrow_free")
        if k is not None:
            R.vector.wait_ge(k[0].h, k[1])
        R.vector.wait_ge(b.sem["smm"].h, b.sem["smm"].n)
        for tt in range(4):
            if tt == 0:
                k = b.hist.get("pb4")
                if k is not None:
                    R.tensor.wait_ge(k[0].h, k[1])
            ins = None
            for i in range(ICH):
                ins = R.tensor.matmul(b.pb[4][:, tt * 8:(tt + 1) * 8],
                                      lnfm_tile(b, 0, i, tt * 128, 128),
                                      gwT[:, i * 8:(i + 1) * 8],
                                      start=(i == 0), stop=(i == ICH - 1))
            ins.then_inc(smm.h, 1)
            smm.inc(1)
        vm = smm.n
        v = R.vector
        v.wait_ge(smm.h, vm)
        c = lambda a, w=32: rs[:, a:a + w]
        L3 = lambda a, w=8: rs[:, a:a + 32].rearrange("p (t e) -> p t e", e=w)
        bc4 = lambda a, reps=8: rs[:, a:a + 4].unsqueeze(2).broadcast_to([128, 4, reps])
        v.tensor_copy(c(0), b.pb[4][:, 0:32]).then_inc(sev.h, 1)
        b.hist["pb4"] = (sev, sev.inc(1))
        vspace(R, b)
        v.tensor_reduce(out=c(400, 4), in_=L3(0), axis=AX.X, op=ALU.max)
        vspace(R, b)
        v.tensor_tensor(out=L3(32), in0=L3(0), in1=bc4(400), op=ALU.subtract)
        ve = vpub(R, b, sev)
        R.scalar.wait_ge(sev.h, ve)
        R.scalar.activation(out=c(64), in_=c(32), func=AF.Exp)
        va = spub(R, b, sact)
        v.wait_ge(sact.h, va)
        v.tensor_reduce(out=c(404, 4), in_=L3(64), axis=AX.X, op=ALU.add)
        vspace(R, b)
        v.reciprocal(out=c(408, 4), in_=c(404, 4))
        vspace(R, b)
        v.tensor_tensor(out=L3(96), in0=L3(64), in1=bc4(408), op=ALU.mult)
        vspace(R, b)
        v.tensor_tensor(out=c(128, 16).rearrange("p (t g) -> p t g", g=4),
                        in0=L3(96)[:, :, 0::2], in1=L3(96)[:, :, 1::2], op=ALU.max)
        vspace(R, b)
        v.tensor_reduce(out=c(412, 4), in_=c(128, 16).rearrange("p (t g) -> p t g", g=4),
                        axis=AX.X, op=ALU.max)
        vspace(R, b)
        v.tensor_tensor(out=c(144, 16).rearrange("p (t g) -> p t g", g=4),
                        in0=c(128, 16).rearrange("p (t g) -> p t g", g=4),
                        in1=bc4(412, 4), op=ALU.is_equal)
        vspace(R, b)
        v.scalar_tensor_tensor(out=c(160, 16), in0=c(144, 16), scalar=BIG,
                               in1=c(128, 16), op0=ALU.mult, op1=ALU.add)
        vspace(R, b)
        v.tensor_reduce(out=c(416, 4), in_=c(160, 16).rearrange("p (t g) -> p t g", g=4),
                        axis=AX.X, op=ALU.max)
        vspace(R, b)
        v.tensor_tensor(out=c(176, 16).rearrange("p (t g) -> p t g", g=4),
                        in0=c(160, 16).rearrange("p (t g) -> p t g", g=4),
                        in1=bc4(416, 4), op=ALU.is_equal)
        vspace(R, b)
        v.tensor_tensor(out=c(192, 16), in0=c(144, 16), in1=c(176, 16), op=ALU.add)
        vspace(R, b)
        gmx = c(192, 16).rearrange("p (t g) -> p t g", g=4).unsqueeze(3) \
            .broadcast_to([128, 4, 4, 2])
        v.tensor_tensor(out=c(208).rearrange("p (t g e) -> p t g e", g=4, e=2),
                        in0=L3(96).rearrange("p t (g e) -> p t g e", e=2),
                        in1=gmx, op=ALU.mult)
        vspace(R, b)
        v.tensor_reduce(out=c(420, 4), in_=L3(208), axis=AX.X, op=ALU.max)
        vspace(R, b)
        v.tensor_tensor(out=L3(240), in0=L3(208), in1=bc4(420), op=ALU.is_equal)
        vspace(R, b)
        v.scalar_tensor_tensor(out=c(272), in0=c(240), scalar=BIG, in1=c(208),
                               op0=ALU.mult, op1=ALU.add)
        vspace(R, b)
        v.tensor_reduce(out=c(424, 4), in_=L3(272), axis=AX.X, op=ALU.max)
        vspace(R, b)
        v.tensor_tensor(out=L3(304), in0=L3(272), in1=bc4(424), op=ALU.is_equal)
        vspace(R, b)
        v.tensor_tensor(out=c(336), in0=c(240), in1=c(304), op=ALU.add)
        vspace(R, b)
        v.tensor_tensor(out=c(368), in0=c(208), in1=c(336), op=ALU.mult)
        vwm = vpub(R, b, sev)
        R.tensor.wait_ge(sev.h, vwm)
        for tt in range(4):
            for half in range(2):
                k = b.hist.get("pbT")
                if k is not None:
                    R.tensor.wait_ge(k[0].h, k[1])
                ins = None
                for nn in range(4):
                    n = half * 4 + nn
                    col = 368 + tt * 8 + n
                    ins = R.tensor.transpose(b.pbT[0:1, nn * 128:(nn + 1) * 128],
                                             rs[:, col:col + 1], b.ident_f32[:, :])
                ins.then_inc(strp.h, 1)
                vt = strp.inc(1)
                v.wait_ge(strp.h, vt)
                dstw = wrow[0:1, tt * 1024 + half * 512:tt * 1024 + (half + 1) * 512]
                v.tensor_copy(dstw, b.pbT[0:1, 0:512]).then_inc(sev.h, 1)
                b.hist["pbT"] = (sev, sev.inc(1))
        vpub(R, b, sev)
        wrow_done = sev.n
        R.tensor.wait_ge(sev.h, wrow_done)
        for n in range(8):
            k = b.hist.get("pbwb")
            if k is not None:
                R.tensor.wait_ge(k[0].h, k[1])
            rhs = wrow[0:1, :].rearrange("q (a e cc) -> q a e cc", a=4, e=8)[:, :, n, :]
            R.tensor.matmul(b.pb[3][:, :].rearrange("p (a cc) -> p a cc", cc=128),
                            b.ones1[0:1, :], rhs,
                            start=True, stop=True).then_inc(smm.h, 1)
            vm = smm.inc(1)
            R.vector.wait_ge(smm.h, vm)
            R.vector.tensor_copy(wbc[:, n * 512:(n + 1) * 512], b.pb[3][:, :]) \
                .then_inc(sev.h, 1)
            b.hist["pbwb"] = (sev, sev.inc(1))
        wbc_done = sev.n
        b.hist["wrow_free"] = (sev, wbc_done)
        if b.debug:
            R.sync.wait_ge(sev.h, vwm)
            R.sync.dma_start(out=b.dbg_wmat[tg], in_=rs[:, 368:400]).then_inc(sst.h, 16)
            sst.inc(16)
            R.sync.wait_ge(sev.h, wbc_done)
            R.sync.dma_start(out=b.dbg_wbc[tg], in_=wbc[:, :]).then_inc(sst.h, 16)
            sst.inc(16)
            b.hist["wrow_free"] = (sst, sst.n)
        tmpa = b.lnt[:, 0:512]
        R.tensor.wait_ge(sev.h, wbc_done)
        R.gpsimd.wait_ge(strp.h, strp.n)
        R.vector.wait_ge(smm.h, smm.n)
        for n in range(9):
            nch = 11 if n < 8 else 22
            for j in range(nch):
                sl = (n * 11 + j) % 2 if n < 8 else (88 + j) % 2
                gslot = slabg[:, sl * 2048:sl * 2048 + 2048]
                uslot = slabu[:, sl * 2048:sl * 2048 + 2048]
                k = b.hist.get(("l1gw", sl))
                if k is not None:
                    R.gpsimd.wait_ge(k[0].h, k[1])
                gs = b.c_g[n][j] if n < 8 else b.c_shg[j]
                us = b.c_u[n][j] if n < 8 else b.c_shu[j]
                R.gpsimd.dma_start(out=gslot, in_=gs).then_inc(sw.h, 16)
                R.gpsimd.dma_start(out=uslot, in_=us).then_inc(sw.h, 16)
                vw = sw.inc(32)
                pbg = b.pb[sl * 2]
                pbu = b.pb[sl * 2 + 1]
                k = b.hist.get(("pbgu", sl))
                if k is not None:
                    R.tensor.wait_ge(k[0].h, k[1])
                R.tensor.wait_ge(sw.h, vw)
                insg = insu = None
                for i in range(ICH):
                    insg = R.tensor.matmul(pbg[:, :], gslot[:, i * 128:(i + 1) * 128],
                                           lnfm_tile(b, 0, i, 0, 512),
                                           start=(i == 0), stop=(i == ICH - 1))
                insg.then_inc(smm.h, 1)
                vg = smm.inc(1)
                for i in range(ICH):
                    insu = R.tensor.matmul(pbu[:, :], uslot[:, i * 128:(i + 1) * 128],
                                           lnfm_tile(b, 0, i, 0, 512),
                                           start=(i == 0), stop=(i == ICH - 1))
                insu.then_inc(smm.h, 1)
                vu = smm.inc(1)
                b.hist[("l1gw", sl)] = (smm, vu)
                sislot = silu2[:, sl * 512:sl * 512 + 512]
                R.scalar.wait_ge(smm.h, vg)
                k = b.hist.get(("l1si", sl))
                if k is not None:
                    R.scalar.wait_ge(k[0].h, k[1])
                R.scalar.activation(out=sislot, in_=pbg[:, :], func=AF.Silu) \
                    .then_inc(sact.h, 1)
                va = sact.inc(1)
                av = a_of(n, j) if n < 8 else ash_of(j)
                R.vector.wait_ge(sact.h, va)
                R.vector.wait_ge(smm.h, vu)
                if n < 8:
                    R.vector.tensor_tensor(out=tmpa, in0=pbu[:, :],
                                           in1=wbc[:, n * 512:(n + 1) * 512], op=ALU.mult)
                    R.vector.tensor_tensor(out=av, in0=tmpa, in1=sislot,
                                           op=ALU.mult).then_inc(sev.h, 1)
                else:
                    R.vector.tensor_tensor(out=av, in0=pbu[:, :], in1=sislot,
                                           op=ALU.mult).then_inc(sev.h, 1)
                hval = sev.inc(1)
                b.hist[("pbgu", sl)] = (sev, hval)
                b.hist[("l1si", sl)] = (sev, hval)
        a_done = sev.n
        R.gpsimd.wait_ge(sev.h, a_done)
        if b.debug and tg == 0:
            R.sync.wait_ge(sev.h, a_done)
            R.sync.dma_start(out=b.dbg_a, in_=carve(0, 112640)).then_inc(sst.h, 16)
            sst.inc(16)
        R.tensor.wait_ge(sev.h, a_done)
        for og in range(4):
            for kk in range(4):
                oc = og * 4 + kk
                pbo = b.pb[kk % 2]
                k = b.hist.get(("pbdown", kk % 2))
                if k is not None:
                    R.tensor.wait_ge(k[0].h, k[1])
                nmm = 0
                ins = None
                for n in range(9):
                    nch = 11 if n < 8 else 22
                    if n < 8:
                        dsl = n % 2
                        dslot = slabd[:, dsl * 1408:dsl * 1408 + 1408]
                        guards = [("l1dw", dsl)]
                    else:
                        dslot = slabd[:, 0:2816]
                        guards = [("l1dw", 0), ("l1dw", 1)]
                    for gk in guards:
                        k = b.hist.get(gk)
                        if k is not None:
                            R.gpsimd.wait_ge(k[0].h, k[1])
                    ds = b.c_d[n][oc] if n < 8 else b.c_shd[oc]
                    R.gpsimd.dma_start(out=dslot, in_=ds).then_inc(sw2.h, 16)
                    vw = sw2.inc(16)
                    R.tensor.wait_ge(sw2.h, vw)
                    for j in range(nch):
                        av = a_of(n, j) if n < 8 else ash_of(j)
                        ins = R.tensor.matmul(pbo[:, :], dslot[:, j * 128:(j + 1) * 128],
                                              av, start=(nmm == 0),
                                              stop=(n == 8 and j == nch - 1))
                        nmm += 1
                    ins.then_inc(smm.h, 1)
                    val = smm.inc(1)
                    for gk in guards:
                        b.hist[gk] = (smm, val)
                vm = smm.n
                R.scalar.wait_ge(smm.h, vm)
                k = b.hist.get(("st4", kk % 2))
                if k is not None:
                    R.scalar.wait_ge(k[0].h, k[1])
                R.scalar.activation(out=stage4[:, kk * 512:(kk + 1) * 512],
                                    in_=pbo[:, :], func=AF.Copy, bias=0.0) \
                    .then_inc(sact.h, 1)
                b.hist[("pbdown", kk % 2)] = (sact, sact.inc(1))
            st4_done = sact.n
            R.tensor.wait_ge(sact.h, st4_done)
            for tt in range(4):
                row0 = tg * 512 + tt * 128
                xs = b.xt[:, (tt % 2) * D + og * 512:(tt % 2) * D + (og + 1) * 512]
                k = b.hist.get(("ln_x", tt % 2))
                if k is not None:
                    R.sync.wait_ge(k[0].h, k[1])
                R.sync.dma_start(out=xs, in_=src[row0:row0 + 128, og * 512:(og + 1) * 512]) \
                    .then_inc(sx.h, 16)
                vx = sx.inc(16)
                R.vector.wait_ge(sx.h, vx)
                res = b.res[:, (tt % 2) * D + og * 512:(tt % 2) * D + (og + 1) * 512]
                k = b.hist.get(("res", tt % 2, og))
                if k is not None:
                    R.vector.wait_ge(k[0].h, k[1])
                ptb = b.pt[tt % 2]
                k = b.hist.get(("pt", tt % 2))
                if k is not None:
                    R.tensor.wait_ge(k[0].h, k[1])
                ins = None
                for kk in range(4):
                    ins = R.tensor.transpose(
                        ptb[:, kk * 128:(kk + 1) * 128],
                        stage4[:, kk * 512 + tt * 128:kk * 512 + (tt + 1) * 128],
                        b.ident_bf[:, :])
                ins.then_inc(strp.h, 1)
                vt = strp.inc(1)
                R.vector.wait_ge(strp.h, vt)
                R.vector.scalar_tensor_tensor(
                    out=res, in0=ptb[:, 0:512], scalar=0.5, in1=xs,
                    op0=ALU.mult, op1=ALU.add).then_inc(sev.h, 1)
                b.hist[("pt", tt % 2)] = (sev, sev.inc(1))
                b.hist[("ln_x", tt % 2)] = (sev, sev.n)
                R.sync.wait_ge(sev.h, sev.n)
                R.sync.dma_start(out=dst[row0:row0 + 128, og * 512:(og + 1) * 512],
                                 in_=res).then_inc(sst.h, 16)
                b.hist[("res", tt % 2, og)] = (sst, sst.inc(16))
            b.hist[("st4", 0)] = (b.sem["str"], b.sem["str"].n)
            b.hist[("st4", 1)] = (b.sem["str"], b.sem["str"].n)
        b.hist[("lnfm", 0)] = (smm, smm.n)


def emit_l1attn(R, b, src, dst):
    emit_barrier(R, b)
    carve = b.carve
    sx, sw, sw2 = b.sem["sx"], b.sem["sw"], b.sem["sw2"]
    smm, sev, sact = b.sem["smm"], b.sem["sev"], b.sem["sact"]
    sst, strp, ste = b.sem["sst"], b.sem["str"], b.sem["ste"]
    sq, sk = b.sem["sq"], b.sem["sk"]
    st, st2 = b.stats, b.stats2
    SSCALE = 1.0 / 8.0
    TSCALE = float(1.0 / (np.sqrt(np.float32(128.0)) * 30.0))
    wdown = carve(0, 16384)
    wup = carve(16384, 24576 - 16384)
    qnb0 = carve(24576, 512, F32)
    knb0 = carve(25088, 512, F32)
    bbq = carve(25600, 4096)[0:1, :]
    bbd = carve(29696, 1024)[0:1, :]
    bbu = carve(30720, 2048)[0:1, :]
    qrow = carve(32768, 512, F32)[0:1, :]
    for i in range(ICH):
        R.gpsimd.dma_start(out=wdown[:, i * 512:(i + 1) * 512],
                           in_=b.inp["L1_attn_w_down"][i * 128:(i + 1) * 128, :],
                           max_dma_last_dim=8192).then_inc(sw.h, 16)
        sw.inc(16)
    for l in range(4):
        R.gpsimd.dma_start(out=wup[:, l * 1024:(l + 1) * 1024],
                           in_=b.inp["L1_attn_w_up"][l * 128:(l + 1) * 128, :],
                           max_dma_last_dim=8192).then_inc(sw.h, 16)
        sw.inc(16)
    R.gpsimd.dma_start(out=bbq, in_=b.inp["L1_attn_b_q"].unsqueeze(0),
                       max_dma_last_dim=8192).then_inc(sw.h, 16)
    R.gpsimd.dma_start(out=bbd, in_=b.inp["L1_attn_b_down"].unsqueeze(0),
                       max_dma_last_dim=8192).then_inc(sw.h, 16)
    R.gpsimd.dma_start(out=bbu, in_=b.inp["L1_attn_b_up"].unsqueeze(0),
                       max_dma_last_dim=8192).then_inc(sw.h, 16)
    sw.inc(48)
    R.sync.dma_start(out=qrow, in_=b.inp["L1_attn_qn_w"].unsqueeze(0)).then_inc(sx.h, 16)
    R.sync.dma_start(out=carve(33280, 512, F32)[0:1, :],
                     in_=b.inp["L1_attn_kn_w"].unsqueeze(0)).then_inc(sx.h, 16)
    vb = sx.inc(32)
    R.tensor.wait_ge(sx.h, vb)
    R.tensor.matmul(b.pb[0][:, 0:128], b.ones1f[0:1, :], qrow, start=True, stop=True)
    R.tensor.matmul(b.pb[1][:, 0:128], b.ones1f[0:1, :], carve(33280, 512, F32)[0:1, :],
                    start=True, stop=True).then_inc(smm.h, 1)
    vm = smm.inc(1)
    R.vector.wait_ge(smm.h, vm)
    R.vector.tensor_copy(qnb0, b.pb[0][:, 0:128])
    R.vector.tensor_copy(knb0, b.pb[1][:, 0:128]).then_inc(sev.h, 1)
    qnb_done = sev.inc(1)
    qt4 = carve(65536, 32768, F32)
    qt2 = carve(98304, 8192, F32)
    tmp1 = carve(106496, 4096, F32)
    tmp2 = carve(110592, 4096, F32)
    qb16 = carve(114688, 4096)
    qstage = carve(118784, 4096)
    cs = carve(122880, 512, F32)
    wqp = carve(123392, 2048)
    latst = carve(125440, 2048)
    latfm = carve(32768, 32768)
    R.tensor.wait_ge(sev.h, qnb_done)
    for tg in range(NTG):
        war = b.hist.get(("lnfm", 0))
        emit_ln_group(R, b, src, tg, 0, war=war)
        ln_ready = b.sem["slnT"].n
        R.tensor.wait_ge(b.sem["slnT"].h, ln_ready)
        for og in range(4):
            for i in range(ICH):
                sl = i % 2
                k = b.hist.get(("wqp", sl))
                if k is not None:
                    R.gpsimd.wait_ge(k[0].h, k[1])
                R.gpsimd.dma_start(
                    out=wqp[:, sl * 512:(sl + 1) * 512],
                    in_=b.inp["L1_attn_w_q"][i * 128:(i + 1) * 128,
                                             og * 512:(og + 1) * 512],
                    max_dma_last_dim=8192).then_inc(sw2.h, 16)
                vw = sw2.inc(16)
                R.tensor.wait_ge(sw2.h, vw)
                ins = None
                for tt in range(4):
                    if i == 0:
                        k = b.hist.get(("pbq", tt))
                        if k is not None:
                            R.tensor.wait_ge(k[0].h, k[1])
                    ins = R.tensor.matmul(b.pb[tt][:, :],
                                          lnfm_tile(b, 0, i, tt * 128, 128),
                                          wqp[:, sl * 512:(sl + 1) * 512],
                                          start=(i == 0), stop=False)
                ins.then_inc(smm.h, 1)
                b.hist[("wqp", sl)] = (smm, smm.inc(1))
            for tt in range(4):
                ins = R.tensor.matmul(b.pb[tt][:, :], b.ones1[0:1, :],
                                      bbq[0:1, og * 512:(og + 1) * 512],
                                      start=False, stop=True)
                ins.then_inc(smm.h, 1)
                vm = smm.inc(1)
                R.scalar.wait_ge(smm.h, vm)
                if og == 0 and tt == 0:
                    k = b.hist.get("qtbuf")
                    if k is not None:
                        R.scalar.wait_ge(k[0].h, k[1])
                R.scalar.activation(out=qt4[:, tt * 2048 + og * 512:tt * 2048 + (og + 1) * 512],
                                    in_=b.pb[tt][:, :], func=AF.Copy, bias=0.0) \
                    .then_inc(sact.h, 1)
                b.hist[("pbq", tt)] = (sact, sact.inc(1))
        qdone = sact.n
        for tt in range(4):
            trow = tg * 512 + tt * 128
            qt = qt4[:, tt * 2048:(tt + 1) * 2048]
            v = R.vector
            if tt == 0:
                v.wait_ge(sact.h, qdone)
            v.tensor_tensor(out=qt2, in0=qt, in1=qt, op=ALU.mult)
            v.tensor_reduce(out=st2[:, 0:16], in_=qt2.rearrange("p (h e) -> p h e", e=128),
                            axis=AX.X, op=ALU.add)
            vspace(R, b)
            v.tensor_scalar(out=st2[:, 16:32], in0=st2[:, 0:16], scalar1=1.0 / 128,
                            scalar2=L1_EPS, op0=ALU.mult, op1=ALU.add)
            vspace(R, b)
            v.reciprocal(out=st2[:, 0:16], in_=st2[:, 16:32])
            vr = vpub(R, b, sev)
            R.scalar.wait_ge(sev.h, vr)
            R.scalar.activation(out=st2[:, 16:32], in_=st2[:, 0:16], func=AF.Sqrt)
            va = spub(R, b, sact)
            v.wait_ge(sact.h, va)
            for h in range(16):
                v.tensor_scalar(out=qt2[:, h * 128:(h + 1) * 128],
                                in0=qt[:, h * 128:(h + 1) * 128],
                                scalar1=st2[:, 16 + h:17 + h], scalar2=None, op0=ALU.mult)
            v.tensor_tensor(out=qt.rearrange("p (h e) -> p h e", e=128),
                            in0=qt2.rearrange("p (h e) -> p h e", e=128),
                            in1=qnb0.unsqueeze(1).broadcast_to([128, 16, 128]), op=ALU.mult)
            k = b.hist.get("csfree")
            if k is not None:
                R.sync.wait_ge(k[0].h, k[1])
            R.sync.dma_start(out=cs[:, 0:64], in_=b.inp["tab_cos"][trow:trow + 128, :]) \
                .then_inc(sx.h, 16)
            R.sync.dma_start(out=cs[:, 64:128], in_=b.inp["tab_sin"][trow:trow + 128, :]) \
                .then_inc(sx.h, 16)
            vcs = sx.inc(32)
            v.wait_ge(sx.h, vcs)
            ev = qt[:, 0::2].rearrange("p (h j) -> p h j", j=64)
            od = qt[:, 1::2].rearrange("p (h j) -> p h j", j=64)
            ctile = cs[:, 0:64].unsqueeze(1).broadcast_to([128, 16, 64])
            stile = cs[:, 64:128].unsqueeze(1).broadcast_to([128, 16, 64])
            o_ev = qt2[:, 0::2].rearrange("p (h j) -> p h j", j=64)
            o_od = qt2[:, 1::2].rearrange("p (h j) -> p h j", j=64)
            t1r = tmp1.rearrange("p (h j) -> p h j", j=64)
            t2r = tmp2.rearrange("p (h j) -> p h j", j=64)
            v.tensor_tensor(out=t1r, in0=ev, in1=ctile, op=ALU.mult)
            v.tensor_tensor(out=t2r, in0=od, in1=stile, op=ALU.mult)
            v.tensor_tensor(out=o_ev, in0=t1r, in1=t2r, op=ALU.subtract)
            v.tensor_tensor(out=t1r, in0=ev, in1=stile, op=ALU.mult)
            v.tensor_tensor(out=t2r, in0=od, in1=ctile, op=ALU.mult)
            v.tensor_tensor(out=o_od, in0=t1r, in1=t2r, op=ALU.add)
            v.tensor_copy(qb16, qt2).then_inc(sev.h, 1)
            vq16 = sev.inc(1)
            b.hist["qtbuf"] = (sev, vq16)
            b.hist["csfree"] = (sev, vq16)
            R.tensor.wait_ge(sev.h, vq16)
            for g4 in range(4):
                ptb = b.pt[g4 % 2]
                k = b.hist.get(("pt", g4 % 2))
                if k is not None:
                    R.tensor.wait_ge(k[0].h, k[1])
                ins = None
                for jj in range(4):
                    i = g4 * 4 + jj
                    ins = R.tensor.transpose(ptb[:, jj * 128:(jj + 1) * 128],
                                             qb16[:, i * 128:(i + 1) * 128],
                                             b.ident_bf[:, :])
                ins.then_inc(strp.h, 1)
                vt = strp.inc(1)
                k = b.hist.get("qstg")
                if k is not None:
                    R.vector.wait_ge(k[0].h, k[1])
                R.vector.wait_ge(strp.h, vt)
                R.vector.tensor_copy(qstage[:, g4 * 512:(g4 + 1) * 512], ptb[:, 0:512]) \
                    .then_inc(ste.h, 1)
                b.hist[("pt", g4 % 2)] = (ste, ste.inc(1))
            vts = ste.n
            R.sync.wait_ge(ste.h, vts)
            for i in range(ICH):
                R.sync.dma_start(out=b.qfm[i][:, trow:trow + 128],
                                 in_=qstage[:, i * 128:(i + 1) * 128]).then_inc(sst.h, 16)
                sst.inc(16)
            b.hist["qstg"] = (sst, sst.n)
            pbl = b.pb[4]
            k = b.hist.get("pblat")
            if k is not None:
                R.tensor.wait_ge(k[0].h, k[1])
            ins = None
            for i in range(ICH):
                ins = R.tensor.matmul(pbl[:, :], lnfm_tile(b, 0, i, tt * 128, 128),
                                      wdown[:, i * 512:(i + 1) * 512],
                                      start=(i == 0), stop=False)
            ins = R.tensor.matmul(pbl[:, :], b.ones1[0:1, :], bbd, start=False, stop=True)
            ins.then_inc(smm.h, 1)
            vm = smm.inc(1)
            R.scalar.wait_ge(smm.h, vm)
            k = b.hist.get(("latst", tt % 2))
            if k is not None:
                R.scalar.wait_ge(k[0].h, k[1])
            R.scalar.activation(out=latst[:, (tt % 2) * 512:(tt % 2) * 512 + 512],
                                in_=pbl[:, :], func=AF.Copy, bias=0.0).then_inc(sact.h, 1)
            vl = sact.inc(1)
            b.hist["pblat"] = (sact, vl)
            R.tensor.wait_ge(sact.h, vl)
            ptb = b.pt[tt % 2]
            k = b.hist.get(("pt", tt % 2))
            if k is not None:
                R.tensor.wait_ge(k[0].h, k[1])
            ins = None
            for jj in range(4):
                ins = R.tensor.transpose(ptb[:, jj * 128:(jj + 1) * 128],
                                         latst[:, (tt % 2) * 512 + jj * 128:(tt % 2) * 512 + (jj + 1) * 128],
                                         b.ident_bf[:, :])
            ins.then_inc(strp.h, 1)
            vt = strp.inc(1)
            R.vector.wait_ge(strp.h, vt)
            dstl = latfm.rearrange("p (l t) -> p l t", l=4)[:, :, trow:trow + 128]
            R.vector.tensor_copy(dstl, ptb[:, 0:512].rearrange("p (l t) -> p l t", l=4)) \
                .then_inc(ste.h, 1)
            lat_ev = ste.inc(1)
            b.hist[("pt", tt % 2)] = (ste, lat_ev)
            b.hist[("latst", tt % 2)] = (ste, lat_ev)
        b.hist[("lnfm", 0)] = (smm, smm.n)
    lat_done = b.sem["ste"].n
    kvev = carve(0, 8192, F32)
    ksq = carve(8192, 2048, F32)
    kno = carve(10240, 2048, F32)
    ktmp1 = carve(12288, 2048, F32)
    ktmp2 = carve(14336, 2048, F32)
    kcs = carve(131072, 512, F32)
    kbf = carve(131584, 2048)
    kfm = carve(65536, 32768)
    vtm = carve(98304, 32768)
    R.tensor.wait_ge(b.sem["ste"].h, lat_done)
    for tt2 in range(NTT):
        trow = tt2 * 128
        pbk = b.pb[tt2 % 2]
        pbv = b.pb[2 + tt2 % 2]
        for og, pbx in ((0, pbk), (1, pbv)):
            k = b.hist.get(("pbkv", og, tt2 % 2))
            if k is not None:
                R.tensor.wait_ge(k[0].h, k[1])
            ins = None
            for l in range(4):
                ins = R.tensor.matmul(pbx[:, :],
                                      latfm.rearrange("p (l t) -> p l t", l=4)[:, l, trow:trow + 128],
                                      wup[:, l * 1024 + og * 512:l * 1024 + (og + 1) * 512],
                                      start=(l == 0), stop=False)
            ins = R.tensor.matmul(pbx[:, :], b.ones1[0:1, :],
                                  bbu[0:1, og * 512:(og + 1) * 512], start=False, stop=True)
            ins.then_inc(smm.h, 1)
            smm.inc(1)
        vm = smm.n
        R.scalar.wait_ge(smm.h, vm)
        k = b.hist.get(("kvev", tt2 % 2))
        if k is not None:
            R.scalar.wait_ge(k[0].h, k[1])
        kev = kvev[:, (tt2 % 2) * 1024:(tt2 % 2) * 1024 + 512]
        vev = kvev[:, (tt2 % 2) * 1024 + 512:(tt2 % 2) * 1024 + 1024]
        R.scalar.activation(out=kev, in_=pbk[:, :], func=AF.Copy, bias=0.0).then_inc(sact.h, 1)
        R.scalar.activation(out=vev, in_=pbv[:, :], func=AF.Copy, bias=0.0).then_inc(sact.h, 1)
        va = sact.inc(2)
        v = R.vector
        v.wait_ge(sact.h, va)
        k = b.hist.get("vtmw")
        v.tensor_copy(vtm[:, tt2 * 512:(tt2 + 1) * 512], vev).then_inc(sev.h, 1)
        vvv = sev.inc(1)
        b.hist[("pbkv", 1, tt2 % 2)] = (sev, vvv)
        v.tensor_tensor(out=ksq[:, :], in0=kev, in1=kev, op=ALU.mult)
        v.tensor_reduce(out=st2[:, 0:4], in_=ksq[:, :].rearrange("p (h e) -> p h e", e=128),
                        axis=AX.X, op=ALU.add)
        vspace(R, b)
        v.tensor_scalar(out=st2[:, 4:8], in0=st2[:, 0:4], scalar1=1.0 / 128,
                        scalar2=L1_EPS, op0=ALU.mult, op1=ALU.add)
        vspace(R, b)
        v.reciprocal(out=st2[:, 0:4], in_=st2[:, 4:8])
        vr = vpub(R, b, sev)
        R.scalar.wait_ge(sev.h, vr)
        R.scalar.activation(out=st2[:, 4:8], in_=st2[:, 0:4], func=AF.Sqrt)
        va = spub(R, b, sact)
        v.wait_ge(sact.h, va)
        for h in range(4):
            v.tensor_scalar(out=ksq[:, h * 128:(h + 1) * 128],
                            in0=kev[:, h * 128:(h + 1) * 128],
                            scalar1=st2[:, 4 + h:5 + h], scalar2=None, op0=ALU.mult)
        v.tensor_tensor(out=kno[:, :].rearrange("p (h e) -> p h e", e=128),
                        in0=ksq[:, :].rearrange("p (h e) -> p h e", e=128),
                        in1=knb0.unsqueeze(1).broadcast_to([128, 4, 128]), op=ALU.mult)
        k = b.hist.get("csfree2")
        if k is not None:
            R.sync.wait_ge(k[0].h, k[1])
        R.sync.dma_start(out=kcs[:, 0:64], in_=b.inp["tab_cos"][trow:trow + 128, :]) \
            .then_inc(sx.h, 16)
        R.sync.dma_start(out=kcs[:, 64:128], in_=b.inp["tab_sin"][trow:trow + 128, :]) \
            .then_inc(sx.h, 16)
        vcs = sx.inc(32)
        v.wait_ge(sx.h, vcs)
        evk = kno[:, :][:, 0::2].rearrange("p (h j) -> p h j", j=64)
        odk = kno[:, :][:, 1::2].rearrange("p (h j) -> p h j", j=64)
        ck = kcs[:, 0:64].unsqueeze(1).broadcast_to([128, 4, 64])
        sk_ = kcs[:, 64:128].unsqueeze(1).broadcast_to([128, 4, 64])
        o_ev = ksq[:, :][:, 0::2].rearrange("p (h j) -> p h j", j=64)
        o_od = ksq[:, :][:, 1::2].rearrange("p (h j) -> p h j", j=64)
        tmp1 = ktmp1
        tmp2 = ktmp2
        v.tensor_tensor(out=tmp1[:, 0:256].rearrange("p (h j) -> p h j", j=64), in0=evk, in1=ck, op=ALU.mult)
        v.tensor_tensor(out=tmp2[:, 0:256].rearrange("p (h j) -> p h j", j=64), in0=odk, in1=sk_, op=ALU.mult)
        v.tensor_tensor(out=o_ev, in0=tmp1[:, 0:256].rearrange("p (h j) -> p h j", j=64),
                        in1=tmp2[:, 0:256].rearrange("p (h j) -> p h j", j=64), op=ALU.subtract)
        v.tensor_tensor(out=tmp1[:, 0:256].rearrange("p (h j) -> p h j", j=64), in0=evk, in1=sk_, op=ALU.mult)
        v.tensor_tensor(out=tmp2[:, 0:256].rearrange("p (h j) -> p h j", j=64), in0=odk, in1=ck, op=ALU.mult)
        v.tensor_tensor(out=o_od, in0=tmp1[:, 0:256].rearrange("p (h j) -> p h j", j=64),
                        in1=tmp2[:, 0:256].rearrange("p (h j) -> p h j", j=64), op=ALU.add)
        k = b.hist.get(("kbf", tt2 % 2))
        if k is not None:
            v.wait_ge(k[0].h, k[1])
        v.tensor_copy(kbf[:, (tt2 % 2) * 512:(tt2 % 2) * 512 + 512], ksq[:, :]) \
            .then_inc(sev.h, 1)
        vkb = sev.inc(1)
        b.hist["csfree2"] = (sev, vkb)
        b.hist[("pbkv", 0, tt2 % 2)] = (sev, vkb)
        b.hist[("kvev", tt2 % 2)] = (sev, vkb)
        R.tensor.wait_ge(sev.h, vkb)
        ptb = b.pt[tt2 % 2]
        k = b.hist.get(("pt", tt2 % 2))
        if k is not None:
            R.tensor.wait_ge(k[0].h, k[1])
        ins = None
        for jj in range(4):
            ins = R.tensor.transpose(ptb[:, jj * 128:(jj + 1) * 128],
                                     kbf[:, (tt2 % 2) * 512 + jj * 128:(tt2 % 2) * 512 + (jj + 1) * 128],
                                     b.ident_bf[:, :])
        ins.then_inc(strp.h, 1)
        vt = strp.inc(1)
        R.vector.wait_ge(strp.h, vt)
        dstk = kfm.rearrange("p (h t) -> p h t", h=4)[:, :, trow:trow + 128]
        R.vector.tensor_copy(dstk, ptb[:, 0:512].rearrange("p (h t) -> p h t", h=4)) \
            .then_inc(ste.h, 1)
        kev_ = ste.inc(1)
        b.hist[("pt", tt2 % 2)] = (ste, kev_)
        b.hist[("kbf", tt2 % 2)] = (ste, kev_)
    kv_done = b.sem["ste"].n
    if b.debug:
        R.sync.wait_ge(b.sem["ste"].h, kv_done)
        R.sync.dma_start(out=b.dbg_kfm, in_=kfm).then_inc(sst.h, 16)
        R.sync.dma_start(out=b.dbg_vtm, in_=vtm).then_inc(sst.h, 16)
        sst.inc(32)
    S = carve(0, 16384, F32)
    P = carve(16384, 8192)
    P2v = carve(24576, 8192)
    amask = carve(32768, 8192)
    kposb = carve(40960, 16384, F32)
    Sa = carve(24576, 2048, F32)
    btmp = carve(57344, 8192)
    qh0 = carve(131072, 8192)
    qtile = carve(139264, 512)
    T12 = carve(139776, 4096, F32)
    PTb = carve(143872, 2048)
    ctxst = carve(145920, 1024)
    R.vector.wait_ge(b.sem["ste"].h, kv_done)
    R.gpsimd.wait_ge(b.sem["ste"].h, kv_done)
    R.gpsimd.wait_ge(sst.h, sst.n)
    R.gpsimd.dma_start(out=qh0, in_=b.qfm[0]).then_inc(sq.h, 16)
    vq0 = sq.inc(16)
    R.sync.dma_start(out=kposb, in_=b.inp["tab_kpos"][0:1, :].to_broadcast((128, T))) \
        .then_inc(sx.h, 16)
    vkp = sx.inc(16)
    R.vector.wait_ge(sx.h, vkp)
    R.tensor.wait_ge(b.sem["ste"].h, kv_done)
    for tq in range(NTT):
        L = (tq + 1) * 128
        nch = (L + 511) // 512
        if tq == 0:
            R.tensor.wait_ge(sq.h, vq0)
            R.tensor.wait_ge(sev.h, sev.n)
        for c in range(nch):
            pbo = b.pb[c % 2]
            k = b.hist.get(("pbS", c % 2))
            if k is not None:
                R.tensor.wait_ge(k[0].h, k[1])
            w = min(512, L - c * 512)
            R.tensor.matmul(pbo[:, 0:w], qh0[0:64, tq * 128:(tq + 1) * 128],
                            kfm[0:64, c * 512:c * 512 + w],
                            start=True, stop=True).then_inc(smm.h, 1)
            vm = smm.inc(1)
            if c == 0:
                k = b.hist.get("Sfree")
                if k is not None:
                    R.vector.wait_ge(k[0].h, k[1])
            R.vector.wait_ge(smm.h, vm)
            ndiag = w - 128 if (c == nch - 1) else w
            if c == nch - 1:
                if ndiag > 0:
                    R.vector.tensor_scalar(out=S[:, c * 512:c * 512 + ndiag],
                                           in0=pbo[:, 0:ndiag], scalar1=SSCALE,
                                           scalar2=None, op0=ALU.mult)
                R.vector.scalar_tensor_tensor(
                    out=S[:, c * 512 + ndiag:c * 512 + ndiag + 128],
                    in0=pbo[:, ndiag:ndiag + 128], scalar=SSCALE, in1=b.tri[:, :],
                    op0=ALU.mult, op1=ALU.add).then_inc(sev.h, 1)
            else:
                R.vector.tensor_scalar(out=S[:, c * 512:(c + 1) * 512],
                                       in0=pbo[:, :], scalar1=SSCALE,
                                       scalar2=None, op0=ALU.mult).then_inc(sev.h, 1)
            b.hist[("pbS", c % 2)] = (sev, sev.inc(1))
        v = R.vector
        v.tensor_reduce(out=st[:, 20:21], in_=S[:, 0:L], axis=AX.X, op=ALU.max)
        vspace(R, b)
        v.tensor_scalar(out=st[:, 21:22], in0=st[:, 20:21], scalar1=-64.0, scalar2=None,
                        op0=ALU.add)
        v.tensor_copy(st[:, 22:23], st[:, 20:21])
        vspace(R, b)
        for it in range(16):
            v.tensor_tensor(out=st[:, 23:24], in0=st[:, 21:22], in1=st[:, 22:23], op=ALU.add)
            vspace(R, b)
            v.tensor_scalar(out=st[:, 24:25], in0=st[:, 23:24], scalar1=0.5, scalar2=None,
                            op0=ALU.mult)
            vspace(R, b)
            v.tensor_scalar(out=btmp[:, 0:L], in0=S[:, 0:L], scalar1=st[:, 24:25],
                            scalar2=None, op0=ALU.is_ge)
            v.tensor_reduce(out=st[:, 25:26], in_=btmp[:, 0:L], axis=AX.X, op=ALU.add)
            vspace(R, b)
            v.tensor_scalar(out=st[:, 26:27], in0=st[:, 25:26], scalar1=64.0, scalar2=None,
                            op0=ALU.is_ge)
            vspace(R, b)
            v.tensor_tensor(out=st[:, 27:28], in0=st[:, 24:25], in1=st[:, 21:22], op=ALU.subtract)
            vspace(R, b)
            v.tensor_tensor(out=st[:, 28:29], in0=st[:, 26:27], in1=st[:, 27:28], op=ALU.mult)
            vspace(R, b)
            v.tensor_tensor(out=st[:, 21:22], in0=st[:, 21:22], in1=st[:, 28:29], op=ALU.add)
            v.tensor_tensor(out=st[:, 29:30], in0=st[:, 24:25], in1=st[:, 22:23], op=ALU.subtract)
            vspace(R, b)
            v.tensor_scalar(out=st[:, 30:31], in0=st[:, 26:27], scalar1=-1.0, scalar2=1.0,
                            op0=ALU.mult, op1=ALU.add)
            vspace(R, b)
            v.tensor_tensor(out=st[:, 31:32], in0=st[:, 30:31], in1=st[:, 29:30], op=ALU.mult)
            vspace(R, b)
            v.tensor_tensor(out=st[:, 22:23], in0=st[:, 22:23], in1=st[:, 31:32], op=ALU.add)
            vspace(R, b)
        if b.debug and tq == 31:
            R.sync.wait_ge(sev.h, sev.n)
            R.sync.dma_start(out=b.dbg_S[0], in_=S[:, 0:T]).then_inc(sst.h, 16)
            R.sync.dma_start(out=b.dbg_S[3][:, 0:32], in_=st[:, 0:32]).then_inc(sst.h, 16)
            sst.inc(32)
        k = b.hist.get("amfree")
        if k is not None:
            v.wait_ge(k[0].h, k[1])
        v.tensor_scalar(out=amask[:, 0:L], in0=S[:, 0:L], scalar1=st[:, 21:22],
                        scalar2=BIG, op0=ALU.is_lt, op1=ALU.mult)
        v.tensor_tensor(out=Sa[:, 0:128], in0=amask[:, L - 128:L], in1=b.diagboost[:, :],
                        op=ALU.mult)
        v.tensor_copy(amask[:, L - 128:L], Sa[:, 0:128]).then_inc(sev.h, 1)
        am_done = sev.inc(1)
        b.hist["Sfree"] = (sev, am_done)
        for h in range(16):
            sl = h % 2
            k = b.hist.get(("qtile", sl))
            if k is not None:
                R.gpsimd.wait_ge(k[0].h, k[1])
            R.gpsimd.dma_start(out=qtile[:, sl * 128:(sl + 1) * 128],
                               in_=b.qfm[h][:, tq * 128:(tq + 1) * 128]).then_inc(sq.h, 16)
            vqt = sq.inc(16)
            slope_h = float(2.0 ** (-0.5 * (h + 1)))
            kvh = h // 4
            R.tensor.wait_ge(sq.h, vqt)
            for c in range(nch):
                pbo = b.pb[2 + c % 2]
                k = b.hist.get(("pbS2", c % 2))
                if k is not None:
                    R.tensor.wait_ge(k[0].h, k[1])
                w = min(512, L - c * 512)
                R.tensor.matmul(pbo[:, 0:w], qtile[:, sl * 128:(sl + 1) * 128],
                                kfm.rearrange("p (g t) -> p g t", g=4)[:, kvh, c * 512:c * 512 + w],
                                start=True, stop=True).then_inc(smm.h, 1)
                vm = smm.inc(1)
                R.scalar.wait_ge(smm.h, vm)
                k = b.hist.get(("T12", c % 2))
                if k is not None:
                    R.scalar.wait_ge(k[0].h, k[1])
                R.scalar.activation(out=T12[:, (c % 2) * 512:(c % 2) * 512 + w],
                                    in_=pbo[:, 0:w], func=AF.Tanh,
                                    scale=TSCALE).then_inc(sact.h, 1)
                va = sact.inc(1)
                b.hist[("pbS2", c % 2)] = (sact, va)
                if c == 0:
                    R.vector.wait_ge(sev.h, am_done)
                    k = b.hist.get("Sfree2")
                    if k is not None:
                        R.vector.wait_ge(k[0].h, k[1])
                    k = b.hist.get("P2free")
                    if k is not None:
                        R.vector.wait_ge(k[0].h, k[1])
                R.vector.wait_ge(sact.h, va)
                R.vector.scalar_tensor_tensor(out=Sa[:, 0:w],
                                              in0=T12[:, (c % 2) * 512:(c % 2) * 512 + w],
                                              scalar=30.0,
                                              in1=amask[:, c * 512:c * 512 + w],
                                              op0=ALU.mult, op1=ALU.add)
                v2 = R.vector.scalar_tensor_tensor(out=S[:, c * 512:c * 512 + w],
                                                   in0=kposb[:, c * 512:c * 512 + w],
                                                   scalar=slope_h, in1=Sa[:, 0:w],
                                                   op0=ALU.mult, op1=ALU.add)
                v2.then_inc(sev.h, 1)
                b.hist[("T12", c % 2)] = (sev, sev.inc(1))
            if b.debug and tq == 31 and h == 0:
                R.sync.wait_ge(sev.h, sev.n)
                R.sync.dma_start(out=b.dbg_S[1], in_=S[:, 0:T]).then_inc(sst.h, 16)
                sst.inc(16)
            v = R.vector
            v.tensor_reduce(out=st[:, 10:11], in_=S[:, 0:L], axis=AX.X, op=ALU.max,
                            negate=True)
            vspace(R, b)
            v.tensor_copy(st[:, 11:12], st[:, 10:11])
            vneg = vpub(R, b, sev)
            R.scalar.wait_ge(sev.h, vneg)
            k = b.hist.get("Pfree")
            if k is not None:
                R.scalar.wait_ge(k[0].h, k[1])
            R.scalar.activation(out=P[:, 0:L], in_=S[:, 0:L], func=AF.Exp,
                                bias=st[:, 11:12], accum_out=st[:, 12:13])
            sspace(R, b, 4)
            R.scalar.activation(out=st[:, 13:14], in_=st[:, 12:13], func=AF.Copy, bias=0.0)
            sspace(R, b, 2)
            R.scalar.activation(out=st[:, 15:16], in_=st[:, 13:14], func=AF.Copy, bias=0.0)
            va = spub(R, b, sact)
            b.hist["Sfree2"] = (sact, va)
            v.wait_ge(sact.h, va)
            v.reciprocal(out=st[:, 14:15], in_=st[:, 15:16])
            vspace(R, b, 3)
            k = b.hist.get("P2free")
            if k is not None:
                v.wait_ge(k[0].h, k[1])
            v.tensor_scalar(out=P2v[:, 0:L], in0=P[:, 0:L], scalar1=st[:, 14:15],
                            scalar2=None, op0=ALU.mult).then_inc(sev.h, 1)
            vp2 = sev.inc(1)
            b.hist["Pfree"] = (sev, vp2)
            R.tensor.wait_ge(sev.h, vp2)
            nkb = tq + 1
            ng4 = (nkb + 3) // 4
            for g4 in range(ng4):
                cnt = min(4, nkb - g4 * 4)
                ptb = b.pt[g4 % 2]
                k = b.hist.get(("pt", g4 % 2))
                if k is not None:
                    R.tensor.wait_ge(k[0].h, k[1])
                ins = None
                for jj in range(cnt):
                    kb = g4 * 4 + jj
                    ins = R.tensor.transpose(ptb[:, jj * 128:(jj + 1) * 128],
                                             P2v[:, kb * 128:(kb + 1) * 128],
                                             b.ident_bf[:, :])
                ins.then_inc(strp.h, 1)
                vt = strp.inc(1)
                k = b.hist.get(("PTs", g4 % 2))
                if k is not None:
                    R.vector.wait_ge(k[0].h, k[1])
                R.vector.wait_ge(strp.h, vt)
                R.vector.tensor_copy(PTb[:, (g4 % 2) * 512:(g4 % 2) * 512 + cnt * 128],
                                     ptb[:, 0:cnt * 128]).then_inc(ste.h, 1)
                vte = ste.inc(1)
                b.hist[("pt", g4 % 2)] = (ste, vte)
                R.tensor.wait_ge(ste.h, vte)
                if g4 == 0:
                    k = b.hist.get("pbctx")
                    if k is not None:
                        R.tensor.wait_ge(k[0].h, k[1])
                ins = None
                for jj in range(cnt):
                    kb = g4 * 4 + jj
                    ins = R.tensor.matmul(b.pb[4][:, 0:128],
                                          vtm[:, kb * 512 + kvh * 128:kb * 512 + (kvh + 1) * 128],
                                          PTb[:, (g4 % 2) * 512 + jj * 128:(g4 % 2) * 512 + (jj + 1) * 128],
                                          start=(kb == 0), stop=(kb == nkb - 1))
                ins.then_inc(smm.h, 1)
                b.hist[("PTs", g4 % 2)] = (smm, smm.inc(1))
            b.hist["P2free"] = (smm, smm.n)
            R.scalar.wait_ge(smm.h, smm.n)
            k = b.hist.get("ctxst")
            if k is not None:
                R.scalar.wait_ge(k[0].h, k[1])
            R.scalar.activation(out=ctxst[:, (tq % 4) * 128:(tq % 4) * 128 + 128],
                                in_=b.pb[4][:, 0:128], func=AF.Copy,
                                bias=0.0).then_inc(sact.h, 1)
            b.hist["pbctx"] = (sact, sact.inc(1))
            b.hist[("qtile", sl)] = (sact, sact.n)
            if True:
                R.sync.wait_ge(sact.h, sact.n)
                R.sync.dma_start(out=b.ctx[h][:, tq * 128:(tq + 1) * 128],
                                 in_=ctxst[:, (tq % 4) * 128:(tq % 4) * 128 + 128]) \
                    .then_inc(sst.h, 16)
                b.hist["ctxst"] = (sst, sst.inc(16))
        b.hist["amfree"] = (sev, sev.n)
    bgw = carve(0, 4096)
    crh = carve(4096, 2048)
    gbrow = carve(6144, 8192, F32)
    ydst = carve(14336, 2048)
    R.gpsimd.wait_ge(sst.h, sst.n)
    R.sync.wait_ge(sst.h, sst.n)
    for half in range(2):
        R.gpsimd.dma_start(out=bgw[:, half * 1024:(half + 1) * 1024]
                           .rearrange("d (h e) -> d h e", e=128),
                           in_=b.inp["L1_attn_gate_w"][half * 8:(half + 1) * 8]
                           .rearrange("h d e -> d h e"),
                           max_dma_last_dim=8192).then_inc(sw.h, 16)
        sw.inc(16)
    gw_done = sw.n
    R.sync.dma_start(out=gbrow[0:16, 0:128], in_=b.inp["L1_attn_gate_b"][:, :]) \
        .then_inc(sx.h, 16)
    vgb = sx.inc(16)
    R.tensor.wait_ge(sx.h, vgb)
    k = b.hist.get("pbT")
    if k is not None:
        R.tensor.wait_ge(k[0].h, k[1])
    R.tensor.transpose(b.pbT[:, 0:16], gbrow[0:16, 0:128], b.ident_f32[0:16, 0:16]) \
        .then_inc(strp.h, 1)
    vt = strp.inc(1)
    R.vector.wait_ge(strp.h, vt)
    R.vector.tensor_copy(st2[:, 32:48], b.pbT[:, 0:16])
    vgb2 = vpub(R, b, sev)
    R.scalar.wait_ge(sev.h, vgb2)
    R.tensor.wait_ge(sw.h, gw_done)
    for h in range(16):
        for tg in range(NTG):
            sl = (h * NTG + tg) % 2
            k = b.hist.get(("crh", sl))
            if k is not None:
                R.gpsimd.wait_ge(k[0].h, k[1])
            R.gpsimd.dma_start(out=crh[:, sl * 512:(sl + 1) * 512],
                               in_=b.ctx[h][:, tg * 512:(tg + 1) * 512]).then_inc(sq.h, 16)
            vc = sq.inc(16)
            pbo = b.pb[sl]
            k = b.hist.get(("pbg", sl))
            if k is not None:
                R.tensor.wait_ge(k[0].h, k[1])
            R.tensor.wait_ge(sq.h, vc)
            R.tensor.matmul(pbo[:, :], bgw[:, h * 128:(h + 1) * 128],
                            crh[:, sl * 512:(sl + 1) * 512],
                            start=True, stop=True).then_inc(smm.h, 1)
            vm = smm.inc(1)
            R.scalar.wait_ge(smm.h, vm)
            k = b.hist.get(("T12", sl))
            if k is not None:
                R.scalar.wait_ge(k[0].h, k[1])
            R.scalar.activation(out=T12[:, sl * 512:(sl + 1) * 512], in_=pbo[:, :],
                                func=AF.Sigmoid, bias=st2[:, 32 + h:33 + h]) \
                .then_inc(sact.h, 1)
            va = sact.inc(1)
            b.hist[("pbg", sl)] = (sact, va)
            k = b.hist.get(("ydst", sl))
            if k is not None:
                R.vector.wait_ge(k[0].h, k[1])
            R.vector.wait_ge(sact.h, va)
            R.vector.tensor_tensor(out=ydst[:, sl * 512:(sl + 1) * 512],
                                   in0=T12[:, sl * 512:(sl + 1) * 512],
                                   in1=crh[:, sl * 512:(sl + 1) * 512], op=ALU.mult) \
                .then_inc(sev.h, 1)
            vy = sev.inc(1)
            b.hist[("T12", sl)] = (sev, vy)
            b.hist[("crh", sl)] = (sev, vy)
            R.sync.wait_ge(sev.h, vy)
            R.sync.dma_start(out=b.yp[h][:, tg * 512:(tg + 1) * 512],
                             in_=ydst[:, sl * 512:(sl + 1) * 512]).then_inc(sst.h, 16)
            b.hist[("ydst", sl)] = (sst, sst.inc(16))
    wslab = carve(0, 65536)
    yrhs = carve(65536, 16384)
    outfm = carve(81920, 16384)
    borow = carve(98304, 8192, F32)[0:1, :]
    R.gpsimd.wait_ge(sst.h, sst.n)
    R.sync.wait_ge(sst.h, sst.n)
    R.sync.dma_start(out=borow[0:1, 0:2048], in_=b.inp["L1_attn_b_o"].unsqueeze(0)) \
        .then_inc(sx.h, 16)
    vb = sx.inc(16)
    R.tensor.wait_ge(sx.h, vb)
    k = b.hist.get("pbT")
    if k is not None:
        R.tensor.wait_ge(k[0].h, k[1])
    for q4 in range(4):
        ins = None
        for kk in range(4):
            i = q4 * 4 + kk
            ins = R.tensor.transpose(b.pbT[:, i:i + 1],
                                     borow[0:1, i * 128:(i + 1) * 128],
                                     b.ones1f[0:1, 0:1])
    ins.then_inc(strp.h, 1)
    vt = strp.inc(1)
    R.vector.wait_ge(strp.h, vt)
    R.vector.tensor_copy(b.stats2[:, 16:32], b.pbT[:, 0:16])
    bocol_ready = vpub(R, b, sev)
    for oc in range(ICH):
        emit_strip_load(R, b, b.inp["L1_attn_w_o"], wslab, oc, sw)
    wload = sw.n
    for tg in range(NTG):
        k = b.hist.get("crhs")
        if k is not None:
            R.sync.wait_ge(k[0].h, k[1])
        for i in range(ICH):
            R.sync.dma_start(out=yrhs[:, i * 512:(i + 1) * 512],
                             in_=b.yp[i][:, tg * 512:(tg + 1) * 512]).then_inc(sx.h, 16)
        vls = sx.inc(16 * ICH)
        R.tensor.wait_ge(sx.h, vls)
        if tg == 0:
            R.tensor.wait_ge(sw.h, wload)
            R.scalar.wait_ge(sev.h, bocol_ready)
        for oc in range(ICH):
            pbo = b.pb[oc % 2]
            k = b.hist.get(("pbqkv", oc % 2))
            if k is not None:
                R.tensor.wait_ge(k[0].h, k[1])
            ins = None
            for i in range(ICH):
                ins = R.tensor.matmul(pbo[:, :],
                                      wslab[:, oc * 2048 + i * 128:oc * 2048 + (i + 1) * 128],
                                      yrhs[:, i * 512:(i + 1) * 512],
                                      start=(i == 0), stop=(i == ICH - 1))
            ins.then_inc(smm.h, 1)
            vm = smm.inc(1)
            R.scalar.wait_ge(smm.h, vm)
            if oc == 0:
                k = b.hist.get("outfm")
                if k is not None:
                    R.scalar.wait_ge(k[0].h, k[1])
            R.scalar.activation(out=outfm[:, oc * 512:(oc + 1) * 512], in_=pbo[:, :],
                                func=AF.Identity, bias=b.stats2[:, 16 + oc:17 + oc]) \
                .then_inc(sact.h, 1)
            b.hist[("pbqkv", oc % 2)] = (sact, sact.inc(1))
        outfm_done = sact.n
        b.hist["crhs"] = (smm, smm.n)
        R.tensor.wait_ge(sact.h, outfm_done)
        for tt in range(4):
            row0 = tg * 512 + tt * 128
            xs = b.xt[:, (tt % 2) * D:(tt % 2) * D + D]
            k = b.hist.get(("ln_x", tt % 2))
            if k is not None:
                R.sync.wait_ge(k[0].h, k[1])
            R.sync.dma_start(out=xs, in_=src[row0:row0 + 128, :]).then_inc(sx.h, 16)
            vx = sx.inc(16)
            R.vector.wait_ge(sx.h, vx)
            res = b.res[:, (tt % 2) * D:(tt % 2) * D + D]
            k = b.hist.get(("res", tt % 2))
            if k is not None:
                R.vector.wait_ge(k[0].h, k[1])
            for r in range(4):
                ptb = b.pt[r % 2]
                k = b.hist.get(("pt", r % 2))
                if k is not None:
                    R.tensor.wait_ge(k[0].h, k[1])
                ins = None
                for kk in range(4):
                    oc = r * 4 + kk
                    ins = R.tensor.transpose(
                        ptb[:, kk * 128:(kk + 1) * 128],
                        outfm[:, oc * 512 + tt * 128:oc * 512 + (tt + 1) * 128],
                        b.ident_bf[:, :])
                ins.then_inc(strp.h, 1)
                vt = strp.inc(1)
                R.vector.wait_ge(strp.h, vt)
                R.vector.scalar_tensor_tensor(
                    out=res[:, r * 512:(r + 1) * 512], in0=ptb[:, 0:512],
                    scalar=1.0, in1=xs[:, r * 512:(r + 1) * 512],
                    op0=ALU.mult, op1=ALU.add).then_inc(sev.h, 1)
                b.hist[("pt", r % 2)] = (sev, sev.inc(1))
            b.hist[("ln_x", tt % 2)] = (sev, sev.n)
            R.sync.wait_ge(sev.h, sev.n)
            R.sync.dma_start(out=dst[row0:row0 + 128, :], in_=res).then_inc(sst.h, 16)
            b.hist[("res", tt % 2)] = (sst, sst.inc(16))
        b.hist["outfm"] = (b.sem["str"], b.sem["str"].n)
